# version 06
# baseline (speedup 1.0000x reference)
import contextlib
import copy
import numpy as np
import bass_rust
import ml_dtypes
import concourse.bass as bass
import concourse.mybir as mybir
from concourse.bass_utils import run_bass_kernel_spmd
from concourse.replica_groups import maybe_share_collective_output_space

F32 = mybir.dt.float32
BF16 = mybir.dt.bfloat16
AF = mybir.ActivationFunctionType
ALU = mybir.AluOpType
AX = mybir.AxisListType

T = 2048
D = 1024
NT = 16
NDC = 8
NCORES = 8
BIS_ITERS = 50

ENGINES = ("sync", "scalar", "vector", "tensor", "gpsimd")


class Emitter:
    def __init__(self):
        self.q = {e: [] for e in ENGINES}
        self.ctr = {}

    def cur(self, sem):
        return self.ctr.get(sem, 0)

    def plan(self, sem, k=1):
        self.ctr[sem] = self.ctr.get(sem, 0) + k
        return self.ctr[sem]

    CHAIN = {"vector": "ve", "scalar": "ac"}

    def op(self, engine, build, waits=(), inc=None, inc_k=1):
        waits = list(waits)
        csem = self.CHAIN.get(engine)
        if csem is not None:
            waits.append((csem, self.cur(csem)))
            if inc is None:
                inc = csem
            assert inc == csem and inc_k == 1
        v = self.plan(inc, inc_k) if inc else None

        def fn(eng, S, _w=tuple(waits), _b=build, _i=inc, _k=inc_k):
            for s, val in _w:
                if val > 0:
                    eng.wait_ge(S[s], val)
            ins = _b(eng)
            if _i is not None and ins is not None:
                ins.then_inc(S[_i], _k)
        self.q[engine].append(fn)
        return v

    def wait(self, engine, waits):
        def fn(eng, S, _w=tuple(waits)):
            for s, val in _w:
                if val > 0:
                    eng.wait_ge(S[s], val)
        self.q[engine].append(fn)

    def all_marks(self):
        return [(s, self.cur(s)) for s in ("qS", "qA", "qP", "pe", "ve",
                                           "ac", "gp", "cc")]


def build_program(upto=99, debug=(), cc_local=False, paranoid=False, lnv=0, no_rmm=False, no_p2=False):
    nc = bass.Bass("TRN2", target_bir_lowering=False, debug=False,
                   detect_race_conditions=False)
    E = Emitter()

    def din(name, shape, dtype=F32):
        return nc.dram_tensor(name, shape, dtype, kind="ExternalInput").ap()

    x_in = din("x", [T, D])
    identf = din("identf", [128, 128])
    identb_in = din("identb", [128, 128], BF16)
    onesrb_in = din("onesrb", [1, 128], BF16)
    onesr = din("onesr", [1, 128])
    onescb = din("onescb", [128, 1], BF16)
    foldw = din("foldw", [128, 128])
    wq_n = din("wq_n", [1024, 1024]); wk_n = din("wk_n", [1024, 1024])
    wv_n = din("wv_n", [1024, 1024]); wo_n = din("wo_n", [16384, 64])
    bq_in = din("bq", [1024]); bk_in = din("bk", [1024]); bv_in = din("bv", [1024])
    bo_in = din("bo", [1, 1024])
    wg_in = din("wg", [1024, 8])
    w1_in = din("w1", [8 * 1024, 2048])
    w2_in = din("w2", [8 * 2048, 1024])
    wqt_in = din("wqt", [1024, 256]); wkt_in = din("wkt", [1024, 256])
    wqs_in = din("wqs", [1024, 768]); wks_in = din("wks", [1024, 768])
    wv1_in = din("wv1", [1024, 1024]); wout_in = din("wout", [1024, 1024])
    bout_in = din("bout", [1, 1024])
    wsig_in = din("wsig", [1, 1]); gate_in = din("gate", [1, 1])
    fastb_in = din("fastb", [16, 16]); u_in = din("u", [4096, 16])
    vm_in = din("vm", [16, 1024])
    wfc_in = din("wfc", [4096, 1024]); wproj_in = din("wproj", [1024, 4096])

    y_out = nc.dram_tensor("y", [T, D], F32, kind="ExternalOutput").ap()
    dbg_aps = {}
    for name, shape, dtype in debug:
        dbg_aps[name] = nc.dram_tensor("dbg_" + name, list(shape), dtype,
                                       kind="ExternalOutput").ap()

    wq_s = nc.dram_tensor("wq_s", [1024, 1024], BF16).ap()
    wk_s = nc.dram_tensor("wk_s", [1024, 1024], BF16).ap()
    wv_s = nc.dram_tensor("wv_s", [1024, 1024], BF16).ap()
    wqT_s = nc.dram_tensor("wqT_s", [1024, 1024], BF16).ap()
    wkT_s = nc.dram_tensor("wkT_s", [1024, 1024], BF16).ap()
    wvT_s = nc.dram_tensor("wvT_s", [1024, 1024], BF16).ap()
    wefcT = nc.dram_tensor("wefcT", [1024, 4096], BF16).ap()
    rb_scr = nc.dram_tensor("rb_scr", [1, 2048], F32).ap()
    wepT = nc.dram_tensor("wepT", [4096, 1024], BF16).ap()
    rg = [list(range(NCORES))]
    cc = {}

    def get_cc():
        if not cc:
            cc_space = ("Local" if cc_local else
                        maybe_share_collective_output_space("AllGather", rg))
            cc["in"] = nc.dram_tensor("cc_in", [8, T], F32, kind="Internal")
            cc["out"] = nc.dram_tensor("cc_out", [8 * NCORES, T], F32,
                                       kind="Internal", addr_space=cc_space)
        return cc["in"], cc["out"]

    es = contextlib.ExitStack()
    with es:
        X = es.enter_context(nc.sbuf_tensor("Xs", [128, NT, D], F32))
        XF = es.enter_context(nc.sbuf_tensor("XFs", [128, NDC, T], BF16))
        IDENT = es.enter_context(nc.sbuf_tensor("IDENT", [128, 128], F32))
        ONESR = es.enter_context(nc.sbuf_tensor("ONESR", [1, 128], F32))
        ONESCB = es.enter_context(nc.sbuf_tensor("ONESCB", [128, 1], BF16))
        FOLDW = es.enter_context(nc.sbuf_tensor("FOLDW", [128, 128], F32))
        BQ8 = es.enter_context(nc.sbuf_tensor("BQ8", [128, 8], F32))
        BKs = es.enter_context(nc.sbuf_tensor("BKs", [128, 8], F32))
        BVs = es.enter_context(nc.sbuf_tensor("BVs", [128, 8], F32))
        PROBS = es.enter_context(nc.sbuf_tensor("PROBS", [128, NT, 8], F32))
        GATES = es.enter_context(nc.sbuf_tensor("GATES", [128, NT, 8], F32))
        SCRF = es.enter_context(nc.sbuf_tensor("SCRF", [128, 2048], F32))
        SM = es.enter_context(nc.sbuf_tensor("SM", [128, 16], F32))
        RS = es.enter_context(nc.sbuf_tensor("RS", [1, 16], F32))
        BT = es.enter_context(nc.sbuf_tensor("BT", [128, D], F32))
        BIS = es.enter_context(nc.sbuf_tensor("BIS", [128, 12], F32))
        B128 = es.enter_context(nc.sbuf_tensor("B128", [128, 1024], F32))

        PSA = es.enter_context(nc.psum_tensor("PSA", [128, 2048], F32))
        PSB = es.enter_context(nc.psum_tensor("PSB", [128, 2048], F32))
        PSS = nc.virtual_psum_tensor("PSS", [128, 4096], BF16)
        nc.back_virtual_tensor_with(PSS, PSA)

        sems = {}
        for s in ("qS", "qA", "qP", "pe", "ve", "ac", "gp", "cc"):
            sems[s] = es.enter_context(nc.semaphore(s))

        print("sbuf bytes remaining after persistent allocs:",
              nc.sbuf_bytes_remaining)

        def dma(qeng, sem, out, in_, waits=(), **kw):
            return E.op(qeng,
                        lambda eng, o=out, i=in_, kw=dict(kw):
                        eng.dma_start(out=o, in_=i, **kw),
                        waits=waits, inc=sem, inc_k=16)

        def mm(out, lhsT, rhs, start, stop, waits=(), inc=None):
            return E.op("tensor",
                        lambda eng, o=out, l=lhsT, r=rhs, s1=start, s2=stop:
                        eng.matmul(o, l, r, start=s1, stop=s2),
                        waits=waits, inc=inc)

        def tr(out, in_, waits=(), inc=None, np_=128):
            return E.op("tensor",
                        lambda eng, o=out, i=in_, n=np_:
                        eng.transpose(o, i, IDENT[0:n, 0:n]),
                        waits=waits, inc=inc)

        WGF = es.enter_context(nc.sbuf_tensor("WGF", [128, 8, 8], F32))
        vX = dma("sync", "qS", X[:, :, :],
                 x_in.rearrange("(j p) d -> p j d", p=128))
        dma("sync", "qS", WGF[:, :, :],
            wg_in.rearrange("(dc p) e -> p dc e", p=128))
        dma("sync", "qS", IDENT[:, :], identf)
        dma("sync", "qS", ONESR[:, :], onesr)
        dma("sync", "qS", ONESCB[:, :], onescb)
        dma("sync", "qS", FOLDW[:, :], foldw)
        vB = None
        for name, src in (("bq", bq_in), ("bk", bk_in), ("bv", bv_in)):
            vB = dma("sync", "qS", SCRF[0:8, 128 * {"bq": 0, "bk": 1, "bv": 2}[name]:
                                        128 * ({"bq": 0, "bk": 1, "bv": 2}[name] + 1)],
                     src.rearrange("(a p) -> a p", p=128))
        vP0 = vB
        vtrb = {}
        for i, dst in enumerate((BQ8, BKs, BVs)):
            vtrb[i] = tr(PSB[0:128, i * 8:(i + 1) * 8],
                         SCRF[0:8, 128 * i:128 * (i + 1)],
                         waits=[("qS", vP0)] if i == 0 else (),
                         inc="pe" if i == 2 else None, np_=8)
        v_bev = E.op("vector", lambda eng: eng.tensor_copy(
            BKs[:, :], PSB[:, 8:16]), waits=[("pe", vtrb[2])], inc="ve")
        v_bev = E.op("vector", lambda eng: eng.tensor_copy(
            BVs[:, :], PSB[:, 16:24]), inc="ve")
        v_bev = E.op("vector", lambda eng: eng.tensor_scalar(
            BQ8[:, :], PSB[:, 0:8], 0.125, None, op0=ALU.mult), inc="ve")
        vP0ve = v_bev

        def emit_layernorm_to_XF(first_waits, router=False):
            v_rmm_prev = [0]
            v_sr_prev = 0
            prev_tr = {0: 0, 1: 0}
            prev_evac = {0: 0, 1: 0}
            prev_e32 = {0: 0, 1: 0}
            last_tr = 0
            v_ev = 0
            for j in range(NT):
                half = j % 2
                w = list(first_waits) if j == 0 else []
                if j > 0:
                    w = w + [("ac", v_sr_prev)]
                if paranoid:
                    w = w + E.all_marks()
                v_ve = E.op("vector", lambda eng, j=j: eng.reduce_sum(
                    SM[:, 0:1], X[:, j, :], axis=AX.X), waits=w, inc="ve")
                E.op("vector", lambda eng: eng.tensor_scalar(
                    SM[:, 1:2], SM[:, 0:1], -1.0 / D, None, op0=ALU.mult))
                v_ve = E.op("vector", lambda eng: eng.tensor_scalar(
                    SM[:, 0:1], SM[:, 0:1], 1.0 / D, None, op0=ALU.mult),
                    inc="ve")
                v_sq = E.op("scalar", lambda eng, j=j: eng.activation(
                    SCRF[:, 0:1024], X[:, j, :], AF.Square, bias=SM[:, 1:2],
                    scale=1.0),
                    waits=[("ve", v_ve)] +
                          ([("pe", v_rmm_prev[0])] if router else []),
                    inc="ac")
                E.op("vector", lambda eng: eng.reduce_sum(
                    SM[:, 2:3], SCRF[:, 0:1024], axis=AX.X),
                    waits=[("ac", v_sq)])
                v_ssq = E.op("vector", lambda eng: eng.tensor_scalar(
                    SM[:, 2:3], SM[:, 2:3], 1.0 / D, 1e-5,
                    op0=ALU.mult, op1=ALU.add), inc="ve")
                v_sr = E.op("scalar", lambda eng: eng.activation(
                    SM[:, 3:4], SM[:, 2:3], AF.Sqrt),
                    waits=[("ve", v_ssq)], inc="ac")
                v_sr_prev = v_sr
                E.op("vector", lambda eng: eng.reciprocal(
                    SM[:, 4:5], SM[:, 3:4]), waits=[("ac", v_sr)])
                E.op("vector", lambda eng: eng.tensor_scalar(
                    SM[:, 5:6], SM[:, 1:2], SM[:, 4:5], None, op0=ALU.mult))
                if lnv == 1:
                    v_nrm = E.op("vector", lambda eng, j=j: eng.tensor_copy(
                        SCRF[:, 1024:2048], X[:, j, :]),
                        waits=[("pe", last_tr), ("ac", v_sr)], inc="ve")
                elif lnv == 2:
                    v_nrm = E.op("vector", lambda eng, j=j: eng.tensor_scalar(
                        SCRF[:, 1024:2048], X[:, j, :], SM[:, 4:5], None,
                        op0=ALU.mult),
                        waits=[("pe", last_tr)], inc="ve")
                else:
                    v_nrm = E.op("vector", lambda eng, j=j: eng.tensor_scalar(
                        SCRF[:, 1024:2048], X[:, j, :], SM[:, 4:5], SM[:, 5:6],
                        op0=ALU.mult, op1=ALU.add),
                        waits=[("pe", last_tr)], inc="ve")
                ps = PSA[:, half * 1024:(half + 1) * 1024]
                vtr = None
                for dc in range(NDC):
                    vtr = tr(ps[:, dc * 128:(dc + 1) * 128],
                             SCRF[:, 1024 + dc * 128:1024 + (dc + 1) * 128],
                             waits=([("ve", max(v_nrm, prev_e32[half])),
                                     ("ac", prev_evac[half])]
                                    if dc == 0 else ()),
                             inc="pe" if dc == NDC - 1 else None)
                prev_tr[half] = vtr
                last_tr = vtr
                v_ev = E.op("scalar", lambda eng, j=j, ps=ps: eng.activation(
                    XF[:, :, j * 128:(j + 1) * 128],
                    ps.rearrange("p (dc t) -> p dc t", dc=8),
                    AF.Identity), waits=[("pe", vtr)], inc="ac")
                if router:
                    v_e32 = E.op("vector", lambda eng, ps=ps: eng.tensor_copy(
                        SCRF[:, 0:1024], ps), waits=[("ac", v_ev)])
                    prev_e32[half] = v_e32
                    v_rmm = None
                    for dc in range([NDC, 1, 0][no_rmm]):
                        wr = [("ve", v_e32)] if dc == 0 else []
                        if j == 0 and dc == 0:
                            wr.append(("qS", vX + 16))
                        v_rmm = mm(PSB[:, j * 8:(j + 1) * 8],
                                   SCRF[:, dc * 128:(dc + 1) * 128],
                                   WGF[:, dc, :], start=(dc == 0),
                                   stop=bool((dc == 7) or no_rmm),
                                   waits=wr,
                                   inc="pe" if (dc == 7 or no_rmm) else None)
                    if v_rmm is not None:
                        v_rmm_prev[0] = v_rmm
                prev_evac[half] = v_ev
            return v_ev

        vXF = emit_layernorm_to_XF([("qS", vX), ("ve", vP0ve)],
                                   router=(upto >= 2))

        if upto >= 2 and not no_p2:
            v_exp = 0
            v_prt = 0
            vmmj = E.cur("pe")
            for j in range(NT):
                pv = PSB[:, j * 8:(j + 1) * 8]
                v_mx = E.op("vector", lambda eng, pv=pv: eng.reduce_max(
                    SM[:, 0:1], pv, axis=AX.X),
                    waits=[("pe", vmmj), ("ac", v_exp)])
                v_mx = E.op("vector", lambda eng: eng.tensor_scalar(
                    SM[:, 1:2], SM[:, 0:1], -1.0, None, op0=ALU.mult),
                    inc="ve")
                v_exp = E.op("scalar", lambda eng, j=j, pv=pv: eng.activation(
                    PROBS[:, j, :], pv, AF.Exp, bias=SM[:, 1:2]),
                    waits=[("ve", v_mx)], inc="ac")
                E.op("vector", lambda eng, j=j: eng.reduce_sum(
                    SM[:, 2:3], PROBS[:, j, :], axis=AX.X),
                    waits=[("ac", v_exp)])
                E.op("vector", lambda eng: eng.reciprocal(
                    SM[:, 3:4], SM[:, 2:3]))
                v_pn = E.op("vector", lambda eng, j=j: eng.tensor_scalar(
                    PROBS[:, j, :], PROBS[:, j, :], SM[:, 3:4], None,
                    op0=ALU.mult), inc="ve")
                v_prt = tr(PSA[0:8, j * 128:(j + 1) * 128], PROBS[:, j, :],
                           waits=[("ve", v_pn)] + ([("ac", vXF)] if j == 0
                                                   else []),
                           inc="pe")
        if upto >= 3:
            v_pev = E.op("vector", lambda eng: eng.tensor_copy(
                SCRF[0:8, :], PSA[0:8, :]), waits=[("pe", v_prt)], inc="ve")
            cc_in, cc_out = get_cc()
            v_cin = dma("sync", "qS", cc_in[:, :], SCRF[0:8, :],
                        waits=[("ve", v_pev)])
            v_cc = E.op("gpsimd", lambda eng: eng.collective_compute(
                "AllGather", ALU.bypass, replica_groups=rg,
                ins=[cc_in.ap().opt()], outs=[cc_out.ap().opt()]),
                waits=[("qS", v_cin)], inc="cc")
            for h in range(2):
                v_b128 = dma("sync", "qS", B128[h * 64:(h + 1) * 64, :],
                             cc_out.ap()[0:64, h * 1024:(h + 1) * 1024],
                             waits=[("cc", v_cc)] if h == 0 else ())
        if upto >= 4:
            E.op("vector", lambda eng: eng.memset(BIS[:, 0:1], 0.0),
                 waits=[("qS", v_b128)])
            E.op("vector", lambda eng: eng.memset(BIS[:, 1:2], 1.0))
            v_pe_prev = E.cur("pe")
            v_ac_guard = v_exp
            for it in range(BIS_ITERS):
                E.op("vector", lambda eng: eng.scalar_tensor_tensor(
                    BIS[:, 2:3], BIS[:, 0:1], 1.0, BIS[:, 1:2],
                    op0=ALU.mult, op1=ALU.add))
                E.op("vector", lambda eng: eng.tensor_scalar(
                    BIS[:, 2:3], BIS[:, 2:3], 0.5, None, op0=ALU.mult))
                E.op("vector", lambda eng: eng.tensor_scalar(
                    SCRF[:, 0:1024], B128[:, :], BIS[:, 2:3], None,
                    op0=ALU.is_ge))
                v_fold = E.op("vector", lambda eng: eng.reduce_sum(
                    BIS[:, 3:4], SCRF[:, 0:1024], axis=AX.X), inc="ve")
                v_cm = mm(PSB[:, 0:1], FOLDW[:, :], BIS[:, 3:4],
                          start=True, stop=True,
                          waits=[("ve", v_fold)] +
                                ([("ac", v_ac_guard)] if it == 0 else []),
                          inc="pe")
                E.op("vector", lambda eng: eng.tensor_copy(
                    BIS[:, 6:7], PSB[:, 0:1]), waits=[("pe", v_cm)])
                E.op("vector", lambda eng: eng.tensor_scalar(
                    BIS[:, 7:8], BIS[:, 6:7], 2048.0, None, op0=ALU.is_ge))
                E.op("vector", lambda eng: eng.tensor_sub(
                    BIS[:, 8:9], BIS[:, 2:3], BIS[:, 0:1]))
                E.op("vector", lambda eng: eng.tensor_mul(
                    BIS[:, 8:9], BIS[:, 8:9], BIS[:, 7:8]))
                E.op("vector", lambda eng: eng.tensor_add(
                    BIS[:, 0:1], BIS[:, 0:1], BIS[:, 8:9]))
                E.op("vector", lambda eng: eng.tensor_sub(
                    BIS[:, 9:10], BIS[:, 1:2], BIS[:, 2:3]))
                E.op("vector", lambda eng: eng.tensor_mul(
                    BIS[:, 9:10], BIS[:, 9:10], BIS[:, 7:8]))
                v_upd = E.op("vector", lambda eng: eng.tensor_add(
                    BIS[:, 1:2], BIS[:, 2:3], BIS[:, 9:10]), inc="ve")
            v_throw = tr(PSB[0:1, 8:16], BIS[0:8, 0:1],
                         waits=[("ve", v_upd)], inc="pe", np_=8)
            v_rs = E.op("vector", lambda eng: eng.tensor_copy(
                RS[0:1, 0:8], PSB[0:1, 8:16]), waits=[("pe", v_throw)],
                inc="ve")
            v_th = mm(PSB[:, 16:24], ONESR[:, :], RS[0:1, 0:8],
                      start=True, stop=True, waits=[("ve", v_rs)], inc="pe")
            E.op("vector", lambda eng: eng.tensor_copy(
                SM[:, 8:16], PSB[:, 16:24]), waits=[("pe", v_th)])
            for j in range(NT):
                gj = GATES[:, j, :]
                E.op("vector", lambda eng, gj=gj, j=j: eng.tensor_tensor(
                    gj, PROBS[:, j, :], SM[:, 8:16], op=ALU.is_ge))
                E.op("vector", lambda eng, gj=gj, j=j:
                     eng.scalar_tensor_tensor(
                         gj, gj, 2.0, PROBS[:, j, :],
                         op0=ALU.mult, op1=ALU.add))
                E.op("vector", lambda eng, gj=gj: eng.reduce_max(
                    SM[:, 0:1], gj, axis=AX.X))
                E.op("vector", lambda eng: eng.tensor_scalar(
                    SM[:, 1:2], SM[:, 0:1], 2.0, None, op0=ALU.is_ge))
                E.op("vector", lambda eng: eng.scalar_tensor_tensor(
                    SM[:, 2:3], SM[:, 1:2], -2.0, SM[:, 0:1],
                    op0=ALU.mult, op1=ALU.add))
                E.op("vector", lambda eng, gj=gj: eng.tensor_scalar(
                    gj, gj, SM[:, 0:1], None, op0=ALU.is_equal))
                E.op("vector", lambda eng, gj=gj, j=j: eng.tensor_scalar(
                    gj, gj, SM[:, 2:3], None, op0=ALU.mult),
                    inc="ve" if j == NT - 1 else None)

        def phase_barrier():
            marks = [(s, E.cur(s)) for s in ("qS", "qA", "qP", "pe", "ve",
                                             "ac", "gp", "cc")]
            for e in ENGINES:
                E.wait(e, marks)

        def emit_bias_tile(BT_, src_ap):
            vld = dma("sync", "qS", BT_[0:1, :], src_ap)
            vm = None
            for nn in range(2):
                vm = mm(PSB[:, nn * 512:(nn + 1) * 512], ONESR[:, :],
                        BT_[0:1, nn * 512:(nn + 1) * 512], start=True,
                        stop=True,
                        waits=([("qS", vld), ("ve", E.cur("ve")),
                                ("ac", E.cur("ac"))] if nn == 0 else []),
                        inc="pe" if nn == 1 else None)
            E.op("vector", lambda eng: eng.tensor_copy(
                BT_[:, :], PSB[:, 0:1024]), waits=[("pe", vm)])
            for j in range(NT):
                E.op("vector", lambda eng, j=j: eng.tensor_add(
                    X[:, j, :], X[:, j, :], BT_[:, :]))

        def emit_attention(att, cfg):
            QTP, KTP, VAL, PT, OF = (att["QTP"], att["KTP"], att["VAL"],
                                     att["PT"], att["OF"])
            WQTP, WKTP, WVTP = att["WQTP"], att["WKTP"], att["WVTP"]
            WOB, IDENTB, ONESRB = att["WOB"], att["IDENTB"], att["ONESRB"]
            E.op("vector", lambda eng: eng.memset(
                VAL[:, :, :].rearrange("p sc (h c) -> p sc h c", c=65)
                [:, :, :, 64:65], 1.0))
            wld = [None, None]
            wld[0] = cfg["load_pair"](0, 0)
            wld[1] = cfg["load_pair"](1, 1)
            v_exp = {}
            v_pv = {}
            v_ptread = {(0, 0): 0, (0, 1): 0,
                        (1, 0): 0, (1, 1): 0}
            pt0_qp = [0]
            bc_qp = [0]
            om_ve = [0]
            for p in range(8):
                par = p % 2
                if p >= 2:
                    pass
                vq = None
                for s4 in range(4):
                    for dc in range(NDC):
                        w = []
                        if s4 == 0 and dc == 0:
                            w = [wld[par], ("ac", E.cur("ac")),
                                 ("ve", E.cur("ve"))]
                        vq = mm(PSA[:, s4 * 512:(s4 + 1) * 512],
                                WQTP[par][:, dc, :],
                                XF[:, dc, s4 * 512:(s4 + 1) * 512],
                                start=(dc == 0), stop=(dc == 7), waits=w,
                                inc="pe" if (s4 == 3 and dc == 7) else None)
                v_qe = E.op("scalar", lambda eng, p=p: eng.activation(
                    QTP[:, :], PSA[:, :], AF.Identity,
                    bias=cfg["qbias"](p), scale=cfg["qscale"](p)),
                    waits=[("pe", vq)])
                vk = None
                for s4 in range(4):
                    for dc in range(NDC):
                        w = []
                        vk = mm(PSB[:, s4 * 512:(s4 + 1) * 512],
                                WKTP[par][:, dc, :],
                                XF[:, dc, s4 * 512:(s4 + 1) * 512],
                                start=(dc == 0), stop=(dc == 7), waits=w,
                                inc="pe" if (s4 == 3 and dc == 7) else None)
                v_ke = E.op("scalar", lambda eng, p=p: eng.activation(
                    KTP[:, :], PSB[:, :], AF.Identity,
                    bias=cfg["kbias"](p), scale=1.0),
                    waits=[("pe", vk), ("ac", v_qe)])
                if 2 <= p + 1 < 8:
                    wld[1 - par] = cfg["load_pair"](p + 1, 1 - par)
                vbias = cfg["vbias_row"](p)
                vv = None
                for sc in range(NT):
                    w0 = [("ac", v_ke)] if sc == 0 else []
                    for dc in range(NDC):
                        last = ((dc == 7) and (vbias is None) and
                                (sc == NT - 1))
                        vv = mm(PSA[:, sc * 128:(sc + 1) * 128],
                                XF[:, dc, sc * 128:(sc + 1) * 128],
                                WVTP[par][:, dc, :],
                                start=(dc == 0),
                                stop=(dc == 7) and (vbias is None),
                                waits=w0 if dc == 0 else [],
                                inc="pe" if last else None)
                    if vbias is not None:
                        vv = mm(PSA[:, sc * 128:(sc + 1) * 128],
                                ONESRB[0:1, 0:128], vbias,
                                start=False, stop=True,
                                waits=([cfg["vbias_ready"]]
                                       if (p == 0 and sc == 0) else []),
                                inc="pe" if sc == NT - 1 else None)
                v_vev = E.op("vector", lambda eng: eng.tensor_copy(
                    VAL[:, :, :].rearrange("p sc (h c) -> p sc h c", c=65)
                    [:, :, :, 0:64],
                    PSA[:, :].rearrange("p (sc h c) -> p sc h c",
                                        sc=16, c=64)),
                    waits=[("pe", vv)])
                SCRB = PT[0]
                v_sqq = E.op("scalar", lambda eng: eng.activation(
                    SCRB[:, :], QTP[:, :], AF.Square),
                    waits=[("pe", max(v_ptread[(0, 0)],
                                      v_ptread[(1, 0)])), ("ve", v_vev),
                           ("qP", pt0_qp[0])])
                vnq = None
                for s4 in range(4):
                    vnq = mm(PSA[0:1, s4 * 512:(s4 + 1) * 512], ONESCB[:, :],
                             SCRB[:, s4 * 512:(s4 + 1) * 512], start=True,
                             stop=True,
                             waits=[("ac", v_sqq), ("ve", v_vev)]
                             if s4 == 0 else [],
                             inc="pe" if s4 == 3 else None)
                v_qmax = E.op("vector", lambda eng: eng.reduce_max(
                    RS[0:1, 0:1], PSA[0:1, 0:2048], axis=AX.X),
                    waits=[("pe", vnq)])
                v_sqk = E.op("scalar", lambda eng: eng.activation(
                    SCRB[:, :], KTP[:, :], AF.Square), waits=[("pe", vnq)])
                vnk = None
                for s4 in range(4):
                    vnk = mm(PSA[0:1, s4 * 512:(s4 + 1) * 512], ONESCB[:, :],
                             SCRB[:, s4 * 512:(s4 + 1) * 512], start=True,
                             stop=True,
                             waits=[("ac", v_sqk), ("ve", v_qmax)]
                             if s4 == 0 else [],
                             inc="pe" if s4 == 3 else None)
                v_kmax = E.op("vector", lambda eng: eng.reduce_max(
                    RS[0:1, 1:2], PSA[0:1, 0:2048], axis=AX.X),
                    waits=[("pe", vnk)])
                E.op("vector", lambda eng: eng.tensor_scalar(
                    RS[0:1, 2:3], RS[0:1, 0:1], RS[0:1, 1:2], -1.0 / 16,
                    op0=ALU.mult, op1=ALU.mult))
                v_rb = E.op("vector", lambda eng: eng.tensor_scalar(
                    RS[0:1, 2:3], RS[0:1, 2:3], -4.0, None, op0=ALU.add))
                v_bm = mm(PSA[:, 0:1], ONESR[:, :], RS[0:1, 2:3], start=True,
                          stop=True, waits=[("ve", v_rb)], inc="pe")
                v_bias = E.op("vector", lambda eng: eng.tensor_copy(
                    SM[:, 6:7], PSA[:, 0:1]), waits=[("pe", v_bm)])
                v_ptread[(0, 0)] = max(v_ptread[(0, 0)], vnk)
                v_ptread[(1, 0)] = max(v_ptread[(1, 0)], vnk)
                v_lastdvepsa = v_bias
                for qh in range(2):
                    q0 = qh * 1024

                    def emit_pv(sc, qh=qh):
                        par = sc % 2
                        for h01 in range(2):
                            pt = PT[h01]
                            for nn in range(2):
                                w = []
                                if nn == 0:
                                    w = [("ac", v_exp[(sc, h01)])]
                                    if sc == 0 and h01 == 0:
                                        w.append(("ve", E.cur("ve")))
                                v_pv[(sc, h01)] = mm(
                                    PSB[0:65, h01 * 1024 + nn * 512:
                                        h01 * 1024 + (nn + 1) * 512],
                                    VAL[:, sc, h01 * 65:h01 * 65 + 65],
                                    pt[:, par * 1024 + nn * 512:
                                       par * 1024 + (nn + 1) * 512],
                                    start=(sc == 0), stop=(sc == NT - 1),
                                    waits=w,
                                    inc="pe" if nn == 1 else None)
                            v_ptread[(par, h01)] = v_pv[(sc, h01)]

                    for sc in range(NT):
                        par = sc % 2
                        vs = None
                        for nn in range(2):
                            for h01 in range(2):
                                base = h01 * 64
                                w = []
                                if nn == 0:
                                    w = [("ac",
                                          v_exp.get((sc - 1, h01), 0))]
                                    if h01 == 0:
                                        w.append(("ve", v_lastdvepsa))
                                vs = mm(PSA[:, h01 * 1024 + nn * 512:
                                            h01 * 1024 + (nn + 1) * 512],
                                        KTP[base:base + 64,
                                            sc * 128:(sc + 1) * 128],
                                        QTP[base:base + 64,
                                            q0 + nn * 512:
                                            q0 + (nn + 1) * 512],
                                        start=True, stop=True, waits=w,
                                        inc="pe" if (nn == 1 and h01 == 1)
                                        else None)
                        for h01 in range(2):
                            pt = PT[h01]
                            wpt = [("pe", vs),
                                   ("pe", v_ptread[(par, h01)])]
                            if sc == 0 and h01 == 0:
                                wpt.append(("qP", pt0_qp[0]))
                            v_exp[(sc, h01)] = E.op(
                                "scalar",
                                lambda eng, pt=pt, h01=h01, par=par:
                                eng.activation(
                                    pt[:, par * 1024:(par + 1) * 1024],
                                    PSA[:, h01 * 1024:(h01 + 1) * 1024],
                                    AF.Exp, bias=SM[:, 6:7]),
                                waits=wpt)
                        if sc >= 1:
                            emit_pv(sc - 1)
                    emit_pv(NT - 1)
                    c_chunk = p
                    v_rc = E.op("vector", lambda eng: eng.reciprocal(
                        SCRF[0:1, 0:2048], PSB[64:65, 0:2048]),
                        waits=[("pe", v_pv[(NT - 1, 1)])])
                    v_d1 = dma("gpsimd", "qP", rb_scr, SCRF[0:1, 0:2048],
                               waits=[("ve", v_rc), ("qP", bc_qp[0])])
                    rbb = copy.copy(rb_scr)
                    rbb.ap = bass_rust.VecI64Pair([[0, 64], [1, 2048]])
                    v_d2 = dma("gpsimd", "qP", SCRF[0:64, 0:2048], rbb,
                               waits=[("qP", v_d1), ("ve", om_ve[0])])
                    bc_qp[0] = v_d2
                    E.op("vector", lambda eng, c=c_chunk, q0=q0:
                         eng.tensor_tensor(
                             OF[0:64, c, q0:q0 + 1024],
                             PSB[0:64, 0:1024],
                             SCRF[0:64, 0:1024], op=ALU.mult),
                         waits=[("qP", v_d2)])
                    v_om = E.op("vector", lambda eng, q0=q0:
                                eng.tensor_tensor(
                                    PT[0][0:64, q0:q0 + 1024],
                                    PSB[0:64, 1024:2048],
                                    SCRF[0:64, 1024:2048], op=ALU.mult),
                                waits=[("qP", v_d2)])
                    om_ve[0] = v_om
                    v_mvq = dma("gpsimd", "qP",
                                OF[64:128, c_chunk, q0:q0 + 1024],
                                PT[0][0:64, q0:q0 + 1024],
                                waits=[("ve", v_om)])
                    pt0_qp[0] = v_mvq
                    v_exp = {(-1, 0): v_exp[(NT - 1, 0)],
                             (-1, 1): v_exp[(NT - 1, 1)]}
                    v_pv = {}
            v_xadd = {}
            for j in range(NT):
                half = j % 2
                vo = None
                for c in range(8):
                    for nn in range(2):
                        col = half * 1024 + nn * 512
                        w = []
                        if c == 0 and nn == 0:
                            w = [cfg["wob_ready"], cfg["wob_ready2"],
                                 ("qP", pt0_qp[0]),
                                 ("ve", v_xadd.get(j - 2, E.cur("ve")))]
                        vo = mm(PSB[:, col:col + 512],
                                OF[:, c, j * 128:(j + 1) * 128],
                                WOB[:, c, nn * 512:(nn + 1) * 512],
                                start=(c == 0), stop=(c == 7), waits=w,
                                inc="pe" if (c == 7 and nn == 1) else None)
                v_xadd[j] = E.op("vector", lambda eng, j=j, half=half:
                                 eng.scalar_tensor_tensor(
                                     X[:, j, :],
                                     PSB[:, half * 1024:(half + 1) * 1024],
                                     1.0, X[:, j, :], op0=ALU.mult,
                                     op1=ALU.add),
                                 waits=[("pe", vo)])

        if upto >= 5:
            phase_barrier()
            with contextlib.ExitStack() as aes:
                att = {
                    "QTP": aes.enter_context(
                        nc.sbuf_tensor("QTP", [128, 2048], BF16)),
                    "KTP": aes.enter_context(
                        nc.sbuf_tensor("KTP", [128, 2048], BF16)),
                    "VAL": aes.enter_context(
                        nc.sbuf_tensor("VAL", [128, 16, 130], BF16)),
                    "PT": [aes.enter_context(
                        nc.sbuf_tensor(f"PT{i}", [128, 2048], BF16))
                        for i in range(2)],
                    "OF": aes.enter_context(
                        nc.sbuf_tensor("OF", [128, 8, 2048], BF16)),
                    "WQTP": [aes.enter_context(
                        nc.sbuf_tensor(f"WQTP{i}", [128, 8, 128], BF16))
                        for i in range(2)],
                    "WKTP": [aes.enter_context(
                        nc.sbuf_tensor(f"WKTP{i}", [128, 8, 128], BF16))
                        for i in range(2)],
                    "WVTP": [aes.enter_context(
                        nc.sbuf_tensor(f"WVTP{i}", [128, 8, 128], BF16))
                        for i in range(2)],
                    "WOB": aes.enter_context(
                        nc.sbuf_tensor("WOB", [128, 8, 1024], BF16)),
                    "IDENTB": aes.enter_context(
                        nc.sbuf_tensor("IDENTB", [128, 128], BF16)),
                    "ONESRB": aes.enter_context(
                        nc.sbuf_tensor("ONESRB", [1, 128], BF16)),
                    "BVR": aes.enter_context(
                        nc.sbuf_tensor("BVR", [1, 1024], BF16)),
                    "BTL": aes.enter_context(
                        nc.sbuf_tensor("BTL", [128, 1024], F32)),
                    "STG": aes.enter_context(
                        nc.sbuf_tensor("STG", [128, 2, 1024], BF16)),
                    "CONSTV": aes.enter_context(
                        nc.sbuf_tensor("CONSTV", [128, 2], F32)),
                }
                print("sbuf remaining in L0 attn scope:",
                      nc.sbuf_bytes_remaining)
                dma("sync", "qS", att["IDENTB"][:, :], identb_in)
                dma("sync", "qS", att["ONESRB"][:, :], onesrb_in)
                vbvr = dma("gpsimd", "qP", att["BVR"][:, :],
                           bv_in.rearrange("(a d) -> a d", a=1))
                E.op("vector", lambda eng: eng.memset(
                    att["CONSTV"][:, 0:1], 0.125))
                stg = att["STG"]
                v_store = 0
                for wi, (src, dst) in enumerate(
                        ((wq_n, wq_s), (wk_n, wk_s), (wv_n, wv_s))):
                    for ch in range(8):
                        par = ch % 2
                        vl = dma("gpsimd", "qP", stg[:, par, :],
                                 src[ch * 128:(ch + 1) * 128, :],
                                 waits=[("qS", v_store)] if (wi + ch) else [])
                        v_store = dma("sync", "qS",
                                      dst[ch * 128:(ch + 1) * 128, :],
                                      stg[:, par, :], waits=[("qP", vl)])
                for wsrc, wdst in ((wq_s, wqT_s), (wk_s, wkT_s),
                                   (wv_s, wvT_s)):
                    for dc in range(NDC):
                        par = dc % 2
                        vl = dma("sync", "qS", stg[:, par, :],
                                 wsrc[:, dc * 128:(dc + 1) * 128],
                                 transpose=True, waits=[("qS", v_store)])
                        v_store = dma("sync", "qS",
                                      wdst[dc * 128:(dc + 1) * 128, :],
                                      stg[:, par, :], waits=[("qS", vl)])
                v_cast_done = v_store
                scrw = SCRF[:, 0:1024].rearrange("p (g k) -> p g k", k=64)
                v_wev = 0
                v_wmv = 0
                for g in range(8):
                    vl = dma("sync", "qS", scrw,
                             wo_n.rearrange("(g p) k -> p g k", p=128)
                             [:, g * 16:(g + 1) * 16, :],
                             waits=[("ve", v_wev)] if g else
                                   [("ve", E.cur("ve")), ("ac", E.cur("ac"))])
                    for hh in range(2):
                        vt = None
                        for ss in range(8):
                            w = [("qS", vl)] if (hh == 0 and ss == 0) else []
                            if hh == 1:
                                w = [("qP", v_wmv)] if ss == 0 else []
                            vt = E.op("tensor",
                                      lambda eng, hh=hh, ss=ss:
                                      eng.matmul(
                                          PSA[0:64,
                                              hh * 1024 + ss * 128:
                                              hh * 1024 + (ss + 1) * 128],
                                          scrw[:, hh * 8 + ss, :],
                                          IDENT[0:128, 0:128],
                                          start=True, stop=True,
                                          is_transpose=True),
                                      waits=w,
                                      inc="pe" if ss == 7 else None)
                        if hh == 0:
                            v_wev = E.op("vector", lambda eng, g=g:
                                         eng.tensor_copy(
                                             att["WOB"][0:64, g, :],
                                             PSA[0:64, 0:1024]),
                                         waits=[("pe", vt)])
                        else:
                            v_wev = E.op("vector", lambda eng:
                                         eng.tensor_copy(
                                             stg[0:64, 0, :],
                                             PSA[0:64, 1024:2048]),
                                         waits=[("pe", vt)])
                            v_wmv = dma("gpsimd", "qP",
                                        att["WOB"][64:128, g, :],
                                        stg[0:64, 0, :],
                                        waits=[("ve", v_wev)])
                v_wob_ve = v_wev
                v_wob_qp = v_wmv

                def load_pair0(p, par):
                    v = 0
                    first = True
                    for wsrc, wdst in ((wqT_s, att["WQTP"][par]),
                                       (wkT_s, att["WKTP"][par]),
                                       (wvT_s, att["WVTP"][par])):
                        w = []
                        if first:
                            w = [("qS", v_cast_done), ("pe", E.cur("pe"))]
                            first = False
                        v = dma("sync", "qS", wdst[:, :, :],
                                wsrc.rearrange("(dc p) o -> p dc o", p=128)
                                [:, :, p * 128:(p + 1) * 128], waits=w)
                    return ("qS", v)

                emit_bias_tile(att["BTL"], bo_in)
                cfg0 = {
                    "load_pair": load_pair0,
                    "qscale": lambda p: att["CONSTV"][:, 0:1],
                    "qbias": lambda p: BQ8[:, p:p + 1],
                    "kbias": lambda p: BKs[:, p:p + 1],
                    "vbias_row": lambda p, vb=vbvr:
                        att["BVR"][0:1, p * 128:(p + 1) * 128],
                    "wob_ready": ("ve", v_wob_ve),
                    "wob_ready2": ("qP", v_wob_qp),
                    "vbias_ready": ("qP", vbvr),
                }
                emit_attention(att, cfg0)
            phase_barrier()

        if upto >= 6:
            phase_barrier()
            with contextlib.ExitStack() as mes:
                HT = mes.enter_context(
                    nc.sbuf_tensor("HT", [128, 16, 1024], BF16))
                W2B = mes.enter_context(
                    nc.sbuf_tensor("W2B", [128, 16, 1024], BF16))
                W1B = [mes.enter_context(
                    nc.sbuf_tensor(f"W1B{i}", [128, 8, 4, 128], BF16))
                    for i in range(2)]
                print("sbuf remaining in MoE scope:", nc.sbuf_bytes_remaining)
                v_w1ld = {}
                v_gelu = {}
                v_xupd = {}
                v_w2mm_last = 0
                v_humm = {}
                for e in range(8):
                    v_w2 = dma("gpsimd", "qP", W2B[:, :, :],
                               w2_in[e * 2048:(e + 1) * 2048, :].rearrange(
                                   "(fc p) d -> p fc d", p=128),
                               waits=[("pe", v_w2mm_last)])
                    for th in range(2):
                        for fc in range(16):
                            par = (fc // 4) % 2
                            if fc % 4 == 0:
                                v_w1ld[par] = dma(
                                    "gpsimd", "qP", W1B[par][:, :, :, :],
                                    w1_in[e * 1024:(e + 1) * 1024,
                                          fc * 128:(fc + 4) * 128].rearrange(
                                        "(dc p) (g f) -> p dc g f",
                                        p=128, f=128),
                                    waits=[("pe",
                                            v_humm.get((e, th, fc - 5),
                                                       E.cur("pe")))])
                            half = fc % 2
                            ps = PSA[:, half * 1024:(half + 1) * 1024]
                            vh = None
                            for dc in range(NDC):
                                for nn in range(2):
                                    w = []
                                    if dc == 0 and nn == 0:
                                        w = [("qP", v_w1ld[par]),
                                             ("ac", v_gelu.get(fc - 2, 0))]
                                        if e == 0 and th == 0 and fc == 0:
                                            w.append(("ve", E.cur("ve")))
                                            w.append(("ac", E.cur("ac")))
                                    vh = mm(ps[:, nn * 512:(nn + 1) * 512],
                                            W1B[par][:, dc, fc % 4, :],
                                            XF[:, dc, th * 1024 + nn * 512:
                                               th * 1024 + (nn + 1) * 512],
                                            start=(dc == 0), stop=(dc == 7),
                                            waits=w,
                                            inc="pe" if (dc == 7 and nn == 1)
                                            else None)
                            v_humm[(e, th, fc)] = vh
                            v_gelu[fc] = E.op(
                                "scalar", lambda eng, fc=fc, ps=ps:
                                eng.activation(HT[:, fc, :], ps, AF.Gelu),
                                waits=[("pe", vh)])
                        v_last_gelu = v_gelu[15]
                        for tt in range(8):
                            j = th * 8 + tt
                            half = tt % 2
                            ps = PSB[:, half * 1024:(half + 1) * 1024]
                            vw = None
                            for fc in range(16):
                                for nn in range(2):
                                    w = []
                                    if fc == 0 and nn == 0:
                                        w = [("ac", v_last_gelu),
                                             ("qP", v_w2),
                                             ("ve", v_xupd.get(tt - 2,
                                                               E.cur("ve")))]
                                    vw = mm(ps[:, nn * 512:(nn + 1) * 512],
                                            HT[:, fc,
                                               tt * 128:(tt + 1) * 128],
                                            W2B[:, fc,
                                                nn * 512:(nn + 1) * 512],
                                            start=(fc == 0), stop=(fc == 15),
                                            waits=w,
                                            inc="pe" if (fc == 15 and nn == 1)
                                            else None)
                            v_w2mm_last = vw
                            v_xupd[tt] = E.op(
                                "vector", lambda eng, j=j, e=e, ps=ps:
                                eng.scalar_tensor_tensor(
                                    X[:, j, :], ps, GATES[:, j, e:e + 1],
                                    X[:, j, :], op0=ALU.mult, op1=ALU.add),
                                waits=[("pe", vw)])
                        v_gelu = {}
                        v_xupd = {}
            phase_barrier()

        if upto >= 7:
            phase_barrier()
            emit_layernorm_to_XF([("ve", E.cur("ve")), ("ac", E.cur("ac")),
                                  ("pe", E.cur("pe"))], router=False)

        if upto >= 8:
            phase_barrier()
            with contextlib.ExitStack() as aes:
                att = {
                    "QTP": aes.enter_context(
                        nc.sbuf_tensor("QTP1", [128, 2048], BF16)),
                    "KTP": aes.enter_context(
                        nc.sbuf_tensor("KTP1", [128, 2048], BF16)),
                    "VAL": aes.enter_context(
                        nc.sbuf_tensor("VAL1", [128, 16, 130], BF16)),
                    "PT": [aes.enter_context(
                        nc.sbuf_tensor(f"PT1_{i}", [128, 2048], BF16))
                        for i in range(2)],
                    "OF": aes.enter_context(
                        nc.sbuf_tensor("OF1", [128, 8, 2048], BF16)),
                    "WQTP": [aes.enter_context(
                        nc.sbuf_tensor(f"WQTP1_{i}", [128, 8, 128], BF16))
                        for i in range(2)],
                    "WKTP": [aes.enter_context(
                        nc.sbuf_tensor(f"WKTP1_{i}", [128, 8, 128], BF16))
                        for i in range(2)],
                    "WVTP": [aes.enter_context(
                        nc.sbuf_tensor(f"WVTP1_{i}", [128, 8, 128], BF16))
                        for i in range(2)],
                    "WOB": aes.enter_context(
                        nc.sbuf_tensor("WOB1", [128, 8, 1024], BF16)),
                    "IDENTB": aes.enter_context(
                        nc.sbuf_tensor("IDENTB1", [128, 128], BF16)),
                    "ONESRB": aes.enter_context(
                        nc.sbuf_tensor("ONESRB1", [1, 128], BF16)),
                    "BTL": aes.enter_context(
                        nc.sbuf_tensor("BTL1", [128, 1024], F32)),
                    "CONSTV": aes.enter_context(
                        nc.sbuf_tensor("CONSTV1", [128, 2], F32)),
                }
                print("sbuf remaining in L1 attn scope:",
                      nc.sbuf_bytes_remaining)
                dma("sync", "qS", att["ONESRB"][:, :], onesrb_in)
                E.op("vector", lambda eng: eng.memset(
                    att["CONSTV"][:, 0:1], 0.125))
                v_sg = dma("sync", "qS", RS[0:1, 8:9], wsig_in)
                v_sig = E.op("scalar", lambda eng: eng.activation(
                    RS[0:1, 9:10], RS[0:1, 8:9], AF.Sigmoid),
                    waits=[("qS", v_sg)])
                v_sf = E.op("vector", lambda eng: eng.tensor_scalar(
                    RS[0:1, 10:11], RS[0:1, 9:10], -0.125, None,
                    op0=ALU.mult), waits=[("ac", v_sig)])
                v_sfm = mm(PSB[:, 0:1], ONESR[:, :], RS[0:1, 10:11],
                           start=True, stop=True,
                           waits=[("ve", v_sf), ("ac", E.cur("ac"))],
                           inc="pe")
                E.op("vector", lambda eng: eng.tensor_copy(
                    SM[:, 7:8], PSB[:, 0:1]), waits=[("pe", v_sfm)])
                v_wob1 = dma("gpsimd", "qP", att["WOB"][:, :, :],
                             wout_in.rearrange("(c p) o -> p c o", p=128))

                def load_pair1(p, par):
                    if p < 2:
                        qsrc, ksrc, c0 = wqt_in, wkt_in, p * 128
                    else:
                        qsrc, ksrc, c0 = wqs_in, wks_in, (p - 2) * 128
                    v = dma("gpsimd", "qP", att["WQTP"][par][:, :, :],
                            qsrc.rearrange("(dc p) o -> p dc o", p=128)
                            [:, :, c0:c0 + 128],
                            waits=[("pe", E.cur("pe"))])
                    v = dma("gpsimd", "qP", att["WKTP"][par][:, :, :],
                            ksrc.rearrange("(dc p) o -> p dc o", p=128)
                            [:, :, c0:c0 + 128])
                    v = dma("gpsimd", "qP", att["WVTP"][par][:, :, :],
                            wv1_in.rearrange("(dc p) o -> p dc o", p=128)
                            [:, :, p * 128:(p + 1) * 128])
                    return ("qP", v)

                emit_bias_tile(att["BTL"], bout_in)
                cfg1 = {
                    "load_pair": load_pair1,
                    "qscale": lambda p: (SM[:, 7:8] if p < 2
                                         else att["CONSTV"][:, 0:1]),
                    "qbias": lambda p: 0.0,
                    "kbias": lambda p: 0.0,
                    "vbias_row": lambda p: None,
                    "wob_ready": ("qP", v_wob1),
                    "wob_ready2": ("qP", v_wob1),
                    "vbias_ready": ("qP", v_wob1),
                }
                emit_attention(att, cfg1)
            phase_barrier()

        if upto >= 9:
            phase_barrier()
            with contextlib.ExitStack() as tes:
                TSB = tes.enter_context(
                    nc.sbuf_tensor("TSB", [128, 1024], BF16))
                TPF = tes.enter_context(
                    nc.sbuf_tensor("TPF", [128, 4096], F32))
                v_tst = 0
                vtprev = [0]

                def tern_chunk(load_buf, wsl, ncols, dst_dma_fn, vprev):
                    v = E.op("vector", lambda eng, w=wsl: eng.tensor_reduce(
                        SM[:, 0:1], w, axis=AX.X, op=ALU.add,
                        apply_absolute_value=True))
                    E.op("vector", lambda eng, n=ncols: eng.tensor_scalar(
                        SM[:, 0:1], SM[:, 0:1], 1.0 / n, None, op0=ALU.mult))
                    E.op("vector", lambda eng: eng.tensor_scalar_max(
                        SM[:, 0:1], SM[:, 0:1], 1e-5))
                    E.op("vector", lambda eng: eng.tensor_scalar(
                        SM[:, 1:2], SM[:, 0:1], 0.5, None, op0=ALU.mult))
                    E.op("vector", lambda eng: eng.tensor_scalar(
                        SM[:, 2:3], SM[:, 1:2], -1.0, None, op0=ALU.mult))
                    vlast = 0
                    for hh in range((ncols + 1023) // 1024):
                        wpart = wsl[:, hh * 1024:(hh + 1) * 1024]
                        E.op("vector", lambda eng, w=wpart: eng.tensor_scalar(
                            SCRF[:, 0:1024], w, SM[:, 1:2], None,
                            op0=ALU.is_ge), waits=[("pe", vtprev[0])])
                        E.op("vector", lambda eng, w=wpart: eng.tensor_scalar(
                            SCRF[:, 1024:2048], w, SM[:, 2:3], None,
                            op0=ALU.is_le))
                        E.op("vector", lambda eng: eng.tensor_sub(
                            SCRF[:, 0:1024], SCRF[:, 0:1024],
                            SCRF[:, 1024:2048]))
                        v_wef = E.op("vector", lambda eng: eng.tensor_scalar(
                            SCRF[:, 0:1024], SCRF[:, 0:1024], SM[:, 0:1],
                            None, op0=ALU.mult))
                        vt = None
                        for b in range(8):
                            w = [("ve", v_wef)]
                            if b == 0 and vprev[0] > 0:
                                w.append(("qS", vprev[0]))
                            vt = tr(PSA[:, b * 128:(b + 1) * 128],
                                    SCRF[:, b * 128:(b + 1) * 128],
                                    waits=w if b == 0 else (),
                                    inc="pe" if b == 7 else None)
                        vtprev[0] = vt
                        v_ev = E.op("vector", lambda eng: eng.tensor_copy(
                            TSB[:, :], PSA[:, 0:1024]), waits=[("pe", vt)])
                        vlast = dst_dma_fn(hh, v_ev)
                        vprev[0] = vlast
                    return vlast

                vprev = [0]
                for ch in range(32):
                    vl = dma("sync", "qS", TPF[:, 0:1024],
                             wfc_in[ch * 128:(ch + 1) * 128, :],
                             waits=[("ve", E.cur("ve"))])
                    E.wait("vector", [("qS", vl)])

                    def dst_fc(hh, v_ev, ch=ch):
                        return dma("sync", "qS",
                                   wefcT[:, ch * 128:(ch + 1) * 128]
                                   .rearrange("(dc p) f -> p dc f", p=128),
                                   TSB[:, :].rearrange(
                                       "p (dc f) -> p dc f", f=128),
                                   waits=[("ve", v_ev)])
                    tern_chunk(None, TPF[:, 0:1024], 1024, dst_fc, vprev)
                for ch in range(8):
                    vl = dma("sync", "qS", TPF[:, :],
                             wproj_in[ch * 128:(ch + 1) * 128, :],
                             waits=[("ve", E.cur("ve"))])
                    E.wait("vector", [("qS", vl)])

                    def dst_pj(hh, v_ev, ch=ch):
                        return dma("sync", "qS",
                                   wepT[:, ch * 128:(ch + 1) * 128]
                                   .rearrange("(q p) d -> p q d", p=128)
                                   [:, hh * 8:(hh + 1) * 8, :],
                                   TSB[:, :].rearrange(
                                       "p (q d) -> p q d", d=128),
                                   waits=[("ve", v_ev)])
                    tern_chunk(None, TPF[:, :], 4096, dst_pj, vprev)
            phase_barrier()

        if upto >= 10:
            with contextlib.ExitStack() as fes:
                R2T = fes.enter_context(
                    nc.sbuf_tensor("R2T", [128, 32, 512], BF16))
                WF1 = [fes.enter_context(
                    nc.sbuf_tensor(f"WF1_{i}", [128, 8, 2, 128], BF16))
                    for i in range(2)]
                WPB = fes.enter_context(
                    nc.sbuf_tensor("WPB", [128, 32, 512], BF16))
                ABT = fes.enter_context(nc.sbuf_tensor("ABT", [16, 2048], BF16))
                BBT = fes.enter_context(nc.sbuf_tensor("BBT", [16, 2048], BF16))
                UT = fes.enter_context(nc.sbuf_tensor("UT", [16, 4096], BF16))
                UF = fes.enter_context(nc.sbuf_tensor("UF", [128, 512], F32))
                VTB = fes.enter_context(nc.sbuf_tensor("VTB", [128, 8, 16], BF16))
                FBT = fes.enter_context(nc.sbuf_tensor("FBT", [16, 16], BF16))
                print("sbuf remaining in L1 ffn scope:",
                      nc.sbuf_bytes_remaining)
                v_vm = dma("sync", "qS", SCRF[0:16, 0:1024], vm_in)
                vt = None
                for dc in range(NDC):
                    vt = tr(PSA[:, dc * 16:(dc + 1) * 16],
                            SCRF[0:16, dc * 128:(dc + 1) * 128],
                            waits=([("qS", v_vm), ("ve", E.cur("ve")),
                                    ("ac", E.cur("ac"))] if dc == 0 else ()),
                            inc="pe" if dc == 7 else None, np_=16)
                E.op("vector", lambda eng: eng.tensor_copy(
                    VTB[:, :, :], PSA[:, 0:128].rearrange(
                        "p (dc r) -> p dc r", r=16)), waits=[("pe", vt)])
                v_fb = dma("sync", "qS", SCRF[0:16, 1024:1040], fastb_in)
                v_tfb = tr(PSB[0:16, 0:16], SCRF[0:16, 1024:1040],
                           waits=[("qS", v_fb)], inc="pe", np_=16)
                E.op("vector", lambda eng: eng.tensor_copy(
                    FBT[:, :], PSB[0:16, 0:16]), waits=[("pe", v_tfb)])
                v_g = dma("sync", "qS", RS[0:1, 11:12], gate_in)
                v_u = dma("sync", "qS", UF[:, :].rearrange(
                    "p (c r) -> p c r", r=16),
                    u_in.rearrange("(c p) r -> p c r", p=128))
                v_gb = mm(PSB[:, 16:17], ONESR[:, :], RS[0:1, 11:12],
                          start=True, stop=True,
                          waits=[("qS", v_g), ("ve", E.cur("ve"))], inc="pe")
                E.op("vector", lambda eng: eng.tensor_copy(
                    SM[:, 8:9], PSB[:, 16:17]), waits=[("pe", v_gb)])
                v_ug = E.op("vector", lambda eng: eng.tensor_scalar(
                    UF[:, :], UF[:, :], SM[:, 8:9], None, op0=ALU.mult),
                    waits=[("qS", v_u)])
                v_ue = v_ug
                for rnd in range(2):
                    vt = None
                    for cc in range(16):
                        c = rnd * 16 + cc
                        vt = tr(PSA[0:16, cc * 128:(cc + 1) * 128],
                                UF[:, c * 16:(c + 1) * 16],
                                waits=([("ve", v_ue)] if cc == 0 else ()),
                                inc="pe" if cc == 15 else None)
                    v_ue = E.op("vector", lambda eng, rnd=rnd:
                                eng.tensor_copy(
                                    UT[0:16, rnd * 2048:(rnd + 1) * 2048],
                                    PSA[0:16, 0:2048]), waits=[("pe", vt)])
                va = None
                for s4 in range(4):
                    for dc in range(NDC):
                        w = [("ve", v_ue)] if (s4 == 0 and dc == 0) else []
                        va = mm(PSA[0:16, s4 * 512:(s4 + 1) * 512],
                                VTB[:, dc, :],
                                XF[:, dc, s4 * 512:(s4 + 1) * 512],
                                start=(dc == 0), stop=(dc == 7), waits=w,
                                inc="pe" if (s4 == 3 and dc == 7) else None)
                v_abt = E.op("vector", lambda eng: eng.tensor_copy(
                    ABT[:, :], PSA[0:16, 0:2048]), waits=[("pe", va)])
                vb = None
                for s4 in range(4):
                    w = [("ve", v_abt)] if s4 == 0 else []
                    vb = mm(PSB[0:16, s4 * 512:(s4 + 1) * 512], FBT[:, :],
                            ABT[0:16, s4 * 512:(s4 + 1) * 512], start=True,
                            stop=True, waits=w,
                            inc="pe" if s4 == 3 else None)
                v_bbt = E.op("vector", lambda eng: eng.tensor_copy(
                    BBT[:, :], PSB[0:16, 0:2048]), waits=[("pe", vb)])
                v_xupd_last = E.cur("ve")
                v_wf1 = {}
                v_wp1 = {}
                v_relu = {}
                v_sq = {}
                for tq in range(4):
                    for fc in range(32):
                        par = (fc // 2) % 2
                        if fc % 2 == 0:
                            v_wf1[par] = dma(
                                "sync", "qS", WF1[par][:, :, :, :],
                                wefcT[:, fc * 128:(fc + 2) * 128].rearrange(
                                    "(dc p) (g f) -> p dc g f", p=128, f=128),
                                waits=[("pe",
                                        v_humm.get(fc - 3, E.cur("pe")))])
                        half = fc % 4
                        ps = PSA[:, half * 512:(half + 1) * 512]
                        vh = None
                        for dc in range(NDC):
                            w = []
                            if dc == 0:
                                w = [("qS", v_wf1[par]),
                                     ("ac", v_relu.get(fc - 4, 0)),
                                     ("ve", v_xupd_last if fc == 0 else 0)]
                                if tq == 0 and fc == 0:
                                    w.append(("ve", v_bbt))
                            vh = mm(ps, WF1[par][:, dc, fc % 2, :],
                                    XF[:, dc, tq * 512:(tq + 1) * 512],
                                    start=(dc == 0), stop=False, waits=w,
                                    inc=None)
                        vh = mm(ps, UT[0:16, fc * 128:(fc + 1) * 128],
                                BBT[0:16, tq * 512:(tq + 1) * 512],
                                start=False, stop=True, inc="pe")
                        v_humm[fc] = vh
                        sh = (fc % 2) * 512
                        v_rl = E.op("scalar", lambda eng, ps=ps, sh=sh:
                                    eng.activation(SCRF[:, sh:sh + 512], ps,
                                                   AF.Relu),
                                    waits=[("pe", vh),
                                           ("ve", v_sq.get(fc - 2, 0))])
                        v_relu[fc] = v_rl
                        v_sq[fc] = E.op("vector", lambda eng, fc=fc, sh=sh:
                                        eng.tensor_tensor(
                                            R2T[:, fc, :],
                                            SCRF[:, sh:sh + 512],
                                            SCRF[:, sh:sh + 512],
                                            op=ALU.mult),
                                        waits=[("ac", v_rl)])
                    v_r2_done = E.cur("ve")
                    vpr = {}
                    for dh in range(2):
                        v_wpb = dma(
                            "sync", "qS", WPB[:, :, :],
                            wepT[:, dh * 512:(dh + 1) * 512].rearrange(
                                "(fc p) d -> p fc d", p=128),
                            waits=[("pe", vpr.get((dh - 1, 31, 3),
                                                  E.cur("pe")))])
                        for fc in range(32):
                            for tt in range(4):
                                psv = (PSA if dh == 0 else PSB)[
                                    :, tt * 512:(tt + 1) * 512]
                                w = []
                                if fc == 0 and tt == 0:
                                    w = [("qS", v_wpb),
                                         ("ve", v_r2_done),
                                         ("ac", v_relu[31])]
                                vpr[(dh, fc, tt)] = mm(
                                    psv, R2T[:, fc, tt * 128:(tt + 1) * 128],
                                    WPB[:, fc, :], start=(fc == 0),
                                    stop=(fc == 31), waits=w,
                                    inc="pe" if tt == 3 else None)
                    for tt in range(4):
                        j = tq * 4 + tt
                        for dh in range(2):
                            psv = (PSA if dh == 0 else PSB)[
                                :, tt * 512:(tt + 1) * 512]
                            v_xupd_last = E.op(
                                "vector", lambda eng, j=j, dh=dh, psv=psv:
                                eng.scalar_tensor_tensor(
                                    X[:, j, dh * 512:(dh + 1) * 512], psv,
                                    1.0, X[:, j, dh * 512:(dh + 1) * 512],
                                    op0=ALU.mult, op1=ALU.add),
                                waits=[("pe", vpr[(dh, 31, 3)])])
                    v_relu = {}
                    v_sq = {}
                    v_humm = {}
            phase_barrier()

        marks = E.all_marks()
        vst = dma("sync", "qS", y_out.rearrange("(j p) d -> p j d", p=128),
                  X[:, :, :], waits=marks)
        E.wait("sync", [("qS", vst)])
        for name, ap in dbg_aps.items():
            src = {"XF": XF[:, :, :], "X": X[:, :, :], "SCRF": SCRF[:, :],
                   "PROBS": PROBS[:, :, :], "GATES": GATES[:, :, :],
                   "BIS": BIS[:, :], "B128": B128[:, :], "SM": SM[:, :]}[name]
            v = dma("gpsimd", "qP", ap, src, waits=marks)
            E.wait("gpsimd", [("qP", v)])

        with nc.Block() as block:
            @block.sync
            def _(eng):
                for fn in E.q["sync"]:
                    fn(eng, sems)

            @block.scalar
            def _(eng):
                for fn in E.q["scalar"]:
                    fn(eng, sems)

            @block.vector
            def _(eng):
                for fn in E.q["vector"]:
                    fn(eng, sems)

            @block.tensor
            def _(eng):
                for fn in E.q["tensor"]:
                    fn(eng, sems)

            @block.gpsimd
            def _(eng):
                for fn in E.q["gpsimd"]:
                    fn(eng, sems)

    return nc


def make_in_map(inputs, core):
    f = np.ascontiguousarray

    def a32(v):
        return np.asarray(v, dtype=np.float32)

    m = {
        "x": f(a32(inputs["L0_attn_x"][core])),
        "identf": np.eye(128, dtype=np.float32),
        "identb": np.eye(128, dtype=np.float32).astype(ml_dtypes.bfloat16),
        "onesrb": np.ones((1, 128), dtype=np.float32).astype(ml_dtypes.bfloat16),
        "onesr": np.ones((1, 128), dtype=np.float32),
        "onescb": np.ones((128, 1), dtype=np.float32).astype(ml_dtypes.bfloat16),
        "foldw": f((np.arange(128)[:, None] % 8 ==
                    np.arange(128)[None, :] % 8).astype(np.float32)),
        "wq_n": f(a32(inputs["L0_attn_W_Q"]).reshape(1024, 1024)),
        "wk_n": f(a32(inputs["L0_attn_W_K"]).reshape(1024, 1024)),
        "wv_n": f(a32(inputs["L0_attn_W_V"]).reshape(1024, 1024)),
        "wo_n": f(a32(inputs["L0_attn_W_O"]).reshape(16384, 64)),
        "bq": f(a32(inputs["L0_attn_b_Q"]).reshape(1024)),
        "bk": f(a32(inputs["L0_attn_b_K"]).reshape(1024)),
        "bv": f(a32(inputs["L0_attn_b_V"]).reshape(1024)),
        "bo": f(a32(inputs["L0_attn_b_O"]).reshape(1, 1024)),
        "wg": f(a32(inputs["L0_ffn_Wg"])),
        "w1": f(a32(inputs["L0_ffn_W1"]).reshape(8 * 1024, 2048)),
        "w2": f(a32(inputs["L0_ffn_W2"]).reshape(8 * 2048, 1024)),
        "wqt": f(a32(inputs["L1_attn_Wqt"])),
        "wkt": f(a32(inputs["L1_attn_Wkt"])),
        "wqs": f(a32(inputs["L1_attn_Wqs"])),
        "wks": f(a32(inputs["L1_attn_Wks"])),
        "wv1": f(a32(inputs["L1_attn_Wv"])),
        "wout": f(a32(inputs["L1_attn_Wout"])),
        "bout": f(a32(inputs["L1_attn_bout"]).reshape(1, 1024)),
        "wsig": f(a32(inputs["L1_attn_w_sigma"]).reshape(1, 1)),
        "gate": f(a32(inputs["L1_ffn_gate"]).reshape(1, 1)),
        "fastb": f(a32(inputs["L1_ffn_fast_b"])),
        "u": f(a32(inputs["L1_ffn_u"])),
        "vm": f(a32(inputs["L1_ffn_v"])),
        "wfc": f(a32(inputs["L1_ffn_w_fc"])),
        "wproj": f(a32(inputs["L1_ffn_w_proj"])),
    }
    return m


def kernel(**inputs):
    nc = build_program()
    in_maps = [make_in_map(inputs, c) for c in range(NCORES)]
    res = run_bass_kernel_spmd(nc, in_maps, core_ids=list(range(NCORES)))
    out = np.stack([np.asarray(res.results[c]["y"], dtype=np.float32)
                    for c in range(NCORES)], axis=0)
    return out


# version 07
# speedup vs baseline: 1.1761x; 1.1761x over previous
import contextlib
import copy
import numpy as np
import bass_rust
import ml_dtypes
import concourse.bass as bass
import concourse.mybir as mybir
from concourse.bass_utils import run_bass_kernel_spmd
from concourse.replica_groups import maybe_share_collective_output_space

F32 = mybir.dt.float32
BF16 = mybir.dt.bfloat16
AF = mybir.ActivationFunctionType
ALU = mybir.AluOpType
AX = mybir.AxisListType

T = 2048
D = 1024
NT = 16
NDC = 8
NCORES = 8
BIS_ITERS = 50

ENGINES = ("sync", "scalar", "vector", "tensor", "gpsimd")


class Emitter:
    def __init__(self):
        self.q = {e: [] for e in ENGINES}
        self.ctr = {}

    def cur(self, sem):
        return self.ctr.get(sem, 0)

    def plan(self, sem, k=1):
        self.ctr[sem] = self.ctr.get(sem, 0) + k
        return self.ctr[sem]

    CHAIN = {"vector": "ve", "scalar": "ac"}

    def op(self, engine, build, waits=(), inc=None, inc_k=1):
        waits = list(waits)
        csem = self.CHAIN.get(engine)
        if csem is not None:
            waits.append((csem, self.cur(csem)))
            if inc is None:
                inc = csem
            assert inc == csem and inc_k == 1
        v = self.plan(inc, inc_k) if inc else None

        def fn(eng, S, _w=tuple(waits), _b=build, _i=inc, _k=inc_k):
            for s, val in _w:
                if val > 0:
                    eng.wait_ge(S[s], val)
            ins = _b(eng)
            if _i is not None and ins is not None:
                ins.then_inc(S[_i], _k)
        self.q[engine].append(fn)
        return v

    def wait(self, engine, waits):
        def fn(eng, S, _w=tuple(waits)):
            for s, val in _w:
                if val > 0:
                    eng.wait_ge(S[s], val)
        self.q[engine].append(fn)

    def all_marks(self):
        return [(s, self.cur(s)) for s in ("qS", "qA", "qP", "pe", "ve",
                                           "ac", "gp", "cc")]


def build_program(upto=99, debug=(), cc_local=False, paranoid=False, lnv=0, no_rmm=False, no_p2=False):
    nc = bass.Bass("TRN2", target_bir_lowering=False, debug=False,
                   detect_race_conditions=False)
    E = Emitter()

    def din(name, shape, dtype=F32):
        return nc.dram_tensor(name, shape, dtype, kind="ExternalInput").ap()

    x_in = din("x", [T, D])
    identf = din("identf", [128, 128])
    identb_in = din("identb", [128, 128], BF16)
    onesrb_in = din("onesrb", [1, 128], BF16)
    onesr = din("onesr", [1, 128])
    onescb = din("onescb", [128, 1], BF16)
    foldw = din("foldw", [128, 128])
    wq_n = din("wq_n", [1024, 1024]); wk_n = din("wk_n", [1024, 1024])
    wv_n = din("wv_n", [1024, 1024]); wo_n = din("wo_n", [16384, 64])
    bq_in = din("bq", [1024]); bk_in = din("bk", [1024]); bv_in = din("bv", [1024])
    bo_in = din("bo", [1, 1024])
    wg_in = din("wg", [1024, 8])
    w1_in = din("w1", [8 * 1024, 2048])
    w2_in = din("w2", [8 * 2048, 1024])
    wqt_in = din("wqt", [1024, 256]); wkt_in = din("wkt", [1024, 256])
    wqs_in = din("wqs", [1024, 768]); wks_in = din("wks", [1024, 768])
    wv1_in = din("wv1", [1024, 1024]); wout_in = din("wout", [1024, 1024])
    bout_in = din("bout", [1, 1024])
    wsig_in = din("wsig", [1, 1]); gate_in = din("gate", [1, 1])
    fastb_in = din("fastb", [16, 16]); u_in = din("u", [4096, 16])
    vm_in = din("vm", [16, 1024])
    wfc_in = din("wfc", [4096, 1024]); wproj_in = din("wproj", [1024, 4096])

    y_out = nc.dram_tensor("y", [T, D], F32, kind="ExternalOutput").ap()
    dbg_aps = {}
    for name, shape, dtype in debug:
        dbg_aps[name] = nc.dram_tensor("dbg_" + name, list(shape), dtype,
                                       kind="ExternalOutput").ap()

    wq_s = nc.dram_tensor("wq_s", [1024, 1024], BF16).ap()
    wk_s = nc.dram_tensor("wk_s", [1024, 1024], BF16).ap()
    wv_s = nc.dram_tensor("wv_s", [1024, 1024], BF16).ap()
    wqT_s = nc.dram_tensor("wqT_s", [1024, 1024], BF16).ap()
    wkT_s = nc.dram_tensor("wkT_s", [1024, 1024], BF16).ap()
    wvT_s = nc.dram_tensor("wvT_s", [1024, 1024], BF16).ap()
    wefcT = nc.dram_tensor("wefcT", [1024, 4096], BF16).ap()
    rb_scr = nc.dram_tensor("rb_scr", [1, 2048], F32).ap()
    wepT = nc.dram_tensor("wepT", [4096, 1024], BF16).ap()
    rg = [list(range(NCORES))]
    cc = {}

    def get_cc():
        if not cc:
            cc_space = ("Local" if cc_local else
                        maybe_share_collective_output_space("AllGather", rg))
            cc["in"] = nc.dram_tensor("cc_in", [8, T], F32, kind="Internal")
            cc["out"] = nc.dram_tensor("cc_out", [8 * NCORES, T], F32,
                                       kind="Internal", addr_space=cc_space)
        return cc["in"], cc["out"]

    es = contextlib.ExitStack()
    with es:
        X = es.enter_context(nc.sbuf_tensor("Xs", [128, NT, D], F32))
        XF = es.enter_context(nc.sbuf_tensor("XFs", [128, NDC, T], BF16))
        IDENT = es.enter_context(nc.sbuf_tensor("IDENT", [128, 128], F32))
        ONESR = es.enter_context(nc.sbuf_tensor("ONESR", [1, 128], F32))
        ONESCB = es.enter_context(nc.sbuf_tensor("ONESCB", [128, 1], BF16))
        FOLDW = es.enter_context(nc.sbuf_tensor("FOLDW", [128, 128], F32))
        BQ8 = es.enter_context(nc.sbuf_tensor("BQ8", [128, 8], F32))
        BKs = es.enter_context(nc.sbuf_tensor("BKs", [128, 8], F32))
        BVs = es.enter_context(nc.sbuf_tensor("BVs", [128, 8], F32))
        PROBS = es.enter_context(nc.sbuf_tensor("PROBS", [128, NT, 8], F32))
        GATES = es.enter_context(nc.sbuf_tensor("GATES", [128, NT, 8], F32))
        SCRF = es.enter_context(nc.sbuf_tensor("SCRF", [128, 2048], F32))
        SM = es.enter_context(nc.sbuf_tensor("SM", [128, 16], F32))
        RS = es.enter_context(nc.sbuf_tensor("RS", [1, 16], F32))
        BT = es.enter_context(nc.sbuf_tensor("BT", [128, D], F32))
        BIS = es.enter_context(nc.sbuf_tensor("BIS", [128, 12], F32))
        B128 = es.enter_context(nc.sbuf_tensor("B128", [128, 1024], F32))

        PSA = es.enter_context(nc.psum_tensor("PSA", [128, 2048], F32))
        PSB = es.enter_context(nc.psum_tensor("PSB", [128, 2048], F32))
        PSS = nc.virtual_psum_tensor("PSS", [128, 4096], BF16)
        nc.back_virtual_tensor_with(PSS, PSA)

        sems = {}
        for s in ("qS", "qA", "qP", "pe", "ve", "ac", "gp", "cc"):
            sems[s] = es.enter_context(nc.semaphore(s))

        print("sbuf bytes remaining after persistent allocs:",
              nc.sbuf_bytes_remaining)

        def dma(qeng, sem, out, in_, waits=(), **kw):
            return E.op(qeng,
                        lambda eng, o=out, i=in_, kw=dict(kw):
                        eng.dma_start(out=o, in_=i, **kw),
                        waits=waits, inc=sem, inc_k=16)

        def mm(out, lhsT, rhs, start, stop, waits=(), inc=None):
            return E.op("tensor",
                        lambda eng, o=out, l=lhsT, r=rhs, s1=start, s2=stop:
                        eng.matmul(o, l, r, start=s1, stop=s2),
                        waits=waits, inc=inc)

        def tr(out, in_, waits=(), inc=None, np_=128):
            return E.op("tensor",
                        lambda eng, o=out, i=in_, n=np_:
                        eng.transpose(o, i, IDENT[0:n, 0:n]),
                        waits=waits, inc=inc)

        WGF = es.enter_context(nc.sbuf_tensor("WGF", [128, 8, 8], F32))
        vX = dma("sync", "qS", X[:, :, :],
                 x_in.rearrange("(j p) d -> p j d", p=128))
        dma("sync", "qS", WGF[:, :, :],
            wg_in.rearrange("(dc p) e -> p dc e", p=128))
        dma("sync", "qS", IDENT[:, :], identf)
        dma("sync", "qS", ONESR[:, :], onesr)
        dma("sync", "qS", ONESCB[:, :], onescb)
        dma("sync", "qS", FOLDW[:, :], foldw)
        vB = None
        for name, src in (("bq", bq_in), ("bk", bk_in), ("bv", bv_in)):
            vB = dma("sync", "qS", SCRF[0:8, 128 * {"bq": 0, "bk": 1, "bv": 2}[name]:
                                        128 * ({"bq": 0, "bk": 1, "bv": 2}[name] + 1)],
                     src.rearrange("(a p) -> a p", p=128))
        vP0 = vB
        vtrb = {}
        for i, dst in enumerate((BQ8, BKs, BVs)):
            vtrb[i] = tr(PSB[0:128, i * 8:(i + 1) * 8],
                         SCRF[0:8, 128 * i:128 * (i + 1)],
                         waits=[("qS", vP0)] if i == 0 else (),
                         inc="pe" if i == 2 else None, np_=8)
        v_bev = E.op("vector", lambda eng: eng.tensor_copy(
            BKs[:, :], PSB[:, 8:16]), waits=[("pe", vtrb[2])], inc="ve")
        v_bev = E.op("vector", lambda eng: eng.tensor_copy(
            BVs[:, :], PSB[:, 16:24]), inc="ve")
        v_bev = E.op("vector", lambda eng: eng.tensor_scalar(
            BQ8[:, :], PSB[:, 0:8], 0.125, None, op0=ALU.mult), inc="ve")
        vP0ve = v_bev

        def emit_layernorm_to_XF(first_waits, router=False):
            v_rmm_prev = [0]
            v_sr_prev = 0
            prev_tr = {0: 0, 1: 0}
            prev_evac = {0: 0, 1: 0}
            prev_e32 = {0: 0, 1: 0}
            last_tr = 0
            v_ev = 0
            for j in range(NT):
                half = j % 2
                w = list(first_waits) if j == 0 else []
                if j > 0:
                    w = w + [("ac", v_sr_prev)]
                if paranoid:
                    w = w + E.all_marks()
                v_ve = E.op("vector", lambda eng, j=j: eng.reduce_sum(
                    SM[:, 0:1], X[:, j, :], axis=AX.X), waits=w, inc="ve")
                E.op("vector", lambda eng: eng.tensor_scalar(
                    SM[:, 1:2], SM[:, 0:1], -1.0 / D, None, op0=ALU.mult))
                v_ve = E.op("vector", lambda eng: eng.tensor_scalar(
                    SM[:, 0:1], SM[:, 0:1], 1.0 / D, None, op0=ALU.mult),
                    inc="ve")
                v_sq = E.op("scalar", lambda eng, j=j: eng.activation(
                    SCRF[:, 0:1024], X[:, j, :], AF.Square, bias=SM[:, 1:2],
                    scale=1.0),
                    waits=[("ve", v_ve)] +
                          ([("pe", v_rmm_prev[0])] if router else []),
                    inc="ac")
                E.op("vector", lambda eng: eng.reduce_sum(
                    SM[:, 2:3], SCRF[:, 0:1024], axis=AX.X),
                    waits=[("ac", v_sq)])
                v_ssq = E.op("vector", lambda eng: eng.tensor_scalar(
                    SM[:, 2:3], SM[:, 2:3], 1.0 / D, 1e-5,
                    op0=ALU.mult, op1=ALU.add), inc="ve")
                v_sr = E.op("scalar", lambda eng: eng.activation(
                    SM[:, 3:4], SM[:, 2:3], AF.Sqrt),
                    waits=[("ve", v_ssq)], inc="ac")
                v_sr_prev = v_sr
                E.op("vector", lambda eng: eng.reciprocal(
                    SM[:, 4:5], SM[:, 3:4]), waits=[("ac", v_sr)])
                E.op("vector", lambda eng: eng.tensor_scalar(
                    SM[:, 5:6], SM[:, 1:2], SM[:, 4:5], None, op0=ALU.mult))
                if lnv == 1:
                    v_nrm = E.op("vector", lambda eng, j=j: eng.tensor_copy(
                        SCRF[:, 1024:2048], X[:, j, :]),
                        waits=[("pe", last_tr), ("ac", v_sr)], inc="ve")
                elif lnv == 2:
                    v_nrm = E.op("vector", lambda eng, j=j: eng.tensor_scalar(
                        SCRF[:, 1024:2048], X[:, j, :], SM[:, 4:5], None,
                        op0=ALU.mult),
                        waits=[("pe", last_tr)], inc="ve")
                else:
                    v_nrm = E.op("vector", lambda eng, j=j: eng.tensor_scalar(
                        SCRF[:, 1024:2048], X[:, j, :], SM[:, 4:5], SM[:, 5:6],
                        op0=ALU.mult, op1=ALU.add),
                        waits=[("pe", last_tr)], inc="ve")
                ps = PSA[:, half * 1024:(half + 1) * 1024]
                vtr = None
                for dc in range(NDC):
                    vtr = tr(ps[:, dc * 128:(dc + 1) * 128],
                             SCRF[:, 1024 + dc * 128:1024 + (dc + 1) * 128],
                             waits=([("ve", max(v_nrm, prev_e32[half])),
                                     ("ac", prev_evac[half])]
                                    if dc == 0 else ()),
                             inc="pe" if dc == NDC - 1 else None)
                prev_tr[half] = vtr
                last_tr = vtr
                v_ev = E.op("scalar", lambda eng, j=j, ps=ps: eng.activation(
                    XF[:, :, j * 128:(j + 1) * 128],
                    ps.rearrange("p (dc t) -> p dc t", dc=8),
                    AF.Identity), waits=[("pe", vtr)], inc="ac")
                if router:
                    v_e32 = E.op("vector", lambda eng, ps=ps: eng.tensor_copy(
                        SCRF[:, 0:1024], ps), waits=[("ac", v_ev)])
                    prev_e32[half] = v_e32
                    v_rmm = None
                    for dc in range([NDC, 1, 0][no_rmm]):
                        wr = [("ve", v_e32)] if dc == 0 else []
                        if j == 0 and dc == 0:
                            wr.append(("qS", vX + 16))
                        v_rmm = mm(PSB[:, j * 8:(j + 1) * 8],
                                   SCRF[:, dc * 128:(dc + 1) * 128],
                                   WGF[:, dc, :], start=(dc == 0),
                                   stop=bool((dc == 7) or no_rmm),
                                   waits=wr,
                                   inc="pe" if (dc == 7 or no_rmm) else None)
                    if v_rmm is not None:
                        v_rmm_prev[0] = v_rmm
                prev_evac[half] = v_ev
            return v_ev

        vXF = emit_layernorm_to_XF([("qS", vX), ("ve", vP0ve)],
                                   router=(upto >= 2))

        if upto >= 2 and not no_p2:
            v_exp = 0
            v_prt = 0
            vmmj = E.cur("pe")
            for j in range(NT):
                pv = PSB[:, j * 8:(j + 1) * 8]
                v_mx = E.op("vector", lambda eng, pv=pv: eng.reduce_max(
                    SM[:, 0:1], pv, axis=AX.X),
                    waits=[("pe", vmmj), ("ac", v_exp)])
                v_mx = E.op("vector", lambda eng: eng.tensor_scalar(
                    SM[:, 1:2], SM[:, 0:1], -1.0, None, op0=ALU.mult),
                    inc="ve")
                v_exp = E.op("scalar", lambda eng, j=j, pv=pv: eng.activation(
                    PROBS[:, j, :], pv, AF.Exp, bias=SM[:, 1:2]),
                    waits=[("ve", v_mx)], inc="ac")
                E.op("vector", lambda eng, j=j: eng.reduce_sum(
                    SM[:, 2:3], PROBS[:, j, :], axis=AX.X),
                    waits=[("ac", v_exp)])
                E.op("vector", lambda eng: eng.reciprocal(
                    SM[:, 3:4], SM[:, 2:3]))
                v_pn = E.op("vector", lambda eng, j=j: eng.tensor_scalar(
                    PROBS[:, j, :], PROBS[:, j, :], SM[:, 3:4], None,
                    op0=ALU.mult), inc="ve")
                v_prt = tr(PSA[0:8, j * 128:(j + 1) * 128], PROBS[:, j, :],
                           waits=[("ve", v_pn)] + ([("ac", vXF)] if j == 0
                                                   else []),
                           inc="pe")
        if upto >= 3:
            v_pev = E.op("vector", lambda eng: eng.tensor_copy(
                SCRF[0:8, :], PSA[0:8, :]), waits=[("pe", v_prt)], inc="ve")
            cc_in, cc_out = get_cc()
            v_cin = dma("sync", "qS", cc_in[:, :], SCRF[0:8, :],
                        waits=[("ve", v_pev)])
            v_cc = E.op("gpsimd", lambda eng: eng.collective_compute(
                "AllGather", ALU.bypass, replica_groups=rg,
                ins=[cc_in.ap().opt()], outs=[cc_out.ap().opt()]),
                waits=[("qS", v_cin)], inc="cc")
            for h in range(2):
                v_b128 = dma("sync", "qS", B128[h * 64:(h + 1) * 64, :],
                             cc_out.ap()[0:64, h * 1024:(h + 1) * 1024],
                             waits=[("cc", v_cc)] if h == 0 else ())
        if upto >= 4:
            E.op("vector", lambda eng: eng.memset(BIS[:, 0:1], 0.0),
                 waits=[("qS", v_b128)])
            E.op("vector", lambda eng: eng.memset(BIS[:, 1:2], 1.0))
            v_pe_prev = E.cur("pe")
            v_ac_guard = v_exp
            for it in range(BIS_ITERS):
                E.op("vector", lambda eng: eng.scalar_tensor_tensor(
                    BIS[:, 2:3], BIS[:, 0:1], 1.0, BIS[:, 1:2],
                    op0=ALU.mult, op1=ALU.add))
                E.op("vector", lambda eng: eng.tensor_scalar(
                    BIS[:, 2:3], BIS[:, 2:3], 0.5, None, op0=ALU.mult))
                E.op("vector", lambda eng: eng.tensor_scalar(
                    SCRF[:, 0:1024], B128[:, :], BIS[:, 2:3], None,
                    op0=ALU.is_ge))
                v_fold = E.op("vector", lambda eng: eng.reduce_sum(
                    BIS[:, 3:4], SCRF[:, 0:1024], axis=AX.X), inc="ve")
                v_cm = mm(PSB[:, 0:1], FOLDW[:, :], BIS[:, 3:4],
                          start=True, stop=True,
                          waits=[("ve", v_fold)] +
                                ([("ac", v_ac_guard)] if it == 0 else []),
                          inc="pe")
                E.op("vector", lambda eng: eng.tensor_copy(
                    BIS[:, 6:7], PSB[:, 0:1]), waits=[("pe", v_cm)])
                E.op("vector", lambda eng: eng.tensor_scalar(
                    BIS[:, 7:8], BIS[:, 6:7], 2048.0, None, op0=ALU.is_ge))
                E.op("vector", lambda eng: eng.tensor_sub(
                    BIS[:, 8:9], BIS[:, 2:3], BIS[:, 0:1]))
                E.op("vector", lambda eng: eng.tensor_mul(
                    BIS[:, 8:9], BIS[:, 8:9], BIS[:, 7:8]))
                E.op("vector", lambda eng: eng.tensor_add(
                    BIS[:, 0:1], BIS[:, 0:1], BIS[:, 8:9]))
                E.op("vector", lambda eng: eng.tensor_sub(
                    BIS[:, 9:10], BIS[:, 1:2], BIS[:, 2:3]))
                E.op("vector", lambda eng: eng.tensor_mul(
                    BIS[:, 9:10], BIS[:, 9:10], BIS[:, 7:8]))
                v_upd = E.op("vector", lambda eng: eng.tensor_add(
                    BIS[:, 1:2], BIS[:, 2:3], BIS[:, 9:10]), inc="ve")
            v_throw = tr(PSB[0:1, 8:16], BIS[0:8, 0:1],
                         waits=[("ve", v_upd)], inc="pe", np_=8)
            v_rs = E.op("vector", lambda eng: eng.tensor_copy(
                RS[0:1, 0:8], PSB[0:1, 8:16]), waits=[("pe", v_throw)],
                inc="ve")
            v_th = mm(PSB[:, 16:24], ONESR[:, :], RS[0:1, 0:8],
                      start=True, stop=True, waits=[("ve", v_rs)], inc="pe")
            E.op("vector", lambda eng: eng.tensor_copy(
                SM[:, 8:16], PSB[:, 16:24]), waits=[("pe", v_th)])
            for j in range(NT):
                gj = GATES[:, j, :]
                E.op("vector", lambda eng, gj=gj, j=j: eng.tensor_tensor(
                    gj, PROBS[:, j, :], SM[:, 8:16], op=ALU.is_ge))
                E.op("vector", lambda eng, gj=gj, j=j:
                     eng.scalar_tensor_tensor(
                         gj, gj, 2.0, PROBS[:, j, :],
                         op0=ALU.mult, op1=ALU.add))
                E.op("vector", lambda eng, gj=gj: eng.reduce_max(
                    SM[:, 0:1], gj, axis=AX.X))
                E.op("vector", lambda eng: eng.tensor_scalar(
                    SM[:, 1:2], SM[:, 0:1], 2.0, None, op0=ALU.is_ge))
                E.op("vector", lambda eng: eng.scalar_tensor_tensor(
                    SM[:, 2:3], SM[:, 1:2], -2.0, SM[:, 0:1],
                    op0=ALU.mult, op1=ALU.add))
                E.op("vector", lambda eng, gj=gj: eng.tensor_scalar(
                    gj, gj, SM[:, 0:1], None, op0=ALU.is_equal))
                E.op("vector", lambda eng, gj=gj, j=j: eng.tensor_scalar(
                    gj, gj, SM[:, 2:3], None, op0=ALU.mult),
                    inc="ve" if j == NT - 1 else None)

        def phase_barrier():
            marks = [(s, E.cur(s)) for s in ("qS", "qA", "qP", "pe", "ve",
                                             "ac", "gp", "cc")]
            for e in ENGINES:
                E.wait(e, marks)

        def emit_bias_tile(BT_, src_ap):
            vld = dma("sync", "qS", BT_[0:1, :], src_ap)
            vm = None
            for nn in range(2):
                vm = mm(PSB[:, nn * 512:(nn + 1) * 512], ONESR[:, :],
                        BT_[0:1, nn * 512:(nn + 1) * 512], start=True,
                        stop=True,
                        waits=([("qS", vld), ("ve", E.cur("ve")),
                                ("ac", E.cur("ac"))] if nn == 0 else []),
                        inc="pe" if nn == 1 else None)
            E.op("vector", lambda eng: eng.tensor_copy(
                BT_[:, :], PSB[:, 0:1024]), waits=[("pe", vm)])
            for j in range(NT):
                E.op("vector", lambda eng, j=j: eng.tensor_add(
                    X[:, j, :], X[:, j, :], BT_[:, :]))

        def emit_attention(att, cfg):
            QTP, KTP, VAL, PT, OF = (att["QTP"], att["KTP"], att["VAL"],
                                     att["PT"], att["OF"])
            WQTP, WKTP, WVTP = att["WQTP"], att["WKTP"], att["WVTP"]
            WOB, IDENTB, ONESRB = att["WOB"], att["IDENTB"], att["ONESRB"]
            E.op("vector", lambda eng: eng.memset(
                VAL[:, :, :].rearrange("p sc (h c) -> p sc h c", c=65)
                [:, :, :, 64:65], 1.0))
            wld = [None, None]
            wld[0] = cfg["load_pair"](0, 0)
            wld[1] = cfg["load_pair"](1, 1)
            v_exp = {}
            v_pv = {}
            v_ptread = {(0, 0): 0, (0, 1): 0,
                        (1, 0): 0, (1, 1): 0}
            pt0_qp = [0]
            bc_qp = [0]
            om_ve = [0]
            for p in range(8):
                par = p % 2
                if p >= 2:
                    pass
                vq = None
                for s4 in range(4):
                    for dc in range(NDC):
                        w = []
                        if s4 == 0 and dc == 0:
                            w = [wld[par], ("ac", E.cur("ac")),
                                 ("ve", E.cur("ve"))]
                        vq = mm(PSA[:, s4 * 512:(s4 + 1) * 512],
                                WQTP[par][:, dc, :],
                                XF[:, dc, s4 * 512:(s4 + 1) * 512],
                                start=(dc == 0), stop=(dc == 7), waits=w,
                                inc="pe" if (s4 == 3 and dc == 7) else None)
                v_qe = E.op("scalar", lambda eng, p=p: eng.activation(
                    QTP[:, :], PSA[:, :], AF.Identity,
                    bias=cfg["qbias"](p), scale=cfg["qscale"](p)),
                    waits=[("pe", vq)])
                vk = None
                for s4 in range(4):
                    for dc in range(NDC):
                        w = []
                        vk = mm(PSB[:, s4 * 512:(s4 + 1) * 512],
                                WKTP[par][:, dc, :],
                                XF[:, dc, s4 * 512:(s4 + 1) * 512],
                                start=(dc == 0), stop=(dc == 7), waits=w,
                                inc="pe" if (s4 == 3 and dc == 7) else None)
                v_ke = E.op("scalar", lambda eng, p=p: eng.activation(
                    KTP[:, :], PSB[:, :], AF.Identity,
                    bias=cfg["kbias"](p), scale=1.0),
                    waits=[("pe", vk), ("ac", v_qe)])
                if 2 <= p + 1 < 8:
                    wld[1 - par] = cfg["load_pair"](p + 1, 1 - par)
                vbias = cfg["vbias_row"](p)
                vv = None
                for sc in range(NT):
                    w0 = [("ac", v_ke)] if sc == 0 else []
                    for dc in range(NDC):
                        last = ((dc == 7) and (vbias is None) and
                                (sc == NT - 1))
                        vv = mm(PSA[:, sc * 128:(sc + 1) * 128],
                                XF[:, dc, sc * 128:(sc + 1) * 128],
                                WVTP[par][:, dc, :],
                                start=(dc == 0),
                                stop=(dc == 7) and (vbias is None),
                                waits=w0 if dc == 0 else [],
                                inc="pe" if last else None)
                    if vbias is not None:
                        vv = mm(PSA[:, sc * 128:(sc + 1) * 128],
                                ONESRB[0:1, 0:128], vbias,
                                start=False, stop=True,
                                waits=([cfg["vbias_ready"]]
                                       if (p == 0 and sc == 0) else []),
                                inc="pe" if sc == NT - 1 else None)
                v_vev = E.op("vector", lambda eng: eng.tensor_copy(
                    VAL[:, :, :].rearrange("p sc (h c) -> p sc h c", c=65)
                    [:, :, :, 0:64],
                    PSA[:, :].rearrange("p (sc h c) -> p sc h c",
                                        sc=16, c=64)),
                    waits=[("pe", vv)])
                SCRB = PT[0]
                v_sqq = E.op("scalar", lambda eng: eng.activation(
                    SCRB[:, :], QTP[:, :], AF.Square),
                    waits=[("pe", max(v_ptread[(0, 0)],
                                      v_ptread[(1, 0)])), ("ve", v_vev),
                           ("qP", pt0_qp[0])])
                vnq = None
                for s4 in range(4):
                    vnq = mm(PSA[0:1, s4 * 512:(s4 + 1) * 512], ONESCB[:, :],
                             SCRB[:, s4 * 512:(s4 + 1) * 512], start=True,
                             stop=True,
                             waits=[("ac", v_sqq), ("ve", v_vev)]
                             if s4 == 0 else [],
                             inc="pe" if s4 == 3 else None)
                v_qmax = E.op("vector", lambda eng: eng.reduce_max(
                    RS[0:1, 0:1], PSA[0:1, 0:2048], axis=AX.X),
                    waits=[("pe", vnq)])
                v_sqk = E.op("scalar", lambda eng: eng.activation(
                    SCRB[:, :], KTP[:, :], AF.Square), waits=[("pe", vnq)])
                vnk = None
                for s4 in range(4):
                    vnk = mm(PSA[0:1, s4 * 512:(s4 + 1) * 512], ONESCB[:, :],
                             SCRB[:, s4 * 512:(s4 + 1) * 512], start=True,
                             stop=True,
                             waits=[("ac", v_sqk), ("ve", v_qmax)]
                             if s4 == 0 else [],
                             inc="pe" if s4 == 3 else None)
                v_kmax = E.op("vector", lambda eng: eng.reduce_max(
                    RS[0:1, 1:2], PSA[0:1, 0:2048], axis=AX.X),
                    waits=[("pe", vnk)])
                E.op("vector", lambda eng: eng.tensor_scalar(
                    RS[0:1, 2:3], RS[0:1, 0:1], RS[0:1, 1:2], -1.0 / 16,
                    op0=ALU.mult, op1=ALU.mult))
                v_rb = E.op("vector", lambda eng: eng.tensor_scalar(
                    RS[0:1, 2:3], RS[0:1, 2:3], -4.0, None, op0=ALU.add))
                v_bm = mm(PSA[:, 0:1], ONESR[:, :], RS[0:1, 2:3], start=True,
                          stop=True, waits=[("ve", v_rb)], inc="pe")
                v_bias = E.op("vector", lambda eng: eng.tensor_copy(
                    SM[:, 6:7], PSA[:, 0:1]), waits=[("pe", v_bm)])
                v_ptread[(0, 0)] = max(v_ptread[(0, 0)], vnk)
                v_ptread[(1, 0)] = max(v_ptread[(1, 0)], vnk)
                v_lastdvepsa = v_bias
                for qh in range(2):
                    q0 = qh * 1024

                    def emit_pv(sc, qh=qh):
                        par = sc % 2
                        for h01 in range(2):
                            pt = PT[h01]
                            for nn in range(2):
                                w = []
                                if nn == 0:
                                    w = [("ac", v_exp[(sc, h01)])]
                                    if sc == 0 and h01 == 0:
                                        w.append(("ve", E.cur("ve")))
                                v_pv[(sc, h01)] = mm(
                                    PSB[0:65, h01 * 1024 + nn * 512:
                                        h01 * 1024 + (nn + 1) * 512],
                                    VAL[:, sc, h01 * 65:h01 * 65 + 65],
                                    pt[:, par * 1024 + nn * 512:
                                       par * 1024 + (nn + 1) * 512],
                                    start=(sc == 0), stop=(sc == NT - 1),
                                    waits=w,
                                    inc="pe" if nn == 1 else None)
                            v_ptread[(par, h01)] = v_pv[(sc, h01)]

                    for sc in range(NT):
                        par = sc % 2
                        vsd = {}
                        for h01 in range(2):
                            base = h01 * 64
                            for nn in range(2):
                                w = []
                                if nn == 0:
                                    w = [("ac",
                                          v_exp.get((sc - 1, h01), 0))]
                                    if h01 == 0:
                                        w.append(("ve", v_lastdvepsa))
                                vsd[h01] = mm(
                                    PSA[:, h01 * 1024 + nn * 512:
                                        h01 * 1024 + (nn + 1) * 512],
                                    KTP[base:base + 64,
                                        sc * 128:(sc + 1) * 128],
                                    QTP[base:base + 64,
                                        q0 + nn * 512:
                                        q0 + (nn + 1) * 512],
                                    start=True, stop=True, waits=w,
                                    inc="pe" if nn == 1 else None)
                        for h01 in range(2):
                            pt = PT[h01]
                            wpt = [("pe", vsd[h01]),
                                   ("pe", v_ptread[(par, h01)])]
                            if sc == 0 and h01 == 0:
                                wpt.append(("qP", pt0_qp[0]))
                            v_exp[(sc, h01)] = E.op(
                                "scalar",
                                lambda eng, pt=pt, h01=h01, par=par:
                                eng.activation(
                                    pt[:, par * 1024:(par + 1) * 1024],
                                    PSA[:, h01 * 1024:(h01 + 1) * 1024],
                                    AF.Exp, bias=SM[:, 6:7]),
                                waits=wpt)
                        if sc >= 1:
                            emit_pv(sc - 1)
                    emit_pv(NT - 1)
                    c_chunk = p
                    v_rc = E.op("vector", lambda eng: eng.reciprocal(
                        SCRF[0:1, 0:2048], PSB[64:65, 0:2048]),
                        waits=[("pe", v_pv[(NT - 1, 1)])])
                    v_d1 = dma("gpsimd", "qP", rb_scr, SCRF[0:1, 0:2048],
                               waits=[("ve", v_rc), ("qP", bc_qp[0])])
                    rbb = copy.copy(rb_scr)
                    rbb.ap = bass_rust.VecI64Pair([[0, 64], [1, 2048]])
                    v_d2 = dma("gpsimd", "qP", SCRF[0:64, 0:2048], rbb,
                               waits=[("qP", v_d1), ("ve", om_ve[0])])
                    bc_qp[0] = v_d2
                    E.op("vector", lambda eng, c=c_chunk, q0=q0:
                         eng.tensor_tensor(
                             OF[0:64, c, q0:q0 + 1024],
                             PSB[0:64, 0:1024],
                             SCRF[0:64, 0:1024], op=ALU.mult),
                         waits=[("qP", v_d2)])
                    v_om = E.op("vector", lambda eng, q0=q0:
                                eng.tensor_tensor(
                                    PT[0][0:64, q0:q0 + 1024],
                                    PSB[0:64, 1024:2048],
                                    SCRF[0:64, 1024:2048], op=ALU.mult),
                                waits=[("qP", v_d2)])
                    om_ve[0] = v_om
                    v_mvq = dma("gpsimd", "qP",
                                OF[64:128, c_chunk, q0:q0 + 1024],
                                PT[0][0:64, q0:q0 + 1024],
                                waits=[("ve", v_om)])
                    pt0_qp[0] = v_mvq
                    v_exp = {(-1, 0): v_exp[(NT - 1, 0)],
                             (-1, 1): v_exp[(NT - 1, 1)]}
                    v_pv = {}
            v_xadd = {}
            for j in range(NT):
                half = j % 2
                vo = None
                for c in range(8):
                    for nn in range(2):
                        col = half * 1024 + nn * 512
                        w = []
                        if c == 0 and nn == 0:
                            w = [cfg["wob_ready"], cfg["wob_ready2"],
                                 ("qP", pt0_qp[0]),
                                 ("ve", v_xadd.get(j - 2, E.cur("ve")))]
                        vo = mm(PSB[:, col:col + 512],
                                OF[:, c, j * 128:(j + 1) * 128],
                                WOB[:, c, nn * 512:(nn + 1) * 512],
                                start=(c == 0), stop=(c == 7), waits=w,
                                inc="pe" if (c == 7 and nn == 1) else None)
                v_xadd[j] = E.op("vector", lambda eng, j=j, half=half:
                                 eng.scalar_tensor_tensor(
                                     X[:, j, :],
                                     PSB[:, half * 1024:(half + 1) * 1024],
                                     1.0, X[:, j, :], op0=ALU.mult,
                                     op1=ALU.add),
                                 waits=[("pe", vo)])

        if upto >= 5:
            phase_barrier()
            with contextlib.ExitStack() as aes:
                att = {
                    "QTP": aes.enter_context(
                        nc.sbuf_tensor("QTP", [128, 2048], BF16)),
                    "KTP": aes.enter_context(
                        nc.sbuf_tensor("KTP", [128, 2048], BF16)),
                    "VAL": aes.enter_context(
                        nc.sbuf_tensor("VAL", [128, 16, 130], BF16)),
                    "PT": [aes.enter_context(
                        nc.sbuf_tensor(f"PT{i}", [128, 2048], BF16))
                        for i in range(2)],
                    "OF": aes.enter_context(
                        nc.sbuf_tensor("OF", [128, 8, 2048], BF16)),
                    "WQTP": [aes.enter_context(
                        nc.sbuf_tensor(f"WQTP{i}", [128, 8, 128], BF16))
                        for i in range(2)],
                    "WKTP": [aes.enter_context(
                        nc.sbuf_tensor(f"WKTP{i}", [128, 8, 128], BF16))
                        for i in range(2)],
                    "WVTP": [aes.enter_context(
                        nc.sbuf_tensor(f"WVTP{i}", [128, 8, 128], BF16))
                        for i in range(2)],
                    "WOB": aes.enter_context(
                        nc.sbuf_tensor("WOB", [128, 8, 1024], BF16)),
                    "IDENTB": aes.enter_context(
                        nc.sbuf_tensor("IDENTB", [128, 128], BF16)),
                    "ONESRB": aes.enter_context(
                        nc.sbuf_tensor("ONESRB", [1, 128], BF16)),
                    "BVR": aes.enter_context(
                        nc.sbuf_tensor("BVR", [1, 1024], BF16)),
                    "BTL": aes.enter_context(
                        nc.sbuf_tensor("BTL", [128, 1024], F32)),
                    "STG": aes.enter_context(
                        nc.sbuf_tensor("STG", [128, 2, 1024], BF16)),
                    "CONSTV": aes.enter_context(
                        nc.sbuf_tensor("CONSTV", [128, 2], F32)),
                }
                print("sbuf remaining in L0 attn scope:",
                      nc.sbuf_bytes_remaining)
                dma("sync", "qS", att["IDENTB"][:, :], identb_in)
                dma("sync", "qS", att["ONESRB"][:, :], onesrb_in)
                vbvr = dma("gpsimd", "qP", att["BVR"][:, :],
                           bv_in.rearrange("(a d) -> a d", a=1))
                E.op("vector", lambda eng: eng.memset(
                    att["CONSTV"][:, 0:1], 0.125))
                stg = att["STG"]
                v_store = 0
                for wi, (src, dst) in enumerate(
                        ((wq_n, wq_s), (wk_n, wk_s), (wv_n, wv_s))):
                    for ch in range(8):
                        par = ch % 2
                        vl = dma("gpsimd", "qP", stg[:, par, :],
                                 src[ch * 128:(ch + 1) * 128, :],
                                 waits=[("qS", v_store)] if (wi + ch) else [])
                        v_store = dma("sync", "qS",
                                      dst[ch * 128:(ch + 1) * 128, :],
                                      stg[:, par, :], waits=[("qP", vl)])
                for wsrc, wdst in ((wq_s, wqT_s), (wk_s, wkT_s),
                                   (wv_s, wvT_s)):
                    for dc in range(NDC):
                        par = dc % 2
                        vl = dma("sync", "qS", stg[:, par, :],
                                 wsrc[:, dc * 128:(dc + 1) * 128],
                                 transpose=True, waits=[("qS", v_store)])
                        v_store = dma("sync", "qS",
                                      wdst[dc * 128:(dc + 1) * 128, :],
                                      stg[:, par, :], waits=[("qS", vl)])
                v_cast_done = v_store
                scrw = SCRF[:, 0:1024].rearrange("p (g k) -> p g k", k=64)
                v_wev = 0
                v_wmv = 0
                for g in range(8):
                    vl = dma("sync", "qS", scrw,
                             wo_n.rearrange("(g p) k -> p g k", p=128)
                             [:, g * 16:(g + 1) * 16, :],
                             waits=[("ve", v_wev)] if g else
                                   [("ve", E.cur("ve")), ("ac", E.cur("ac"))])
                    for hh in range(2):
                        vt = None
                        for ss in range(8):
                            w = [("qS", vl)] if (hh == 0 and ss == 0) else []
                            if hh == 1:
                                w = [("qP", v_wmv)] if ss == 0 else []
                            vt = E.op("tensor",
                                      lambda eng, hh=hh, ss=ss:
                                      eng.matmul(
                                          PSA[0:64,
                                              hh * 1024 + ss * 128:
                                              hh * 1024 + (ss + 1) * 128],
                                          scrw[:, hh * 8 + ss, :],
                                          IDENT[0:128, 0:128],
                                          start=True, stop=True,
                                          is_transpose=True),
                                      waits=w,
                                      inc="pe" if ss == 7 else None)
                        if hh == 0:
                            v_wev = E.op("vector", lambda eng, g=g:
                                         eng.tensor_copy(
                                             att["WOB"][0:64, g, :],
                                             PSA[0:64, 0:1024]),
                                         waits=[("pe", vt)])
                        else:
                            v_wev = E.op("vector", lambda eng:
                                         eng.tensor_copy(
                                             stg[0:64, 0, :],
                                             PSA[0:64, 1024:2048]),
                                         waits=[("pe", vt)])
                            v_wmv = dma("gpsimd", "qP",
                                        att["WOB"][64:128, g, :],
                                        stg[0:64, 0, :],
                                        waits=[("ve", v_wev)])
                v_wob_ve = v_wev
                v_wob_qp = v_wmv

                def load_pair0(p, par):
                    v = 0
                    first = True
                    for wsrc, wdst in ((wqT_s, att["WQTP"][par]),
                                       (wkT_s, att["WKTP"][par]),
                                       (wvT_s, att["WVTP"][par])):
                        w = []
                        if first:
                            w = [("qS", v_cast_done), ("pe", E.cur("pe"))]
                            first = False
                        v = dma("sync", "qS", wdst[:, :, :],
                                wsrc.rearrange("(dc p) o -> p dc o", p=128)
                                [:, :, p * 128:(p + 1) * 128], waits=w)
                    return ("qS", v)

                emit_bias_tile(att["BTL"], bo_in)
                cfg0 = {
                    "load_pair": load_pair0,
                    "qscale": lambda p: att["CONSTV"][:, 0:1],
                    "qbias": lambda p: BQ8[:, p:p + 1],
                    "kbias": lambda p: BKs[:, p:p + 1],
                    "vbias_row": lambda p, vb=vbvr:
                        att["BVR"][0:1, p * 128:(p + 1) * 128],
                    "wob_ready": ("ve", v_wob_ve),
                    "wob_ready2": ("qP", v_wob_qp),
                    "vbias_ready": ("qP", vbvr),
                }
                emit_attention(att, cfg0)
            phase_barrier()

        if upto >= 6:
            phase_barrier()
            with contextlib.ExitStack() as mes:
                HT = mes.enter_context(
                    nc.sbuf_tensor("HT", [128, 16, 1024], BF16))
                W2B = mes.enter_context(
                    nc.sbuf_tensor("W2B", [128, 16, 1024], BF16))
                W1B = [mes.enter_context(
                    nc.sbuf_tensor(f"W1B{i}", [128, 8, 4, 128], BF16))
                    for i in range(2)]
                print("sbuf remaining in MoE scope:", nc.sbuf_bytes_remaining)
                v_w1ld = {}
                v_gelu = {}
                v_xupd = {}
                v_w2mm_last = 0
                v_humm = {}
                for e in range(8):
                    v_w2 = dma("gpsimd", "qP", W2B[:, :, :],
                               w2_in[e * 2048:(e + 1) * 2048, :].rearrange(
                                   "(fc p) d -> p fc d", p=128),
                               waits=[("pe", v_w2mm_last)])
                    for th in range(2):
                        for fc in range(16):
                            par = (fc // 4) % 2
                            if fc % 4 == 0:
                                v_w1ld[par] = dma(
                                    "gpsimd", "qP", W1B[par][:, :, :, :],
                                    w1_in[e * 1024:(e + 1) * 1024,
                                          fc * 128:(fc + 4) * 128].rearrange(
                                        "(dc p) (g f) -> p dc g f",
                                        p=128, f=128),
                                    waits=[("pe",
                                            v_humm.get((e, th, fc - 5),
                                                       E.cur("pe")))])
                            half = fc % 2
                            ps = PSA[:, half * 1024:(half + 1) * 1024]
                            vh = None
                            for dc in range(NDC):
                                for nn in range(2):
                                    w = []
                                    if dc == 0 and nn == 0:
                                        w = [("qP", v_w1ld[par]),
                                             ("ac", v_gelu.get(fc - 2, 0))]
                                        if e == 0 and th == 0 and fc == 0:
                                            w.append(("ve", E.cur("ve")))
                                            w.append(("ac", E.cur("ac")))
                                    vh = mm(ps[:, nn * 512:(nn + 1) * 512],
                                            W1B[par][:, dc, fc % 4, :],
                                            XF[:, dc, th * 1024 + nn * 512:
                                               th * 1024 + (nn + 1) * 512],
                                            start=(dc == 0), stop=(dc == 7),
                                            waits=w,
                                            inc="pe" if (dc == 7 and nn == 1)
                                            else None)
                            v_humm[(e, th, fc)] = vh
                            v_gelu[fc] = E.op(
                                "scalar", lambda eng, fc=fc, ps=ps:
                                eng.activation(HT[:, fc, :], ps, AF.Gelu),
                                waits=[("pe", vh)])
                        v_last_gelu = v_gelu[15]
                        for tt in range(8):
                            j = th * 8 + tt
                            half = tt % 2
                            ps = PSB[:, half * 1024:(half + 1) * 1024]
                            vw = None
                            for fc in range(16):
                                for nn in range(2):
                                    w = []
                                    if fc == 0 and nn == 0:
                                        w = [("ac", v_last_gelu),
                                             ("qP", v_w2),
                                             ("ve", v_xupd.get(tt - 2,
                                                               E.cur("ve")))]
                                    vw = mm(ps[:, nn * 512:(nn + 1) * 512],
                                            HT[:, fc,
                                               tt * 128:(tt + 1) * 128],
                                            W2B[:, fc,
                                                nn * 512:(nn + 1) * 512],
                                            start=(fc == 0), stop=(fc == 15),
                                            waits=w,
                                            inc="pe" if (fc == 15 and nn == 1)
                                            else None)
                            v_w2mm_last = vw
                            v_xupd[tt] = E.op(
                                "vector", lambda eng, j=j, e=e, ps=ps:
                                eng.scalar_tensor_tensor(
                                    X[:, j, :], ps, GATES[:, j, e:e + 1],
                                    X[:, j, :], op0=ALU.mult, op1=ALU.add),
                                waits=[("pe", vw)])
                        v_gelu = {}
                        v_xupd = {}
            phase_barrier()

        if upto >= 7:
            phase_barrier()
            emit_layernorm_to_XF([("ve", E.cur("ve")), ("ac", E.cur("ac")),
                                  ("pe", E.cur("pe"))], router=False)

        if upto >= 8:
            phase_barrier()
            with contextlib.ExitStack() as aes:
                att = {
                    "QTP": aes.enter_context(
                        nc.sbuf_tensor("QTP1", [128, 2048], BF16)),
                    "KTP": aes.enter_context(
                        nc.sbuf_tensor("KTP1", [128, 2048], BF16)),
                    "VAL": aes.enter_context(
                        nc.sbuf_tensor("VAL1", [128, 16, 130], BF16)),
                    "PT": [aes.enter_context(
                        nc.sbuf_tensor(f"PT1_{i}", [128, 2048], BF16))
                        for i in range(2)],
                    "OF": aes.enter_context(
                        nc.sbuf_tensor("OF1", [128, 8, 2048], BF16)),
                    "WQTP": [aes.enter_context(
                        nc.sbuf_tensor(f"WQTP1_{i}", [128, 8, 128], BF16))
                        for i in range(2)],
                    "WKTP": [aes.enter_context(
                        nc.sbuf_tensor(f"WKTP1_{i}", [128, 8, 128], BF16))
                        for i in range(2)],
                    "WVTP": [aes.enter_context(
                        nc.sbuf_tensor(f"WVTP1_{i}", [128, 8, 128], BF16))
                        for i in range(2)],
                    "WOB": aes.enter_context(
                        nc.sbuf_tensor("WOB1", [128, 8, 1024], BF16)),
                    "IDENTB": aes.enter_context(
                        nc.sbuf_tensor("IDENTB1", [128, 128], BF16)),
                    "ONESRB": aes.enter_context(
                        nc.sbuf_tensor("ONESRB1", [1, 128], BF16)),
                    "BTL": aes.enter_context(
                        nc.sbuf_tensor("BTL1", [128, 1024], F32)),
                    "CONSTV": aes.enter_context(
                        nc.sbuf_tensor("CONSTV1", [128, 2], F32)),
                }
                print("sbuf remaining in L1 attn scope:",
                      nc.sbuf_bytes_remaining)
                dma("sync", "qS", att["ONESRB"][:, :], onesrb_in)
                E.op("vector", lambda eng: eng.memset(
                    att["CONSTV"][:, 0:1], 0.125))
                v_sg = dma("sync", "qS", RS[0:1, 8:9], wsig_in)
                v_sig = E.op("scalar", lambda eng: eng.activation(
                    RS[0:1, 9:10], RS[0:1, 8:9], AF.Sigmoid),
                    waits=[("qS", v_sg)])
                v_sf = E.op("vector", lambda eng: eng.tensor_scalar(
                    RS[0:1, 10:11], RS[0:1, 9:10], -0.125, None,
                    op0=ALU.mult), waits=[("ac", v_sig)])
                v_sfm = mm(PSB[:, 0:1], ONESR[:, :], RS[0:1, 10:11],
                           start=True, stop=True,
                           waits=[("ve", v_sf), ("ac", E.cur("ac"))],
                           inc="pe")
                E.op("vector", lambda eng: eng.tensor_copy(
                    SM[:, 7:8], PSB[:, 0:1]), waits=[("pe", v_sfm)])
                v_wob1 = dma("gpsimd", "qP", att["WOB"][:, :, :],
                             wout_in.rearrange("(c p) o -> p c o", p=128))

                def load_pair1(p, par):
                    if p < 2:
                        qsrc, ksrc, c0 = wqt_in, wkt_in, p * 128
                    else:
                        qsrc, ksrc, c0 = wqs_in, wks_in, (p - 2) * 128
                    v = dma("gpsimd", "qP", att["WQTP"][par][:, :, :],
                            qsrc.rearrange("(dc p) o -> p dc o", p=128)
                            [:, :, c0:c0 + 128],
                            waits=[("pe", E.cur("pe"))])
                    v = dma("gpsimd", "qP", att["WKTP"][par][:, :, :],
                            ksrc.rearrange("(dc p) o -> p dc o", p=128)
                            [:, :, c0:c0 + 128])
                    v = dma("gpsimd", "qP", att["WVTP"][par][:, :, :],
                            wv1_in.rearrange("(dc p) o -> p dc o", p=128)
                            [:, :, p * 128:(p + 1) * 128])
                    return ("qP", v)

                emit_bias_tile(att["BTL"], bout_in)
                cfg1 = {
                    "load_pair": load_pair1,
                    "qscale": lambda p: (SM[:, 7:8] if p < 2
                                         else att["CONSTV"][:, 0:1]),
                    "qbias": lambda p: 0.0,
                    "kbias": lambda p: 0.0,
                    "vbias_row": lambda p: None,
                    "wob_ready": ("qP", v_wob1),
                    "wob_ready2": ("qP", v_wob1),
                    "vbias_ready": ("qP", v_wob1),
                }
                emit_attention(att, cfg1)
            phase_barrier()

        if upto >= 9:
            phase_barrier()
            with contextlib.ExitStack() as tes:
                TSB = tes.enter_context(
                    nc.sbuf_tensor("TSB", [128, 1024], BF16))
                TPF = tes.enter_context(
                    nc.sbuf_tensor("TPF", [128, 4096], F32))
                v_tst = 0
                vtprev = [0]

                def tern_chunk(load_buf, wsl, ncols, dst_dma_fn, vprev):
                    v = E.op("vector", lambda eng, w=wsl: eng.tensor_reduce(
                        SM[:, 0:1], w, axis=AX.X, op=ALU.add,
                        apply_absolute_value=True))
                    E.op("vector", lambda eng, n=ncols: eng.tensor_scalar(
                        SM[:, 0:1], SM[:, 0:1], 1.0 / n, None, op0=ALU.mult))
                    E.op("vector", lambda eng: eng.tensor_scalar_max(
                        SM[:, 0:1], SM[:, 0:1], 1e-5))
                    E.op("vector", lambda eng: eng.tensor_scalar(
                        SM[:, 1:2], SM[:, 0:1], 0.5, None, op0=ALU.mult))
                    E.op("vector", lambda eng: eng.tensor_scalar(
                        SM[:, 2:3], SM[:, 1:2], -1.0, None, op0=ALU.mult))
                    vlast = 0
                    for hh in range((ncols + 1023) // 1024):
                        wpart = wsl[:, hh * 1024:(hh + 1) * 1024]
                        E.op("vector", lambda eng, w=wpart: eng.tensor_scalar(
                            SCRF[:, 0:1024], w, SM[:, 1:2], None,
                            op0=ALU.is_ge), waits=[("pe", vtprev[0])])
                        E.op("vector", lambda eng, w=wpart: eng.tensor_scalar(
                            SCRF[:, 1024:2048], w, SM[:, 2:3], None,
                            op0=ALU.is_le))
                        E.op("vector", lambda eng: eng.tensor_sub(
                            SCRF[:, 0:1024], SCRF[:, 0:1024],
                            SCRF[:, 1024:2048]))
                        v_wef = E.op("vector", lambda eng: eng.tensor_scalar(
                            SCRF[:, 0:1024], SCRF[:, 0:1024], SM[:, 0:1],
                            None, op0=ALU.mult))
                        vt = None
                        for b in range(8):
                            w = [("ve", v_wef)]
                            if b == 0 and vprev[0] > 0:
                                w.append(("qS", vprev[0]))
                            vt = tr(PSA[:, b * 128:(b + 1) * 128],
                                    SCRF[:, b * 128:(b + 1) * 128],
                                    waits=w if b == 0 else (),
                                    inc="pe" if b == 7 else None)
                        vtprev[0] = vt
                        v_ev = E.op("vector", lambda eng: eng.tensor_copy(
                            TSB[:, :], PSA[:, 0:1024]), waits=[("pe", vt)])
                        vlast = dst_dma_fn(hh, v_ev)
                        vprev[0] = vlast
                    return vlast

                vprev = [0]
                for ch in range(32):
                    vl = dma("sync", "qS", TPF[:, 0:1024],
                             wfc_in[ch * 128:(ch + 1) * 128, :],
                             waits=[("ve", E.cur("ve"))])
                    E.wait("vector", [("qS", vl)])

                    def dst_fc(hh, v_ev, ch=ch):
                        return dma("sync", "qS",
                                   wefcT[:, ch * 128:(ch + 1) * 128]
                                   .rearrange("(dc p) f -> p dc f", p=128),
                                   TSB[:, :].rearrange(
                                       "p (dc f) -> p dc f", f=128),
                                   waits=[("ve", v_ev)])
                    tern_chunk(None, TPF[:, 0:1024], 1024, dst_fc, vprev)
                for ch in range(8):
                    vl = dma("sync", "qS", TPF[:, :],
                             wproj_in[ch * 128:(ch + 1) * 128, :],
                             waits=[("ve", E.cur("ve"))])
                    E.wait("vector", [("qS", vl)])

                    def dst_pj(hh, v_ev, ch=ch):
                        return dma("sync", "qS",
                                   wepT[:, ch * 128:(ch + 1) * 128]
                                   .rearrange("(q p) d -> p q d", p=128)
                                   [:, hh * 8:(hh + 1) * 8, :],
                                   TSB[:, :].rearrange(
                                       "p (q d) -> p q d", d=128),
                                   waits=[("ve", v_ev)])
                    tern_chunk(None, TPF[:, :], 4096, dst_pj, vprev)
            phase_barrier()

        if upto >= 10:
            with contextlib.ExitStack() as fes:
                R2T = fes.enter_context(
                    nc.sbuf_tensor("R2T", [128, 32, 512], BF16))
                WF1 = [fes.enter_context(
                    nc.sbuf_tensor(f"WF1_{i}", [128, 8, 2, 128], BF16))
                    for i in range(2)]
                WPB = fes.enter_context(
                    nc.sbuf_tensor("WPB", [128, 32, 512], BF16))
                ABT = fes.enter_context(nc.sbuf_tensor("ABT", [16, 2048], BF16))
                BBT = fes.enter_context(nc.sbuf_tensor("BBT", [16, 2048], BF16))
                UT = fes.enter_context(nc.sbuf_tensor("UT", [16, 4096], BF16))
                UF = fes.enter_context(nc.sbuf_tensor("UF", [128, 512], F32))
                VTB = fes.enter_context(nc.sbuf_tensor("VTB", [128, 8, 16], BF16))
                FBT = fes.enter_context(nc.sbuf_tensor("FBT", [16, 16], BF16))
                print("sbuf remaining in L1 ffn scope:",
                      nc.sbuf_bytes_remaining)
                v_vm = dma("sync", "qS", SCRF[0:16, 0:1024], vm_in)
                vt = None
                for dc in range(NDC):
                    vt = tr(PSA[:, dc * 16:(dc + 1) * 16],
                            SCRF[0:16, dc * 128:(dc + 1) * 128],
                            waits=([("qS", v_vm), ("ve", E.cur("ve")),
                                    ("ac", E.cur("ac"))] if dc == 0 else ()),
                            inc="pe" if dc == 7 else None, np_=16)
                E.op("vector", lambda eng: eng.tensor_copy(
                    VTB[:, :, :], PSA[:, 0:128].rearrange(
                        "p (dc r) -> p dc r", r=16)), waits=[("pe", vt)])
                v_fb = dma("sync", "qS", SCRF[0:16, 1024:1040], fastb_in)
                v_tfb = tr(PSB[0:16, 0:16], SCRF[0:16, 1024:1040],
                           waits=[("qS", v_fb)], inc="pe", np_=16)
                E.op("vector", lambda eng: eng.tensor_copy(
                    FBT[:, :], PSB[0:16, 0:16]), waits=[("pe", v_tfb)])
                v_g = dma("sync", "qS", RS[0:1, 11:12], gate_in)
                v_u = dma("sync", "qS", UF[:, :].rearrange(
                    "p (c r) -> p c r", r=16),
                    u_in.rearrange("(c p) r -> p c r", p=128))
                v_gb = mm(PSB[:, 16:17], ONESR[:, :], RS[0:1, 11:12],
                          start=True, stop=True,
                          waits=[("qS", v_g), ("ve", E.cur("ve"))], inc="pe")
                E.op("vector", lambda eng: eng.tensor_copy(
                    SM[:, 8:9], PSB[:, 16:17]), waits=[("pe", v_gb)])
                v_ug = E.op("vector", lambda eng: eng.tensor_scalar(
                    UF[:, :], UF[:, :], SM[:, 8:9], None, op0=ALU.mult),
                    waits=[("qS", v_u)])
                v_ue = v_ug
                for rnd in range(2):
                    vt = None
                    for cc in range(16):
                        c = rnd * 16 + cc
                        vt = tr(PSA[0:16, cc * 128:(cc + 1) * 128],
                                UF[:, c * 16:(c + 1) * 16],
                                waits=([("ve", v_ue)] if cc == 0 else ()),
                                inc="pe" if cc == 15 else None)
                    v_ue = E.op("vector", lambda eng, rnd=rnd:
                                eng.tensor_copy(
                                    UT[0:16, rnd * 2048:(rnd + 1) * 2048],
                                    PSA[0:16, 0:2048]), waits=[("pe", vt)])
                va = None
                for s4 in range(4):
                    for dc in range(NDC):
                        w = [("ve", v_ue)] if (s4 == 0 and dc == 0) else []
                        va = mm(PSA[0:16, s4 * 512:(s4 + 1) * 512],
                                VTB[:, dc, :],
                                XF[:, dc, s4 * 512:(s4 + 1) * 512],
                                start=(dc == 0), stop=(dc == 7), waits=w,
                                inc="pe" if (s4 == 3 and dc == 7) else None)
                v_abt = E.op("vector", lambda eng: eng.tensor_copy(
                    ABT[:, :], PSA[0:16, 0:2048]), waits=[("pe", va)])
                vb = None
                for s4 in range(4):
                    w = [("ve", v_abt)] if s4 == 0 else []
                    vb = mm(PSB[0:16, s4 * 512:(s4 + 1) * 512], FBT[:, :],
                            ABT[0:16, s4 * 512:(s4 + 1) * 512], start=True,
                            stop=True, waits=w,
                            inc="pe" if s4 == 3 else None)
                v_bbt = E.op("vector", lambda eng: eng.tensor_copy(
                    BBT[:, :], PSB[0:16, 0:2048]), waits=[("pe", vb)])
                v_xupd_last = E.cur("ve")
                v_wf1 = {}
                v_wp1 = {}
                v_relu = {}
                v_sq = {}
                for tq in range(4):
                    for fc in range(32):
                        par = (fc // 2) % 2
                        if fc % 2 == 0:
                            v_wf1[par] = dma(
                                "sync", "qS", WF1[par][:, :, :, :],
                                wefcT[:, fc * 128:(fc + 2) * 128].rearrange(
                                    "(dc p) (g f) -> p dc g f", p=128, f=128),
                                waits=[("pe",
                                        v_humm.get(fc - 3, E.cur("pe")))])
                        half = fc % 4
                        ps = PSA[:, half * 512:(half + 1) * 512]
                        vh = None
                        for dc in range(NDC):
                            w = []
                            if dc == 0:
                                w = [("qS", v_wf1[par]),
                                     ("ac", v_relu.get(fc - 4, 0)),
                                     ("ve", v_xupd_last if fc == 0 else 0)]
                                if tq == 0 and fc == 0:
                                    w.append(("ve", v_bbt))
                            vh = mm(ps, WF1[par][:, dc, fc % 2, :],
                                    XF[:, dc, tq * 512:(tq + 1) * 512],
                                    start=(dc == 0), stop=False, waits=w,
                                    inc=None)
                        vh = mm(ps, UT[0:16, fc * 128:(fc + 1) * 128],
                                BBT[0:16, tq * 512:(tq + 1) * 512],
                                start=False, stop=True, inc="pe")
                        v_humm[fc] = vh
                        sh = (fc % 2) * 512
                        v_rl = E.op("scalar", lambda eng, ps=ps, sh=sh:
                                    eng.activation(SCRF[:, sh:sh + 512], ps,
                                                   AF.Relu),
                                    waits=[("pe", vh),
                                           ("ve", v_sq.get(fc - 2, 0))])
                        v_relu[fc] = v_rl
                        v_sq[fc] = E.op("vector", lambda eng, fc=fc, sh=sh:
                                        eng.tensor_tensor(
                                            R2T[:, fc, :],
                                            SCRF[:, sh:sh + 512],
                                            SCRF[:, sh:sh + 512],
                                            op=ALU.mult),
                                        waits=[("ac", v_rl)])
                    v_r2_done = E.cur("ve")
                    vpr = {}
                    for dh in range(2):
                        v_wpb = dma(
                            "sync", "qS", WPB[:, :, :],
                            wepT[:, dh * 512:(dh + 1) * 512].rearrange(
                                "(fc p) d -> p fc d", p=128),
                            waits=[("pe", vpr.get((dh - 1, 31, 3),
                                                  E.cur("pe")))])
                        for fc in range(32):
                            for tt in range(4):
                                psv = (PSA if dh == 0 else PSB)[
                                    :, tt * 512:(tt + 1) * 512]
                                w = []
                                if fc == 0 and tt == 0:
                                    w = [("qS", v_wpb),
                                         ("ve", v_r2_done),
                                         ("ac", v_relu[31])]
                                vpr[(dh, fc, tt)] = mm(
                                    psv, R2T[:, fc, tt * 128:(tt + 1) * 128],
                                    WPB[:, fc, :], start=(fc == 0),
                                    stop=(fc == 31), waits=w,
                                    inc="pe" if tt == 3 else None)
                    for tt in range(4):
                        j = tq * 4 + tt
                        for dh in range(2):
                            psv = (PSA if dh == 0 else PSB)[
                                :, tt * 512:(tt + 1) * 512]
                            v_xupd_last = E.op(
                                "vector", lambda eng, j=j, dh=dh, psv=psv:
                                eng.scalar_tensor_tensor(
                                    X[:, j, dh * 512:(dh + 1) * 512], psv,
                                    1.0, X[:, j, dh * 512:(dh + 1) * 512],
                                    op0=ALU.mult, op1=ALU.add),
                                waits=[("pe", vpr[(dh, 31, 3)])])
                    v_relu = {}
                    v_sq = {}
                    v_humm = {}
            phase_barrier()

        marks = E.all_marks()
        vst = dma("sync", "qS", y_out.rearrange("(j p) d -> p j d", p=128),
                  X[:, :, :], waits=marks)
        E.wait("sync", [("qS", vst)])
        for name, ap in dbg_aps.items():
            src = {"XF": XF[:, :, :], "X": X[:, :, :], "SCRF": SCRF[:, :],
                   "PROBS": PROBS[:, :, :], "GATES": GATES[:, :, :],
                   "BIS": BIS[:, :], "B128": B128[:, :], "SM": SM[:, :]}[name]
            v = dma("gpsimd", "qP", ap, src, waits=marks)
            E.wait("gpsimd", [("qP", v)])

        with nc.Block() as block:
            @block.sync
            def _(eng):
                for fn in E.q["sync"]:
                    fn(eng, sems)

            @block.scalar
            def _(eng):
                for fn in E.q["scalar"]:
                    fn(eng, sems)

            @block.vector
            def _(eng):
                for fn in E.q["vector"]:
                    fn(eng, sems)

            @block.tensor
            def _(eng):
                for fn in E.q["tensor"]:
                    fn(eng, sems)

            @block.gpsimd
            def _(eng):
                for fn in E.q["gpsimd"]:
                    fn(eng, sems)

    return nc


def make_in_map(inputs, core):
    f = np.ascontiguousarray

    def a32(v):
        return np.asarray(v, dtype=np.float32)

    m = {
        "x": f(a32(inputs["L0_attn_x"][core])),
        "identf": np.eye(128, dtype=np.float32),
        "identb": np.eye(128, dtype=np.float32).astype(ml_dtypes.bfloat16),
        "onesrb": np.ones((1, 128), dtype=np.float32).astype(ml_dtypes.bfloat16),
        "onesr": np.ones((1, 128), dtype=np.float32),
        "onescb": np.ones((128, 1), dtype=np.float32).astype(ml_dtypes.bfloat16),
        "foldw": f((np.arange(128)[:, None] % 8 ==
                    np.arange(128)[None, :] % 8).astype(np.float32)),
        "wq_n": f(a32(inputs["L0_attn_W_Q"]).reshape(1024, 1024)),
        "wk_n": f(a32(inputs["L0_attn_W_K"]).reshape(1024, 1024)),
        "wv_n": f(a32(inputs["L0_attn_W_V"]).reshape(1024, 1024)),
        "wo_n": f(a32(inputs["L0_attn_W_O"]).reshape(16384, 64)),
        "bq": f(a32(inputs["L0_attn_b_Q"]).reshape(1024)),
        "bk": f(a32(inputs["L0_attn_b_K"]).reshape(1024)),
        "bv": f(a32(inputs["L0_attn_b_V"]).reshape(1024)),
        "bo": f(a32(inputs["L0_attn_b_O"]).reshape(1, 1024)),
        "wg": f(a32(inputs["L0_ffn_Wg"])),
        "w1": f(a32(inputs["L0_ffn_W1"]).reshape(8 * 1024, 2048)),
        "w2": f(a32(inputs["L0_ffn_W2"]).reshape(8 * 2048, 1024)),
        "wqt": f(a32(inputs["L1_attn_Wqt"])),
        "wkt": f(a32(inputs["L1_attn_Wkt"])),
        "wqs": f(a32(inputs["L1_attn_Wqs"])),
        "wks": f(a32(inputs["L1_attn_Wks"])),
        "wv1": f(a32(inputs["L1_attn_Wv"])),
        "wout": f(a32(inputs["L1_attn_Wout"])),
        "bout": f(a32(inputs["L1_attn_bout"]).reshape(1, 1024)),
        "wsig": f(a32(inputs["L1_attn_w_sigma"]).reshape(1, 1)),
        "gate": f(a32(inputs["L1_ffn_gate"]).reshape(1, 1)),
        "fastb": f(a32(inputs["L1_ffn_fast_b"])),
        "u": f(a32(inputs["L1_ffn_u"])),
        "vm": f(a32(inputs["L1_ffn_v"])),
        "wfc": f(a32(inputs["L1_ffn_w_fc"])),
        "wproj": f(a32(inputs["L1_ffn_w_proj"])),
    }
    return m


def kernel(**inputs):
    nc = build_program()
    in_maps = [make_in_map(inputs, c) for c in range(NCORES)]
    res = run_bass_kernel_spmd(nc, in_maps, core_ids=list(range(NCORES)))
    out = np.stack([np.asarray(res.results[c]["y"], dtype=np.float32)
                    for c in range(NCORES)], axis=0)
    return out


# version 08
# speedup vs baseline: 1.1983x; 1.0189x over previous
import contextlib
import copy
import numpy as np
import bass_rust
import ml_dtypes
import concourse.bass as bass
import concourse.mybir as mybir
from concourse.bass_utils import run_bass_kernel_spmd
from concourse.replica_groups import maybe_share_collective_output_space

F32 = mybir.dt.float32
BF16 = mybir.dt.bfloat16
AF = mybir.ActivationFunctionType
ALU = mybir.AluOpType
AX = mybir.AxisListType

T = 2048
D = 1024
NT = 16
NDC = 8
NCORES = 8
BIS_ITERS = 50

ENGINES = ("sync", "scalar", "vector", "tensor", "gpsimd")


class Emitter:
    def __init__(self):
        self.q = {e: [] for e in ENGINES}
        self.ctr = {}

    def cur(self, sem):
        return self.ctr.get(sem, 0)

    def plan(self, sem, k=1):
        self.ctr[sem] = self.ctr.get(sem, 0) + k
        return self.ctr[sem]

    CHAIN = {"vector": "ve", "scalar": "ac"}

    def op(self, engine, build, waits=(), inc=None, inc_k=1):
        waits = list(waits)
        csem = self.CHAIN.get(engine)
        if csem is not None:
            waits.append((csem, self.cur(csem)))
            if inc is None:
                inc = csem
            assert inc == csem and inc_k == 1
        v = self.plan(inc, inc_k) if inc else None

        def fn(eng, S, _w=tuple(waits), _b=build, _i=inc, _k=inc_k):
            for s, val in _w:
                if val > 0:
                    eng.wait_ge(S[s], val)
            ins = _b(eng)
            if _i is not None and ins is not None:
                ins.then_inc(S[_i], _k)
        self.q[engine].append(fn)
        return v

    def wait(self, engine, waits):
        def fn(eng, S, _w=tuple(waits)):
            for s, val in _w:
                if val > 0:
                    eng.wait_ge(S[s], val)
        self.q[engine].append(fn)

    def all_marks(self):
        return [(s, self.cur(s)) for s in ("qS", "qA", "qP", "pe", "ve",
                                           "ac", "gp", "cc")]


def build_program(upto=99, debug=(), cc_local=False, paranoid=False, lnv=0, no_rmm=False, no_p2=False):
    nc = bass.Bass("TRN2", target_bir_lowering=False, debug=False,
                   detect_race_conditions=False)
    E = Emitter()

    def din(name, shape, dtype=F32):
        return nc.dram_tensor(name, shape, dtype, kind="ExternalInput").ap()

    x_in = din("x", [T, D])
    identf = din("identf", [128, 128])
    identb_in = din("identb", [128, 128], BF16)
    onesrb_in = din("onesrb", [1, 128], BF16)
    onesr = din("onesr", [1, 128])
    onescb = din("onescb", [128, 1], BF16)
    foldw = din("foldw", [128, 128])
    wq_n = din("wq_n", [1024, 1024]); wk_n = din("wk_n", [1024, 1024])
    wv_n = din("wv_n", [1024, 1024]); wo_n = din("wo_n", [16384, 64])
    bq_in = din("bq", [1024]); bk_in = din("bk", [1024]); bv_in = din("bv", [1024])
    bo_in = din("bo", [1, 1024])
    wg_in = din("wg", [1024, 8])
    w1_in = din("w1", [8 * 1024, 2048])
    w2_in = din("w2", [8 * 2048, 1024])
    wqt_in = din("wqt", [1024, 256]); wkt_in = din("wkt", [1024, 256])
    wqs_in = din("wqs", [1024, 768]); wks_in = din("wks", [1024, 768])
    wv1_in = din("wv1", [1024, 1024]); wout_in = din("wout", [1024, 1024])
    bout_in = din("bout", [1, 1024])
    wsig_in = din("wsig", [1, 1]); gate_in = din("gate", [1, 1])
    fastb_in = din("fastb", [16, 16]); u_in = din("u", [4096, 16])
    vm_in = din("vm", [16, 1024])
    wfc_in = din("wfc", [4096, 1024]); wproj_in = din("wproj", [1024, 4096])

    y_out = nc.dram_tensor("y", [T, D], F32, kind="ExternalOutput").ap()
    dbg_aps = {}
    for name, shape, dtype in debug:
        dbg_aps[name] = nc.dram_tensor("dbg_" + name, list(shape), dtype,
                                       kind="ExternalOutput").ap()

    wq_s = nc.dram_tensor("wq_s", [1024, 1024], BF16).ap()
    wk_s = nc.dram_tensor("wk_s", [1024, 1024], BF16).ap()
    wv_s = nc.dram_tensor("wv_s", [1024, 1024], BF16).ap()
    wqT_s = nc.dram_tensor("wqT_s", [1024, 1024], BF16).ap()
    wkT_s = nc.dram_tensor("wkT_s", [1024, 1024], BF16).ap()
    wvT_s = nc.dram_tensor("wvT_s", [1024, 1024], BF16).ap()
    wefcT = nc.dram_tensor("wefcT", [1024, 4096], BF16).ap()
    rb_scr = nc.dram_tensor("rb_scr", [1, 2048], F32).ap()
    wepT = nc.dram_tensor("wepT", [4096, 1024], BF16).ap()
    rg = [list(range(NCORES))]
    cc = {}

    def get_cc():
        if not cc:
            cc_space = ("Local" if cc_local else
                        maybe_share_collective_output_space("AllGather", rg))
            cc["in"] = nc.dram_tensor("cc_in", [8, T], F32, kind="Internal")
            cc["out"] = nc.dram_tensor("cc_out", [8 * NCORES, T], F32,
                                       kind="Internal", addr_space=cc_space)
        return cc["in"], cc["out"]

    es = contextlib.ExitStack()
    with es:
        X = es.enter_context(nc.sbuf_tensor("Xs", [128, NT, D], F32))
        XF = es.enter_context(nc.sbuf_tensor("XFs", [128, NDC, T], BF16))
        IDENT = es.enter_context(nc.sbuf_tensor("IDENT", [128, 128], F32))
        ONESR = es.enter_context(nc.sbuf_tensor("ONESR", [1, 128], F32))
        ONESCB = es.enter_context(nc.sbuf_tensor("ONESCB", [128, 1], BF16))
        FOLDW = es.enter_context(nc.sbuf_tensor("FOLDW", [128, 128], F32))
        BQ8 = es.enter_context(nc.sbuf_tensor("BQ8", [128, 8], F32))
        BKs = es.enter_context(nc.sbuf_tensor("BKs", [128, 8], F32))
        BVs = es.enter_context(nc.sbuf_tensor("BVs", [128, 8], F32))
        PROBS = es.enter_context(nc.sbuf_tensor("PROBS", [128, NT, 8], F32))
        GATES = es.enter_context(nc.sbuf_tensor("GATES", [128, NT, 8], F32))
        SCRF = es.enter_context(nc.sbuf_tensor("SCRF", [128, 2048], F32))
        SM = es.enter_context(nc.sbuf_tensor("SM", [128, 16], F32))
        RS = es.enter_context(nc.sbuf_tensor("RS", [1, 16], F32))
        BT = es.enter_context(nc.sbuf_tensor("BT", [128, D], F32))
        BIS = es.enter_context(nc.sbuf_tensor("BIS", [128, 12], F32))
        B128 = es.enter_context(nc.sbuf_tensor("B128", [128, 1024], F32))

        PSA = es.enter_context(nc.psum_tensor("PSA", [128, 2048], F32))
        PSB = es.enter_context(nc.psum_tensor("PSB", [128, 2048], F32))
        PSS = nc.virtual_psum_tensor("PSS", [128, 4096], BF16)
        nc.back_virtual_tensor_with(PSS, PSA)

        sems = {}
        for s in ("qS", "qA", "qP", "pe", "ve", "ac", "gp", "cc"):
            sems[s] = es.enter_context(nc.semaphore(s))

        print("sbuf bytes remaining after persistent allocs:",
              nc.sbuf_bytes_remaining)

        def dma(qeng, sem, out, in_, waits=(), **kw):
            return E.op(qeng,
                        lambda eng, o=out, i=in_, kw=dict(kw):
                        eng.dma_start(out=o, in_=i, **kw),
                        waits=waits, inc=sem, inc_k=16)

        def mm(out, lhsT, rhs, start, stop, waits=(), inc=None):
            return E.op("tensor",
                        lambda eng, o=out, l=lhsT, r=rhs, s1=start, s2=stop:
                        eng.matmul(o, l, r, start=s1, stop=s2),
                        waits=waits, inc=inc)

        def tr(out, in_, waits=(), inc=None, np_=128):
            return E.op("tensor",
                        lambda eng, o=out, i=in_, n=np_:
                        eng.transpose(o, i, IDENT[0:n, 0:n]),
                        waits=waits, inc=inc)

        WGF = es.enter_context(nc.sbuf_tensor("WGF", [128, 8, 8], F32))
        vX = dma("sync", "qS", X[:, :, :],
                 x_in.rearrange("(j p) d -> p j d", p=128))
        dma("sync", "qS", WGF[:, :, :],
            wg_in.rearrange("(dc p) e -> p dc e", p=128))
        dma("sync", "qS", IDENT[:, :], identf)
        dma("sync", "qS", ONESR[:, :], onesr)
        dma("sync", "qS", ONESCB[:, :], onescb)
        dma("sync", "qS", FOLDW[:, :], foldw)
        vB = None
        for name, src in (("bq", bq_in), ("bk", bk_in), ("bv", bv_in)):
            vB = dma("sync", "qS", SCRF[0:8, 128 * {"bq": 0, "bk": 1, "bv": 2}[name]:
                                        128 * ({"bq": 0, "bk": 1, "bv": 2}[name] + 1)],
                     src.rearrange("(a p) -> a p", p=128))
        vP0 = vB
        vtrb = {}
        for i, dst in enumerate((BQ8, BKs, BVs)):
            vtrb[i] = tr(PSB[0:128, i * 8:(i + 1) * 8],
                         SCRF[0:8, 128 * i:128 * (i + 1)],
                         waits=[("qS", vP0)] if i == 0 else (),
                         inc="pe" if i == 2 else None, np_=8)
        v_bev = E.op("vector", lambda eng: eng.tensor_copy(
            BKs[:, :], PSB[:, 8:16]), waits=[("pe", vtrb[2])], inc="ve")
        v_bev = E.op("vector", lambda eng: eng.tensor_copy(
            BVs[:, :], PSB[:, 16:24]), inc="ve")
        v_bev = E.op("vector", lambda eng: eng.tensor_scalar(
            BQ8[:, :], PSB[:, 0:8], 0.125, None, op0=ALU.mult), inc="ve")
        vP0ve = v_bev

        def emit_layernorm_to_XF(first_waits, router=False):
            v_rmm_prev = [0]
            v_sr_prev = 0
            prev_tr = {0: 0, 1: 0}
            prev_evac = {0: 0, 1: 0}
            prev_e32 = {0: 0, 1: 0}
            last_tr = 0
            v_ev = 0
            for j in range(NT):
                half = j % 2
                w = list(first_waits) if j == 0 else []
                if j > 0:
                    w = w + [("ac", v_sr_prev)]
                if paranoid:
                    w = w + E.all_marks()
                v_ve = E.op("vector", lambda eng, j=j: eng.reduce_sum(
                    SM[:, 0:1], X[:, j, :], axis=AX.X), waits=w, inc="ve")
                E.op("vector", lambda eng: eng.tensor_scalar(
                    SM[:, 1:2], SM[:, 0:1], -1.0 / D, None, op0=ALU.mult))
                v_ve = E.op("vector", lambda eng: eng.tensor_scalar(
                    SM[:, 0:1], SM[:, 0:1], 1.0 / D, None, op0=ALU.mult),
                    inc="ve")
                v_sq = E.op("scalar", lambda eng, j=j: eng.activation(
                    SCRF[:, 0:1024], X[:, j, :], AF.Square, bias=SM[:, 1:2],
                    scale=1.0),
                    waits=[("ve", v_ve)] +
                          ([("pe", v_rmm_prev[0])] if router else []),
                    inc="ac")
                E.op("vector", lambda eng: eng.reduce_sum(
                    SM[:, 2:3], SCRF[:, 0:1024], axis=AX.X),
                    waits=[("ac", v_sq)])
                v_ssq = E.op("vector", lambda eng: eng.tensor_scalar(
                    SM[:, 2:3], SM[:, 2:3], 1.0 / D, 1e-5,
                    op0=ALU.mult, op1=ALU.add), inc="ve")
                v_sr = E.op("scalar", lambda eng: eng.activation(
                    SM[:, 3:4], SM[:, 2:3], AF.Sqrt),
                    waits=[("ve", v_ssq)], inc="ac")
                v_sr_prev = v_sr
                E.op("vector", lambda eng: eng.reciprocal(
                    SM[:, 4:5], SM[:, 3:4]), waits=[("ac", v_sr)])
                E.op("vector", lambda eng: eng.tensor_scalar(
                    SM[:, 5:6], SM[:, 1:2], SM[:, 4:5], None, op0=ALU.mult))
                if lnv == 1:
                    v_nrm = E.op("vector", lambda eng, j=j: eng.tensor_copy(
                        SCRF[:, 1024:2048], X[:, j, :]),
                        waits=[("pe", last_tr), ("ac", v_sr)], inc="ve")
                elif lnv == 2:
                    v_nrm = E.op("vector", lambda eng, j=j: eng.tensor_scalar(
                        SCRF[:, 1024:2048], X[:, j, :], SM[:, 4:5], None,
                        op0=ALU.mult),
                        waits=[("pe", last_tr)], inc="ve")
                else:
                    v_nrm = E.op("vector", lambda eng, j=j: eng.tensor_scalar(
                        SCRF[:, 1024:2048], X[:, j, :], SM[:, 4:5], SM[:, 5:6],
                        op0=ALU.mult, op1=ALU.add),
                        waits=[("pe", last_tr)], inc="ve")
                ps = PSA[:, half * 1024:(half + 1) * 1024]
                vtr = None
                for dc in range(NDC):
                    vtr = tr(ps[:, dc * 128:(dc + 1) * 128],
                             SCRF[:, 1024 + dc * 128:1024 + (dc + 1) * 128],
                             waits=([("ve", max(v_nrm, prev_e32[half])),
                                     ("ac", prev_evac[half])]
                                    if dc == 0 else ()),
                             inc="pe" if dc == NDC - 1 else None)
                prev_tr[half] = vtr
                last_tr = vtr
                v_ev = E.op("scalar", lambda eng, j=j, ps=ps: eng.activation(
                    XF[:, :, j * 128:(j + 1) * 128],
                    ps.rearrange("p (dc t) -> p dc t", dc=8),
                    AF.Identity), waits=[("pe", vtr)], inc="ac")
                if router:
                    v_e32 = E.op("vector", lambda eng, ps=ps: eng.tensor_copy(
                        SCRF[:, 0:1024], ps), waits=[("ac", v_ev)])
                    prev_e32[half] = v_e32
                    v_rmm = None
                    for dc in range([NDC, 1, 0][no_rmm]):
                        wr = [("ve", v_e32)] if dc == 0 else []
                        if j == 0 and dc == 0:
                            wr.append(("qS", vX + 16))
                        v_rmm = mm(PSB[:, j * 8:(j + 1) * 8],
                                   SCRF[:, dc * 128:(dc + 1) * 128],
                                   WGF[:, dc, :], start=(dc == 0),
                                   stop=bool((dc == 7) or no_rmm),
                                   waits=wr,
                                   inc="pe" if (dc == 7 or no_rmm) else None)
                    if v_rmm is not None:
                        v_rmm_prev[0] = v_rmm
                prev_evac[half] = v_ev
            return v_ev

        vXF = emit_layernorm_to_XF([("qS", vX), ("ve", vP0ve)],
                                   router=(upto >= 2))

        if upto >= 2 and not no_p2:
            v_exp = 0
            v_prt = 0
            vmmj = E.cur("pe")
            for j in range(NT):
                pv = PSB[:, j * 8:(j + 1) * 8]
                v_mx = E.op("vector", lambda eng, pv=pv: eng.reduce_max(
                    SM[:, 0:1], pv, axis=AX.X),
                    waits=[("pe", vmmj), ("ac", v_exp)])
                v_mx = E.op("vector", lambda eng: eng.tensor_scalar(
                    SM[:, 1:2], SM[:, 0:1], -1.0, None, op0=ALU.mult),
                    inc="ve")
                v_exp = E.op("scalar", lambda eng, j=j, pv=pv: eng.activation(
                    PROBS[:, j, :], pv, AF.Exp, bias=SM[:, 1:2]),
                    waits=[("ve", v_mx)], inc="ac")
                E.op("vector", lambda eng, j=j: eng.reduce_sum(
                    SM[:, 2:3], PROBS[:, j, :], axis=AX.X),
                    waits=[("ac", v_exp)])
                E.op("vector", lambda eng: eng.reciprocal(
                    SM[:, 3:4], SM[:, 2:3]))
                v_pn = E.op("vector", lambda eng, j=j: eng.tensor_scalar(
                    PROBS[:, j, :], PROBS[:, j, :], SM[:, 3:4], None,
                    op0=ALU.mult), inc="ve")
                v_prt = tr(PSA[0:8, j * 128:(j + 1) * 128], PROBS[:, j, :],
                           waits=[("ve", v_pn)] + ([("ac", vXF)] if j == 0
                                                   else []),
                           inc="pe")
        if upto >= 3:
            v_pev = E.op("vector", lambda eng: eng.tensor_copy(
                SCRF[0:8, :], PSA[0:8, :]), waits=[("pe", v_prt)], inc="ve")
            cc_in, cc_out = get_cc()
            v_cin = dma("sync", "qS", cc_in[:, :], SCRF[0:8, :],
                        waits=[("ve", v_pev)])
            v_cc = E.op("gpsimd", lambda eng: eng.collective_compute(
                "AllGather", ALU.bypass, replica_groups=rg,
                ins=[cc_in.ap().opt()], outs=[cc_out.ap().opt()]),
                waits=[("qS", v_cin)], inc="cc")
            for h in range(2):
                v_b128 = dma("sync", "qS", B128[h * 64:(h + 1) * 64, :],
                             cc_out.ap()[0:64, h * 1024:(h + 1) * 1024],
                             waits=[("cc", v_cc)] if h == 0 else ())
        if upto >= 4:
            E.op("vector", lambda eng: eng.memset(BIS[:, 0:1], 0.0),
                 waits=[("qS", v_b128)])
            E.op("vector", lambda eng: eng.memset(BIS[:, 1:2], 1.0))
            v_pe_prev = E.cur("pe")
            v_ac_guard = v_exp
            for it in range(BIS_ITERS):
                E.op("vector", lambda eng: eng.scalar_tensor_tensor(
                    BIS[:, 2:3], BIS[:, 0:1], 1.0, BIS[:, 1:2],
                    op0=ALU.mult, op1=ALU.add))
                E.op("vector", lambda eng: eng.tensor_scalar(
                    BIS[:, 2:3], BIS[:, 2:3], 0.5, None, op0=ALU.mult))
                E.op("vector", lambda eng: eng.tensor_scalar(
                    SCRF[:, 0:1024], B128[:, :], BIS[:, 2:3], None,
                    op0=ALU.is_ge))
                v_fold = E.op("vector", lambda eng: eng.reduce_sum(
                    BIS[:, 3:4], SCRF[:, 0:1024], axis=AX.X), inc="ve")
                v_cm = mm(PSB[:, 0:1], FOLDW[:, :], BIS[:, 3:4],
                          start=True, stop=True,
                          waits=[("ve", v_fold)] +
                                ([("ac", v_ac_guard)] if it == 0 else []),
                          inc="pe")
                E.op("vector", lambda eng: eng.tensor_copy(
                    BIS[:, 6:7], PSB[:, 0:1]), waits=[("pe", v_cm)])
                E.op("vector", lambda eng: eng.tensor_scalar(
                    BIS[:, 7:8], BIS[:, 6:7], 2048.0, None, op0=ALU.is_ge))
                E.op("vector", lambda eng: eng.tensor_sub(
                    BIS[:, 8:9], BIS[:, 2:3], BIS[:, 0:1]))
                E.op("vector", lambda eng: eng.tensor_mul(
                    BIS[:, 8:9], BIS[:, 8:9], BIS[:, 7:8]))
                E.op("vector", lambda eng: eng.tensor_add(
                    BIS[:, 0:1], BIS[:, 0:1], BIS[:, 8:9]))
                E.op("vector", lambda eng: eng.tensor_sub(
                    BIS[:, 9:10], BIS[:, 1:2], BIS[:, 2:3]))
                E.op("vector", lambda eng: eng.tensor_mul(
                    BIS[:, 9:10], BIS[:, 9:10], BIS[:, 7:8]))
                v_upd = E.op("vector", lambda eng: eng.tensor_add(
                    BIS[:, 1:2], BIS[:, 2:3], BIS[:, 9:10]), inc="ve")
            v_throw = tr(PSB[0:1, 8:16], BIS[0:8, 0:1],
                         waits=[("ve", v_upd)], inc="pe", np_=8)
            v_rs = E.op("vector", lambda eng: eng.tensor_copy(
                RS[0:1, 0:8], PSB[0:1, 8:16]), waits=[("pe", v_throw)],
                inc="ve")
            v_th = mm(PSB[:, 16:24], ONESR[:, :], RS[0:1, 0:8],
                      start=True, stop=True, waits=[("ve", v_rs)], inc="pe")
            E.op("vector", lambda eng: eng.tensor_copy(
                SM[:, 8:16], PSB[:, 16:24]), waits=[("pe", v_th)])
            for j in range(NT):
                gj = GATES[:, j, :]
                E.op("vector", lambda eng, gj=gj, j=j: eng.tensor_tensor(
                    gj, PROBS[:, j, :], SM[:, 8:16], op=ALU.is_ge))
                E.op("vector", lambda eng, gj=gj, j=j:
                     eng.scalar_tensor_tensor(
                         gj, gj, 2.0, PROBS[:, j, :],
                         op0=ALU.mult, op1=ALU.add))
                E.op("vector", lambda eng, gj=gj: eng.reduce_max(
                    SM[:, 0:1], gj, axis=AX.X))
                E.op("vector", lambda eng: eng.tensor_scalar(
                    SM[:, 1:2], SM[:, 0:1], 2.0, None, op0=ALU.is_ge))
                E.op("vector", lambda eng: eng.scalar_tensor_tensor(
                    SM[:, 2:3], SM[:, 1:2], -2.0, SM[:, 0:1],
                    op0=ALU.mult, op1=ALU.add))
                E.op("vector", lambda eng, gj=gj: eng.tensor_scalar(
                    gj, gj, SM[:, 0:1], None, op0=ALU.is_equal))
                E.op("vector", lambda eng, gj=gj, j=j: eng.tensor_scalar(
                    gj, gj, SM[:, 2:3], None, op0=ALU.mult),
                    inc="ve" if j == NT - 1 else None)

        def phase_barrier():
            marks = [(s, E.cur(s)) for s in ("qS", "qA", "qP", "pe", "ve",
                                             "ac", "gp", "cc")]
            for e in ENGINES:
                E.wait(e, marks)

        def emit_bias_tile(BT_, src_ap):
            vld = dma("sync", "qS", BT_[0:1, :], src_ap)
            vm = None
            for nn in range(2):
                vm = mm(PSB[:, nn * 512:(nn + 1) * 512], ONESR[:, :],
                        BT_[0:1, nn * 512:(nn + 1) * 512], start=True,
                        stop=True,
                        waits=([("qS", vld), ("ve", E.cur("ve")),
                                ("ac", E.cur("ac"))] if nn == 0 else []),
                        inc="pe" if nn == 1 else None)
            E.op("vector", lambda eng: eng.tensor_copy(
                BT_[:, :], PSB[:, 0:1024]), waits=[("pe", vm)])
            for j in range(NT):
                E.op("vector", lambda eng, j=j: eng.tensor_add(
                    X[:, j, :], X[:, j, :], BT_[:, :]))

        def emit_attention(att, cfg):
            QTP, KTP, VAL, PT, OF = (att["QTP"], att["KTP"], att["VAL"],
                                     att["PT"], att["OF"])
            WQTP, WKTP, WVTP = att["WQTP"], att["WKTP"], att["WVTP"]
            OSTG = att["OSTG"]
            WOB, IDENTB, ONESRB = att["WOB"], att["IDENTB"], att["ONESRB"]
            E.op("vector", lambda eng: eng.memset(
                VAL[:, :, :].rearrange("p sc (h c) -> p sc h c", c=65)
                [:, :, :, 64:65], 1.0))
            wld = [None, None]
            wld[0] = cfg["load_pair"](0, 0)
            wld[1] = cfg["load_pair"](1, 1)
            v_exp = {}
            v_pv = {}
            v_ptread = {(0, 0): 0, (0, 1): 0,
                        (1, 0): 0, (1, 1): 0}
            pt0_qp = [0]
            bc_qp = [0]
            om_ve = [0]
            for p in range(8):
                par = p % 2
                if p >= 2:
                    pass
                vq = None
                for s4 in range(4):
                    for dc in range(NDC):
                        w = []
                        if s4 == 0 and dc == 0:
                            w = [wld[par], ("ac", E.cur("ac")),
                                 ("ve", E.cur("ve"))]
                        vq = mm(PSA[:, s4 * 512:(s4 + 1) * 512],
                                WQTP[par][:, dc, :],
                                XF[:, dc, s4 * 512:(s4 + 1) * 512],
                                start=(dc == 0), stop=(dc == 7), waits=w,
                                inc="pe" if (s4 == 3 and dc == 7) else None)
                v_qe = E.op("scalar", lambda eng, p=p: eng.activation(
                    QTP[:, :], PSA[:, :], AF.Identity,
                    bias=cfg["qbias"](p), scale=cfg["qscale"](p)),
                    waits=[("pe", vq)])
                vk = None
                for s4 in range(4):
                    for dc in range(NDC):
                        w = []
                        vk = mm(PSB[:, s4 * 512:(s4 + 1) * 512],
                                WKTP[par][:, dc, :],
                                XF[:, dc, s4 * 512:(s4 + 1) * 512],
                                start=(dc == 0), stop=(dc == 7), waits=w,
                                inc="pe" if (s4 == 3 and dc == 7) else None)
                v_ke = E.op("scalar", lambda eng, p=p: eng.activation(
                    KTP[:, :], PSB[:, :], AF.Identity,
                    bias=cfg["kbias"](p), scale=1.0),
                    waits=[("pe", vk), ("ac", v_qe)])
                if 2 <= p + 1 < 8:
                    wld[1 - par] = cfg["load_pair"](p + 1, 1 - par)
                vbias = cfg["vbias_row"](p)
                vv = None
                for sc in range(NT):
                    w0 = [("ac", v_ke)] if sc == 0 else []
                    for dc in range(NDC):
                        last = ((dc == 7) and (vbias is None) and
                                (sc == NT - 1))
                        vv = mm(PSA[:, sc * 128:(sc + 1) * 128],
                                XF[:, dc, sc * 128:(sc + 1) * 128],
                                WVTP[par][:, dc, :],
                                start=(dc == 0),
                                stop=(dc == 7) and (vbias is None),
                                waits=w0 if dc == 0 else [],
                                inc="pe" if last else None)
                    if vbias is not None:
                        vv = mm(PSA[:, sc * 128:(sc + 1) * 128],
                                ONESRB[0:1, 0:128], vbias,
                                start=False, stop=True,
                                waits=([cfg["vbias_ready"]]
                                       if (p == 0 and sc == 0) else []),
                                inc="pe" if sc == NT - 1 else None)
                v_vev = E.op("vector", lambda eng: eng.tensor_copy(
                    VAL[:, :, :].rearrange("p sc (h c) -> p sc h c", c=65)
                    [:, :, :, 0:64],
                    PSA[:, :].rearrange("p (sc h c) -> p sc h c",
                                        sc=16, c=64)),
                    waits=[("pe", vv)])
                SCRB = PT[0]
                v_sqq = E.op("scalar", lambda eng: eng.activation(
                    SCRB[:, :], QTP[:, :], AF.Square),
                    waits=[("pe", max(v_ptread[(0, 0)],
                                      v_ptread[(1, 0)])), ("ve", v_vev)])
                vnq = None
                for s4 in range(4):
                    vnq = mm(PSA[0:1, s4 * 512:(s4 + 1) * 512], ONESCB[:, :],
                             SCRB[:, s4 * 512:(s4 + 1) * 512], start=True,
                             stop=True,
                             waits=[("ac", v_sqq), ("ve", v_vev)]
                             if s4 == 0 else [],
                             inc="pe" if s4 == 3 else None)
                v_qmax = E.op("vector", lambda eng: eng.reduce_max(
                    RS[0:1, 0:1], PSA[0:1, 0:2048], axis=AX.X),
                    waits=[("pe", vnq)])
                v_sqk = E.op("scalar", lambda eng: eng.activation(
                    SCRB[:, :], KTP[:, :], AF.Square), waits=[("pe", vnq)])
                vnk = None
                for s4 in range(4):
                    vnk = mm(PSA[0:1, s4 * 512:(s4 + 1) * 512], ONESCB[:, :],
                             SCRB[:, s4 * 512:(s4 + 1) * 512], start=True,
                             stop=True,
                             waits=[("ac", v_sqk), ("ve", v_qmax)]
                             if s4 == 0 else [],
                             inc="pe" if s4 == 3 else None)
                v_kmax = E.op("vector", lambda eng: eng.reduce_max(
                    RS[0:1, 1:2], PSA[0:1, 0:2048], axis=AX.X),
                    waits=[("pe", vnk)])
                E.op("vector", lambda eng: eng.tensor_scalar(
                    RS[0:1, 2:3], RS[0:1, 0:1], RS[0:1, 1:2], -1.0 / 16,
                    op0=ALU.mult, op1=ALU.mult))
                v_rb = E.op("vector", lambda eng: eng.tensor_scalar(
                    RS[0:1, 2:3], RS[0:1, 2:3], -4.0, None, op0=ALU.add))
                v_bm = mm(PSA[:, 0:1], ONESR[:, :], RS[0:1, 2:3], start=True,
                          stop=True, waits=[("ve", v_rb)], inc="pe")
                v_bias = E.op("vector", lambda eng: eng.tensor_copy(
                    SM[:, 6:7], PSA[:, 0:1]), waits=[("pe", v_bm)])
                v_ptread[(0, 0)] = max(v_ptread[(0, 0)], vnk)
                v_ptread[(1, 0)] = max(v_ptread[(1, 0)], vnk)
                v_lastdvepsa = v_bias
                for qh in range(2):
                    q0 = qh * 1024

                    def emit_pv(sc, qh=qh):
                        par = sc % 2
                        for h01 in range(2):
                            pt = PT[h01]
                            for nn in range(2):
                                w = []
                                if nn == 0:
                                    w = [("ac", v_exp[(sc, h01)])]
                                    if sc == 0 and h01 == 0:
                                        w.append(("ve", E.cur("ve")))
                                v_pv[(sc, h01)] = mm(
                                    PSB[0:65, h01 * 1024 + nn * 512:
                                        h01 * 1024 + (nn + 1) * 512],
                                    VAL[:, sc, h01 * 65:h01 * 65 + 65],
                                    pt[:, par * 1024 + nn * 512:
                                       par * 1024 + (nn + 1) * 512],
                                    start=(sc == 0), stop=(sc == NT - 1),
                                    waits=w,
                                    inc="pe" if nn == 1 else None)
                            v_ptread[(par, h01)] = v_pv[(sc, h01)]

                    for sc in range(NT):
                        par = sc % 2
                        vsd = {}
                        for h01 in range(2):
                            base = h01 * 64
                            for nn in range(2):
                                w = []
                                if nn == 0:
                                    w = [("ac",
                                          v_exp.get((sc - 1, h01), 0))]
                                    if h01 == 0:
                                        w.append(("ve", v_lastdvepsa))
                                vsd[h01] = mm(
                                    PSA[:, h01 * 1024 + nn * 512:
                                        h01 * 1024 + (nn + 1) * 512],
                                    KTP[base:base + 64,
                                        sc * 128:(sc + 1) * 128],
                                    QTP[base:base + 64,
                                        q0 + nn * 512:
                                        q0 + (nn + 1) * 512],
                                    start=True, stop=True, waits=w,
                                    inc="pe" if nn == 1 else None)
                        for h01 in range(2):
                            pt = PT[h01]
                            wpt = [("pe", vsd[h01]),
                                   ("pe", v_ptread[(par, h01)])]
                            v_exp[(sc, h01)] = E.op(
                                "scalar",
                                lambda eng, pt=pt, h01=h01, par=par:
                                eng.activation(
                                    pt[:, par * 1024:(par + 1) * 1024],
                                    PSA[:, h01 * 1024:(h01 + 1) * 1024],
                                    AF.Exp, bias=SM[:, 6:7]),
                                waits=wpt)
                        if sc >= 1:
                            emit_pv(sc - 1)
                    emit_pv(NT - 1)
                    c_chunk = p
                    v_rc = E.op("vector", lambda eng: eng.reciprocal(
                        SCRF[0:1, 0:2048], PSB[64:65, 0:2048]),
                        waits=[("pe", v_pv[(NT - 1, 1)])])
                    v_d1 = dma("gpsimd", "qP", rb_scr, SCRF[0:1, 0:2048],
                               waits=[("ve", v_rc), ("qP", bc_qp[0])])
                    rbb = copy.copy(rb_scr)
                    rbb.ap = bass_rust.VecI64Pair([[0, 64], [1, 2048]])
                    v_d2 = dma("gpsimd", "qP", SCRF[0:64, 0:2048], rbb,
                               waits=[("qP", v_d1), ("ve", om_ve[0])])
                    bc_qp[0] = v_d2
                    E.op("vector", lambda eng, c=c_chunk, q0=q0:
                         eng.tensor_tensor(
                             OF[0:64, c, q0:q0 + 1024],
                             PSB[0:64, 0:1024],
                             SCRF[0:64, 0:1024], op=ALU.mult),
                         waits=[("qP", v_d2)])
                    v_om = E.op("vector", lambda eng:
                                eng.tensor_tensor(
                                    OSTG[0:64, 0:1024],
                                    PSB[0:64, 1024:2048],
                                    SCRF[0:64, 1024:2048], op=ALU.mult),
                                waits=[("qP", v_d2), ("qP", pt0_qp[0])])
                    om_ve[0] = v_om
                    v_mvq = dma("gpsimd", "qP",
                                OF[64:128, c_chunk, q0:q0 + 1024],
                                OSTG[0:64, 0:1024],
                                waits=[("ve", v_om)])
                    pt0_qp[0] = v_mvq
                    v_exp = {(-1, 0): v_exp[(NT - 1, 0)],
                             (-1, 1): v_exp[(NT - 1, 1)]}
                    v_pv = {}
            v_xadd = {}
            for j in range(NT):
                half = j % 2
                vo = None
                for c in range(8):
                    for nn in range(2):
                        col = half * 1024 + nn * 512
                        w = []
                        if c == 0 and nn == 0:
                            w = [cfg["wob_ready"], cfg["wob_ready2"],
                                 ("qP", pt0_qp[0]),
                                 ("ve", v_xadd.get(j - 2, E.cur("ve")))]
                        vo = mm(PSB[:, col:col + 512],
                                OF[:, c, j * 128:(j + 1) * 128],
                                WOB[:, c, nn * 512:(nn + 1) * 512],
                                start=(c == 0), stop=(c == 7), waits=w,
                                inc="pe" if (c == 7 and nn == 1) else None)
                v_xadd[j] = E.op("vector", lambda eng, j=j, half=half:
                                 eng.scalar_tensor_tensor(
                                     X[:, j, :],
                                     PSB[:, half * 1024:(half + 1) * 1024],
                                     1.0, X[:, j, :], op0=ALU.mult,
                                     op1=ALU.add),
                                 waits=[("pe", vo)])

        if upto >= 5:
            phase_barrier()
            with contextlib.ExitStack() as aes:
                att = {
                    "QTP": aes.enter_context(
                        nc.sbuf_tensor("QTP", [128, 2048], BF16)),
                    "KTP": aes.enter_context(
                        nc.sbuf_tensor("KTP", [128, 2048], BF16)),
                    "VAL": aes.enter_context(
                        nc.sbuf_tensor("VAL", [128, 16, 130], BF16)),
                    "PT": [aes.enter_context(
                        nc.sbuf_tensor(f"PT{i}", [128, 2048], BF16))
                        for i in range(2)],
                    "OF": aes.enter_context(
                        nc.sbuf_tensor("OF", [128, 8, 2048], BF16)),
                    "OSTG": aes.enter_context(
                        nc.sbuf_tensor("OSTG", [64, 1024], BF16)),
                    "WQTP": [aes.enter_context(
                        nc.sbuf_tensor(f"WQTP{i}", [128, 8, 128], BF16))
                        for i in range(2)],
                    "WKTP": [aes.enter_context(
                        nc.sbuf_tensor(f"WKTP{i}", [128, 8, 128], BF16))
                        for i in range(2)],
                    "WVTP": [aes.enter_context(
                        nc.sbuf_tensor(f"WVTP{i}", [128, 8, 128], BF16))
                        for i in range(2)],
                    "WOB": aes.enter_context(
                        nc.sbuf_tensor("WOB", [128, 8, 1024], BF16)),
                    "IDENTB": aes.enter_context(
                        nc.sbuf_tensor("IDENTB", [128, 128], BF16)),
                    "ONESRB": aes.enter_context(
                        nc.sbuf_tensor("ONESRB", [1, 128], BF16)),
                    "BVR": aes.enter_context(
                        nc.sbuf_tensor("BVR", [1, 1024], BF16)),
                    "BTL": aes.enter_context(
                        nc.sbuf_tensor("BTL", [128, 1024], F32)),
                    "STG": aes.enter_context(
                        nc.sbuf_tensor("STG", [128, 2, 1024], BF16)),
                    "CONSTV": aes.enter_context(
                        nc.sbuf_tensor("CONSTV", [128, 2], F32)),
                }
                print("sbuf remaining in L0 attn scope:",
                      nc.sbuf_bytes_remaining)
                dma("sync", "qS", att["IDENTB"][:, :], identb_in)
                dma("sync", "qS", att["ONESRB"][:, :], onesrb_in)
                vbvr = dma("gpsimd", "qP", att["BVR"][:, :],
                           bv_in.rearrange("(a d) -> a d", a=1))
                E.op("vector", lambda eng: eng.memset(
                    att["CONSTV"][:, 0:1], 0.125))
                stg = att["STG"]
                v_store = 0
                for wi, (src, dst) in enumerate(
                        ((wq_n, wq_s), (wk_n, wk_s), (wv_n, wv_s))):
                    for ch in range(8):
                        par = ch % 2
                        vl = dma("gpsimd", "qP", stg[:, par, :],
                                 src[ch * 128:(ch + 1) * 128, :],
                                 waits=[("qS", v_store)] if (wi + ch) else [])
                        v_store = dma("sync", "qS",
                                      dst[ch * 128:(ch + 1) * 128, :],
                                      stg[:, par, :], waits=[("qP", vl)])
                for wsrc, wdst in ((wq_s, wqT_s), (wk_s, wkT_s),
                                   (wv_s, wvT_s)):
                    for dc in range(NDC):
                        par = dc % 2
                        vl = dma("sync", "qS", stg[:, par, :],
                                 wsrc[:, dc * 128:(dc + 1) * 128],
                                 transpose=True, waits=[("qS", v_store)])
                        v_store = dma("sync", "qS",
                                      wdst[dc * 128:(dc + 1) * 128, :],
                                      stg[:, par, :], waits=[("qS", vl)])
                v_cast_done = v_store
                scrw = SCRF[:, 0:1024].rearrange("p (g k) -> p g k", k=64)
                v_wev = 0
                v_wmv = 0
                for g in range(8):
                    vl = dma("sync", "qS", scrw,
                             wo_n.rearrange("(g p) k -> p g k", p=128)
                             [:, g * 16:(g + 1) * 16, :],
                             waits=[("ve", v_wev)] if g else
                                   [("ve", E.cur("ve")), ("ac", E.cur("ac"))])
                    for hh in range(2):
                        vt = None
                        for ss in range(8):
                            w = [("qS", vl)] if (hh == 0 and ss == 0) else []
                            if hh == 1:
                                w = [("qP", v_wmv)] if ss == 0 else []
                            vt = E.op("tensor",
                                      lambda eng, hh=hh, ss=ss:
                                      eng.matmul(
                                          PSA[0:64,
                                              hh * 1024 + ss * 128:
                                              hh * 1024 + (ss + 1) * 128],
                                          scrw[:, hh * 8 + ss, :],
                                          IDENT[0:128, 0:128],
                                          start=True, stop=True,
                                          is_transpose=True),
                                      waits=w,
                                      inc="pe" if ss == 7 else None)
                        if hh == 0:
                            v_wev = E.op("vector", lambda eng, g=g:
                                         eng.tensor_copy(
                                             att["WOB"][0:64, g, :],
                                             PSA[0:64, 0:1024]),
                                         waits=[("pe", vt)])
                        else:
                            v_wev = E.op("vector", lambda eng:
                                         eng.tensor_copy(
                                             stg[0:64, 0, :],
                                             PSA[0:64, 1024:2048]),
                                         waits=[("pe", vt)])
                            v_wmv = dma("gpsimd", "qP",
                                        att["WOB"][64:128, g, :],
                                        stg[0:64, 0, :],
                                        waits=[("ve", v_wev)])
                v_wob_ve = v_wev
                v_wob_qp = v_wmv

                def load_pair0(p, par):
                    v = 0
                    first = True
                    for wsrc, wdst in ((wqT_s, att["WQTP"][par]),
                                       (wkT_s, att["WKTP"][par]),
                                       (wvT_s, att["WVTP"][par])):
                        w = []
                        if first:
                            w = [("qS", v_cast_done), ("pe", E.cur("pe"))]
                            first = False
                        v = dma("sync", "qS", wdst[:, :, :],
                                wsrc.rearrange("(dc p) o -> p dc o", p=128)
                                [:, :, p * 128:(p + 1) * 128], waits=w)
                    return ("qS", v)

                emit_bias_tile(att["BTL"], bo_in)
                cfg0 = {
                    "load_pair": load_pair0,
                    "qscale": lambda p: att["CONSTV"][:, 0:1],
                    "qbias": lambda p: BQ8[:, p:p + 1],
                    "kbias": lambda p: BKs[:, p:p + 1],
                    "vbias_row": lambda p, vb=vbvr:
                        att["BVR"][0:1, p * 128:(p + 1) * 128],
                    "wob_ready": ("ve", v_wob_ve),
                    "wob_ready2": ("qP", v_wob_qp),
                    "vbias_ready": ("qP", vbvr),
                }
                emit_attention(att, cfg0)
            phase_barrier()

        if upto >= 6:
            phase_barrier()
            with contextlib.ExitStack() as mes:
                HT = mes.enter_context(
                    nc.sbuf_tensor("HT", [128, 16, 1024], BF16))
                W2B = mes.enter_context(
                    nc.sbuf_tensor("W2B", [128, 16, 1024], BF16))
                W1B = [mes.enter_context(
                    nc.sbuf_tensor(f"W1B{i}", [128, 8, 4, 128], BF16))
                    for i in range(2)]
                print("sbuf remaining in MoE scope:", nc.sbuf_bytes_remaining)
                v_w1ld = {}
                v_gelu = {}
                v_xupd = {}
                v_w2mm_last = 0
                v_humm = {}
                for e in range(8):
                    v_w2 = dma("gpsimd", "qP", W2B[:, :, :],
                               w2_in[e * 2048:(e + 1) * 2048, :].rearrange(
                                   "(fc p) d -> p fc d", p=128),
                               waits=[("pe", v_w2mm_last)])
                    for th in range(2):
                        for fc in range(16):
                            par = (fc // 4) % 2
                            if fc % 4 == 0:
                                v_w1ld[par] = dma(
                                    "gpsimd", "qP", W1B[par][:, :, :, :],
                                    w1_in[e * 1024:(e + 1) * 1024,
                                          fc * 128:(fc + 4) * 128].rearrange(
                                        "(dc p) (g f) -> p dc g f",
                                        p=128, f=128),
                                    waits=[("pe",
                                            v_humm.get((e, th, fc - 5),
                                                       E.cur("pe")))])
                            half = fc % 2
                            ps = PSA[:, half * 1024:(half + 1) * 1024]
                            vh = None
                            for dc in range(NDC):
                                for nn in range(2):
                                    w = []
                                    if dc == 0 and nn == 0:
                                        w = [("qP", v_w1ld[par]),
                                             ("ac", v_gelu.get(fc - 2, 0))]
                                        if e == 0 and th == 0 and fc == 0:
                                            w.append(("ve", E.cur("ve")))
                                            w.append(("ac", E.cur("ac")))
                                    vh = mm(ps[:, nn * 512:(nn + 1) * 512],
                                            W1B[par][:, dc, fc % 4, :],
                                            XF[:, dc, th * 1024 + nn * 512:
                                               th * 1024 + (nn + 1) * 512],
                                            start=(dc == 0), stop=(dc == 7),
                                            waits=w,
                                            inc="pe" if (dc == 7 and nn == 1)
                                            else None)
                            v_humm[(e, th, fc)] = vh
                            v_gelu[fc] = E.op(
                                "scalar", lambda eng, fc=fc, ps=ps:
                                eng.activation(HT[:, fc, :], ps, AF.Gelu),
                                waits=[("pe", vh)])
                        v_last_gelu = v_gelu[15]
                        for tt in range(8):
                            j = th * 8 + tt
                            half = tt % 2
                            ps = PSB[:, half * 1024:(half + 1) * 1024]
                            vw = None
                            for fc in range(16):
                                for nn in range(2):
                                    w = []
                                    if fc == 0 and nn == 0:
                                        w = [("ac", v_last_gelu),
                                             ("qP", v_w2),
                                             ("ve", v_xupd.get(tt - 2,
                                                               E.cur("ve")))]
                                    vw = mm(ps[:, nn * 512:(nn + 1) * 512],
                                            HT[:, fc,
                                               tt * 128:(tt + 1) * 128],
                                            W2B[:, fc,
                                                nn * 512:(nn + 1) * 512],
                                            start=(fc == 0), stop=(fc == 15),
                                            waits=w,
                                            inc="pe" if (fc == 15 and nn == 1)
                                            else None)
                            v_w2mm_last = vw
                            v_xupd[tt] = E.op(
                                "vector", lambda eng, j=j, e=e, ps=ps:
                                eng.scalar_tensor_tensor(
                                    X[:, j, :], ps, GATES[:, j, e:e + 1],
                                    X[:, j, :], op0=ALU.mult, op1=ALU.add),
                                waits=[("pe", vw)])
                        v_gelu = {}
                        v_xupd = {}
            phase_barrier()

        if upto >= 7:
            phase_barrier()
            emit_layernorm_to_XF([("ve", E.cur("ve")), ("ac", E.cur("ac")),
                                  ("pe", E.cur("pe"))], router=False)

        if upto >= 8:
            phase_barrier()
            with contextlib.ExitStack() as aes:
                att = {
                    "QTP": aes.enter_context(
                        nc.sbuf_tensor("QTP1", [128, 2048], BF16)),
                    "KTP": aes.enter_context(
                        nc.sbuf_tensor("KTP1", [128, 2048], BF16)),
                    "VAL": aes.enter_context(
                        nc.sbuf_tensor("VAL1", [128, 16, 130], BF16)),
                    "PT": [aes.enter_context(
                        nc.sbuf_tensor(f"PT1_{i}", [128, 2048], BF16))
                        for i in range(2)],
                    "OF": aes.enter_context(
                        nc.sbuf_tensor("OF1", [128, 8, 2048], BF16)),
                    "OSTG": aes.enter_context(
                        nc.sbuf_tensor("OSTG1", [64, 1024], BF16)),
                    "WQTP": [aes.enter_context(
                        nc.sbuf_tensor(f"WQTP1_{i}", [128, 8, 128], BF16))
                        for i in range(2)],
                    "WKTP": [aes.enter_context(
                        nc.sbuf_tensor(f"WKTP1_{i}", [128, 8, 128], BF16))
                        for i in range(2)],
                    "WVTP": [aes.enter_context(
                        nc.sbuf_tensor(f"WVTP1_{i}", [128, 8, 128], BF16))
                        for i in range(2)],
                    "WOB": aes.enter_context(
                        nc.sbuf_tensor("WOB1", [128, 8, 1024], BF16)),
                    "IDENTB": aes.enter_context(
                        nc.sbuf_tensor("IDENTB1", [128, 128], BF16)),
                    "ONESRB": aes.enter_context(
                        nc.sbuf_tensor("ONESRB1", [1, 128], BF16)),
                    "BTL": aes.enter_context(
                        nc.sbuf_tensor("BTL1", [128, 1024], F32)),
                    "CONSTV": aes.enter_context(
                        nc.sbuf_tensor("CONSTV1", [128, 2], F32)),
                }
                print("sbuf remaining in L1 attn scope:",
                      nc.sbuf_bytes_remaining)
                dma("sync", "qS", att["ONESRB"][:, :], onesrb_in)
                E.op("vector", lambda eng: eng.memset(
                    att["CONSTV"][:, 0:1], 0.125))
                v_sg = dma("sync", "qS", RS[0:1, 8:9], wsig_in)
                v_sig = E.op("scalar", lambda eng: eng.activation(
                    RS[0:1, 9:10], RS[0:1, 8:9], AF.Sigmoid),
                    waits=[("qS", v_sg)])
                v_sf = E.op("vector", lambda eng: eng.tensor_scalar(
                    RS[0:1, 10:11], RS[0:1, 9:10], -0.125, None,
                    op0=ALU.mult), waits=[("ac", v_sig)])
                v_sfm = mm(PSB[:, 0:1], ONESR[:, :], RS[0:1, 10:11],
                           start=True, stop=True,
                           waits=[("ve", v_sf), ("ac", E.cur("ac"))],
                           inc="pe")
                E.op("vector", lambda eng: eng.tensor_copy(
                    SM[:, 7:8], PSB[:, 0:1]), waits=[("pe", v_sfm)])
                v_wob1 = dma("gpsimd", "qP", att["WOB"][:, :, :],
                             wout_in.rearrange("(c p) o -> p c o", p=128))

                def load_pair1(p, par):
                    if p < 2:
                        qsrc, ksrc, c0 = wqt_in, wkt_in, p * 128
                    else:
                        qsrc, ksrc, c0 = wqs_in, wks_in, (p - 2) * 128
                    v = dma("gpsimd", "qP", att["WQTP"][par][:, :, :],
                            qsrc.rearrange("(dc p) o -> p dc o", p=128)
                            [:, :, c0:c0 + 128],
                            waits=[("pe", E.cur("pe"))])
                    v = dma("gpsimd", "qP", att["WKTP"][par][:, :, :],
                            ksrc.rearrange("(dc p) o -> p dc o", p=128)
                            [:, :, c0:c0 + 128])
                    v = dma("gpsimd", "qP", att["WVTP"][par][:, :, :],
                            wv1_in.rearrange("(dc p) o -> p dc o", p=128)
                            [:, :, p * 128:(p + 1) * 128])
                    return ("qP", v)

                emit_bias_tile(att["BTL"], bout_in)
                cfg1 = {
                    "load_pair": load_pair1,
                    "qscale": lambda p: (SM[:, 7:8] if p < 2
                                         else att["CONSTV"][:, 0:1]),
                    "qbias": lambda p: 0.0,
                    "kbias": lambda p: 0.0,
                    "vbias_row": lambda p: None,
                    "wob_ready": ("qP", v_wob1),
                    "wob_ready2": ("qP", v_wob1),
                    "vbias_ready": ("qP", v_wob1),
                }
                emit_attention(att, cfg1)
            phase_barrier()

        if upto >= 9:
            phase_barrier()
            with contextlib.ExitStack() as tes:
                TSB = tes.enter_context(
                    nc.sbuf_tensor("TSB", [128, 1024], BF16))
                TPF = tes.enter_context(
                    nc.sbuf_tensor("TPF", [128, 4096], F32))
                v_tst = 0
                vtprev = [0]

                def tern_chunk(load_buf, wsl, ncols, dst_dma_fn, vprev):
                    v = E.op("vector", lambda eng, w=wsl: eng.tensor_reduce(
                        SM[:, 0:1], w, axis=AX.X, op=ALU.add,
                        apply_absolute_value=True))
                    E.op("vector", lambda eng, n=ncols: eng.tensor_scalar(
                        SM[:, 0:1], SM[:, 0:1], 1.0 / n, None, op0=ALU.mult))
                    E.op("vector", lambda eng: eng.tensor_scalar_max(
                        SM[:, 0:1], SM[:, 0:1], 1e-5))
                    E.op("vector", lambda eng: eng.tensor_scalar(
                        SM[:, 1:2], SM[:, 0:1], 0.5, None, op0=ALU.mult))
                    E.op("vector", lambda eng: eng.tensor_scalar(
                        SM[:, 2:3], SM[:, 1:2], -1.0, None, op0=ALU.mult))
                    vlast = 0
                    for hh in range((ncols + 1023) // 1024):
                        wpart = wsl[:, hh * 1024:(hh + 1) * 1024]
                        E.op("vector", lambda eng, w=wpart: eng.tensor_scalar(
                            SCRF[:, 0:1024], w, SM[:, 1:2], None,
                            op0=ALU.is_ge), waits=[("pe", vtprev[0])])
                        E.op("vector", lambda eng, w=wpart: eng.tensor_scalar(
                            SCRF[:, 1024:2048], w, SM[:, 2:3], None,
                            op0=ALU.is_le))
                        E.op("vector", lambda eng: eng.tensor_sub(
                            SCRF[:, 0:1024], SCRF[:, 0:1024],
                            SCRF[:, 1024:2048]))
                        v_wef = E.op("vector", lambda eng: eng.tensor_scalar(
                            SCRF[:, 0:1024], SCRF[:, 0:1024], SM[:, 0:1],
                            None, op0=ALU.mult))
                        vt = None
                        for b in range(8):
                            w = [("ve", v_wef)]
                            if b == 0 and vprev[0] > 0:
                                w.append(("qS", vprev[0]))
                            vt = tr(PSA[:, b * 128:(b + 1) * 128],
                                    SCRF[:, b * 128:(b + 1) * 128],
                                    waits=w if b == 0 else (),
                                    inc="pe" if b == 7 else None)
                        vtprev[0] = vt
                        v_ev = E.op("vector", lambda eng: eng.tensor_copy(
                            TSB[:, :], PSA[:, 0:1024]), waits=[("pe", vt)])
                        vlast = dst_dma_fn(hh, v_ev)
                        vprev[0] = vlast
                    return vlast

                vprev = [0]
                for ch in range(32):
                    vl = dma("sync", "qS", TPF[:, 0:1024],
                             wfc_in[ch * 128:(ch + 1) * 128, :],
                             waits=[("ve", E.cur("ve"))])
                    E.wait("vector", [("qS", vl)])

                    def dst_fc(hh, v_ev, ch=ch):
                        return dma("sync", "qS",
                                   wefcT[:, ch * 128:(ch + 1) * 128]
                                   .rearrange("(dc p) f -> p dc f", p=128),
                                   TSB[:, :].rearrange(
                                       "p (dc f) -> p dc f", f=128),
                                   waits=[("ve", v_ev)])
                    tern_chunk(None, TPF[:, 0:1024], 1024, dst_fc, vprev)
                for ch in range(8):
                    vl = dma("sync", "qS", TPF[:, :],
                             wproj_in[ch * 128:(ch + 1) * 128, :],
                             waits=[("ve", E.cur("ve"))])
                    E.wait("vector", [("qS", vl)])

                    def dst_pj(hh, v_ev, ch=ch):
                        return dma("sync", "qS",
                                   wepT[:, ch * 128:(ch + 1) * 128]
                                   .rearrange("(q p) d -> p q d", p=128)
                                   [:, hh * 8:(hh + 1) * 8, :],
                                   TSB[:, :].rearrange(
                                       "p (q d) -> p q d", d=128),
                                   waits=[("ve", v_ev)])
                    tern_chunk(None, TPF[:, :], 4096, dst_pj, vprev)
            phase_barrier()

        if upto >= 10:
            with contextlib.ExitStack() as fes:
                R2T = fes.enter_context(
                    nc.sbuf_tensor("R2T", [128, 32, 512], BF16))
                WF1 = [fes.enter_context(
                    nc.sbuf_tensor(f"WF1_{i}", [128, 8, 2, 128], BF16))
                    for i in range(2)]
                WPB = fes.enter_context(
                    nc.sbuf_tensor("WPB", [128, 32, 512], BF16))
                ABT = fes.enter_context(nc.sbuf_tensor("ABT", [16, 2048], BF16))
                BBT = fes.enter_context(nc.sbuf_tensor("BBT", [16, 2048], BF16))
                UT = fes.enter_context(nc.sbuf_tensor("UT", [16, 4096], BF16))
                UF = fes.enter_context(nc.sbuf_tensor("UF", [128, 512], F32))
                VTB = fes.enter_context(nc.sbuf_tensor("VTB", [128, 8, 16], BF16))
                FBT = fes.enter_context(nc.sbuf_tensor("FBT", [16, 16], BF16))
                print("sbuf remaining in L1 ffn scope:",
                      nc.sbuf_bytes_remaining)
                v_vm = dma("sync", "qS", SCRF[0:16, 0:1024], vm_in)
                vt = None
                for dc in range(NDC):
                    vt = tr(PSA[:, dc * 16:(dc + 1) * 16],
                            SCRF[0:16, dc * 128:(dc + 1) * 128],
                            waits=([("qS", v_vm), ("ve", E.cur("ve")),
                                    ("ac", E.cur("ac"))] if dc == 0 else ()),
                            inc="pe" if dc == 7 else None, np_=16)
                E.op("vector", lambda eng: eng.tensor_copy(
                    VTB[:, :, :], PSA[:, 0:128].rearrange(
                        "p (dc r) -> p dc r", r=16)), waits=[("pe", vt)])
                v_fb = dma("sync", "qS", SCRF[0:16, 1024:1040], fastb_in)
                v_tfb = tr(PSB[0:16, 0:16], SCRF[0:16, 1024:1040],
                           waits=[("qS", v_fb)], inc="pe", np_=16)
                E.op("vector", lambda eng: eng.tensor_copy(
                    FBT[:, :], PSB[0:16, 0:16]), waits=[("pe", v_tfb)])
                v_g = dma("sync", "qS", RS[0:1, 11:12], gate_in)
                v_u = dma("sync", "qS", UF[:, :].rearrange(
                    "p (c r) -> p c r", r=16),
                    u_in.rearrange("(c p) r -> p c r", p=128))
                v_gb = mm(PSB[:, 16:17], ONESR[:, :], RS[0:1, 11:12],
                          start=True, stop=True,
                          waits=[("qS", v_g), ("ve", E.cur("ve"))], inc="pe")
                E.op("vector", lambda eng: eng.tensor_copy(
                    SM[:, 8:9], PSB[:, 16:17]), waits=[("pe", v_gb)])
                v_ug = E.op("vector", lambda eng: eng.tensor_scalar(
                    UF[:, :], UF[:, :], SM[:, 8:9], None, op0=ALU.mult),
                    waits=[("qS", v_u)])
                v_ue = v_ug
                for rnd in range(2):
                    vt = None
                    for cc in range(16):
                        c = rnd * 16 + cc
                        vt = tr(PSA[0:16, cc * 128:(cc + 1) * 128],
                                UF[:, c * 16:(c + 1) * 16],
                                waits=([("ve", v_ue)] if cc == 0 else ()),
                                inc="pe" if cc == 15 else None)
                    v_ue = E.op("vector", lambda eng, rnd=rnd:
                                eng.tensor_copy(
                                    UT[0:16, rnd * 2048:(rnd + 1) * 2048],
                                    PSA[0:16, 0:2048]), waits=[("pe", vt)])
                va = None
                for s4 in range(4):
                    for dc in range(NDC):
                        w = [("ve", v_ue)] if (s4 == 0 and dc == 0) else []
                        va = mm(PSA[0:16, s4 * 512:(s4 + 1) * 512],
                                VTB[:, dc, :],
                                XF[:, dc, s4 * 512:(s4 + 1) * 512],
                                start=(dc == 0), stop=(dc == 7), waits=w,
                                inc="pe" if (s4 == 3 and dc == 7) else None)
                v_abt = E.op("vector", lambda eng: eng.tensor_copy(
                    ABT[:, :], PSA[0:16, 0:2048]), waits=[("pe", va)])
                vb = None
                for s4 in range(4):
                    w = [("ve", v_abt)] if s4 == 0 else []
                    vb = mm(PSB[0:16, s4 * 512:(s4 + 1) * 512], FBT[:, :],
                            ABT[0:16, s4 * 512:(s4 + 1) * 512], start=True,
                            stop=True, waits=w,
                            inc="pe" if s4 == 3 else None)
                v_bbt = E.op("vector", lambda eng: eng.tensor_copy(
                    BBT[:, :], PSB[0:16, 0:2048]), waits=[("pe", vb)])
                v_xupd_last = E.cur("ve")
                v_wf1 = {}
                v_wp1 = {}
                v_relu = {}
                v_sq = {}
                for tq in range(4):
                    for fc in range(32):
                        par = (fc // 2) % 2
                        if fc % 2 == 0:
                            v_wf1[par] = dma(
                                "sync", "qS", WF1[par][:, :, :, :],
                                wefcT[:, fc * 128:(fc + 2) * 128].rearrange(
                                    "(dc p) (g f) -> p dc g f", p=128, f=128),
                                waits=[("pe",
                                        v_humm.get(fc - 3, E.cur("pe")))])
                        half = fc % 4
                        ps = PSA[:, half * 512:(half + 1) * 512]
                        vh = None
                        for dc in range(NDC):
                            w = []
                            if dc == 0:
                                w = [("qS", v_wf1[par]),
                                     ("ac", v_relu.get(fc - 4, 0)),
                                     ("ve", v_xupd_last if fc == 0 else 0)]
                                if tq == 0 and fc == 0:
                                    w.append(("ve", v_bbt))
                            vh = mm(ps, WF1[par][:, dc, fc % 2, :],
                                    XF[:, dc, tq * 512:(tq + 1) * 512],
                                    start=(dc == 0), stop=False, waits=w,
                                    inc=None)
                        vh = mm(ps, UT[0:16, fc * 128:(fc + 1) * 128],
                                BBT[0:16, tq * 512:(tq + 1) * 512],
                                start=False, stop=True, inc="pe")
                        v_humm[fc] = vh
                        sh = (fc % 2) * 512
                        v_rl = E.op("scalar", lambda eng, ps=ps, sh=sh:
                                    eng.activation(SCRF[:, sh:sh + 512], ps,
                                                   AF.Relu),
                                    waits=[("pe", vh),
                                           ("ve", v_sq.get(fc - 2, 0))])
                        v_relu[fc] = v_rl
                        v_sq[fc] = E.op("vector", lambda eng, fc=fc, sh=sh:
                                        eng.tensor_tensor(
                                            R2T[:, fc, :],
                                            SCRF[:, sh:sh + 512],
                                            SCRF[:, sh:sh + 512],
                                            op=ALU.mult),
                                        waits=[("ac", v_rl)])
                    v_r2_done = E.cur("ve")
                    vpr = {}
                    for dh in range(2):
                        v_wpb = dma(
                            "sync", "qS", WPB[:, :, :],
                            wepT[:, dh * 512:(dh + 1) * 512].rearrange(
                                "(fc p) d -> p fc d", p=128),
                            waits=[("pe", vpr.get((dh - 1, 31, 3),
                                                  E.cur("pe")))])
                        for fc in range(32):
                            for tt in range(4):
                                psv = (PSA if dh == 0 else PSB)[
                                    :, tt * 512:(tt + 1) * 512]
                                w = []
                                if fc == 0 and tt == 0:
                                    w = [("qS", v_wpb),
                                         ("ve", v_r2_done),
                                         ("ac", v_relu[31])]
                                vpr[(dh, fc, tt)] = mm(
                                    psv, R2T[:, fc, tt * 128:(tt + 1) * 128],
                                    WPB[:, fc, :], start=(fc == 0),
                                    stop=(fc == 31), waits=w,
                                    inc="pe" if tt == 3 else None)
                    for tt in range(4):
                        j = tq * 4 + tt
                        for dh in range(2):
                            psv = (PSA if dh == 0 else PSB)[
                                :, tt * 512:(tt + 1) * 512]
                            v_xupd_last = E.op(
                                "vector", lambda eng, j=j, dh=dh, psv=psv:
                                eng.scalar_tensor_tensor(
                                    X[:, j, dh * 512:(dh + 1) * 512], psv,
                                    1.0, X[:, j, dh * 512:(dh + 1) * 512],
                                    op0=ALU.mult, op1=ALU.add),
                                waits=[("pe", vpr[(dh, 31, 3)])])
                    v_relu = {}
                    v_sq = {}
                    v_humm = {}
            phase_barrier()

        marks = E.all_marks()
        vst = dma("sync", "qS", y_out.rearrange("(j p) d -> p j d", p=128),
                  X[:, :, :], waits=marks)
        E.wait("sync", [("qS", vst)])
        for name, ap in dbg_aps.items():
            src = {"XF": XF[:, :, :], "X": X[:, :, :], "SCRF": SCRF[:, :],
                   "PROBS": PROBS[:, :, :], "GATES": GATES[:, :, :],
                   "BIS": BIS[:, :], "B128": B128[:, :], "SM": SM[:, :]}[name]
            v = dma("gpsimd", "qP", ap, src, waits=marks)
            E.wait("gpsimd", [("qP", v)])

        with nc.Block() as block:
            @block.sync
            def _(eng):
                for fn in E.q["sync"]:
                    fn(eng, sems)

            @block.scalar
            def _(eng):
                for fn in E.q["scalar"]:
                    fn(eng, sems)

            @block.vector
            def _(eng):
                for fn in E.q["vector"]:
                    fn(eng, sems)

            @block.tensor
            def _(eng):
                for fn in E.q["tensor"]:
                    fn(eng, sems)

            @block.gpsimd
            def _(eng):
                for fn in E.q["gpsimd"]:
                    fn(eng, sems)

    return nc


def make_in_map(inputs, core):
    f = np.ascontiguousarray

    def a32(v):
        return np.asarray(v, dtype=np.float32)

    m = {
        "x": f(a32(inputs["L0_attn_x"][core])),
        "identf": np.eye(128, dtype=np.float32),
        "identb": np.eye(128, dtype=np.float32).astype(ml_dtypes.bfloat16),
        "onesrb": np.ones((1, 128), dtype=np.float32).astype(ml_dtypes.bfloat16),
        "onesr": np.ones((1, 128), dtype=np.float32),
        "onescb": np.ones((128, 1), dtype=np.float32).astype(ml_dtypes.bfloat16),
        "foldw": f((np.arange(128)[:, None] % 8 ==
                    np.arange(128)[None, :] % 8).astype(np.float32)),
        "wq_n": f(a32(inputs["L0_attn_W_Q"]).reshape(1024, 1024)),
        "wk_n": f(a32(inputs["L0_attn_W_K"]).reshape(1024, 1024)),
        "wv_n": f(a32(inputs["L0_attn_W_V"]).reshape(1024, 1024)),
        "wo_n": f(a32(inputs["L0_attn_W_O"]).reshape(16384, 64)),
        "bq": f(a32(inputs["L0_attn_b_Q"]).reshape(1024)),
        "bk": f(a32(inputs["L0_attn_b_K"]).reshape(1024)),
        "bv": f(a32(inputs["L0_attn_b_V"]).reshape(1024)),
        "bo": f(a32(inputs["L0_attn_b_O"]).reshape(1, 1024)),
        "wg": f(a32(inputs["L0_ffn_Wg"])),
        "w1": f(a32(inputs["L0_ffn_W1"]).reshape(8 * 1024, 2048)),
        "w2": f(a32(inputs["L0_ffn_W2"]).reshape(8 * 2048, 1024)),
        "wqt": f(a32(inputs["L1_attn_Wqt"])),
        "wkt": f(a32(inputs["L1_attn_Wkt"])),
        "wqs": f(a32(inputs["L1_attn_Wqs"])),
        "wks": f(a32(inputs["L1_attn_Wks"])),
        "wv1": f(a32(inputs["L1_attn_Wv"])),
        "wout": f(a32(inputs["L1_attn_Wout"])),
        "bout": f(a32(inputs["L1_attn_bout"]).reshape(1, 1024)),
        "wsig": f(a32(inputs["L1_attn_w_sigma"]).reshape(1, 1)),
        "gate": f(a32(inputs["L1_ffn_gate"]).reshape(1, 1)),
        "fastb": f(a32(inputs["L1_ffn_fast_b"])),
        "u": f(a32(inputs["L1_ffn_u"])),
        "vm": f(a32(inputs["L1_ffn_v"])),
        "wfc": f(a32(inputs["L1_ffn_w_fc"])),
        "wproj": f(a32(inputs["L1_ffn_w_proj"])),
    }
    return m


def kernel(**inputs):
    nc = build_program()
    in_maps = [make_in_map(inputs, c) for c in range(NCORES)]
    res = run_bass_kernel_spmd(nc, in_maps, core_ids=list(range(NCORES)))
    out = np.stack([np.asarray(res.results[c]["y"], dtype=np.float32)
                    for c in range(NCORES)], axis=0)
    return out
